# version 16
# baseline (speedup 1.0000x reference)
import numpy as np
import ml_dtypes
import concourse.bass as bass
import concourse.mybir as mybir
from concourse.bass_utils import run_bass_kernel_spmd

F32 = mybir.dt.float32
BF16 = mybir.dt.bfloat16
AF = mybir.ActivationFunctionType
ALU = mybir.AluOpType
AX = mybir.AxisListType

D = 2048
T = 1024
CTX = 2048
NT = T // 128
NCTX = CTX // 128
ND = D // 128
H = 16
LN_EPS = 1e-5
BIG_NEG = -1e30
INTER = 8192
NJC = INTER // 128


class Phase:

    def __init__(self, name):
        self.name = name
        self.fns = {}

    def on(self, engine):
        def deco(fn):
            self.fns.setdefault(engine, []).append(fn)
            return fn
        return deco


_SEM_TABLE = {
    "c_dma": "hx", "a1w": "hx", "a1wg": "gw", "a2w": "gw", "a3k": "hx",
    "a4w": "gw", "a5w": "gw", "a6w": "gw", "a6m0": "hd0", "a6m1": "hd1",
    "a7w": "gw", "a7x0": "hd0", "a7x1": "hd1", "a7o0": "ho0", "a7o1": "ho1",
    "c2d0": "gs0", "c2d1": "gs1", "c2g": "gw", "c4d": "hx", "c6d": "hx",
    "c6cc": "ccs", "c6g": "gw", "c6s0": "gs0", "c7d": "hx", "c8w": "gw",
    "c8x0": "hd0", "c8x1": "hd1", "c8o0": "ho0", "c8o1": "ho1", "adbg": "hx",
}
_SEM_SUFFIX = [
    ("lnd0", "hd0"), ("lnd1", "hd1"), ("lnd2", "hd2"), ("2d", "gw"), ("3d0", "gs0"), ("3d1", "gs1"),
    ("4u0", "gs0"), ("4u1", "gs1"), ("4d0", "gs2"), ("4d1", "gs3"),
    ("5d0", "hd0"), ("5d1", "hd1"), ("5o0", "ho0"), ("5o1", "ho1"), ("5g", "hx"),
]


def _canon(name):
    if name in _SEM_TABLE:
        return _SEM_TABLE[name]
    for suf, pool in _SEM_SUFFIX:
        if name.endswith(suf):
            return pool
    if name[-1] in "TSVZ":
        return "ck" + name[-1]
    raise KeyError(f"unpooled sem name {name}")


class PB:

    def __init__(self, nc):
        self.nc = nc
        self.sems = {}
        self.cnt = {}

    def sem(self, name):
        k = _canon(name)
        if k not in self.sems:
            self.sems[k] = self.nc.alloc_semaphore(name=k)
            self.cnt[k] = 0
        return self.sems[k]

    def getcnt(self, name):
        self.sem(name)
        return self.cnt[_canon(name)]

    def inc(self, ins, name, n=1):
        ins.then_inc(self.sem(name), n)
        self.cnt[_canon(name)] += n
        return self.cnt[_canon(name)]

    def cw(self, e, key):
        e.wait_ge(self.sem(key), self.getcnt(key))

    def ci(self, ins, key):
        self.inc(ins, key)

    def nopinc(self, e, name, n=1):
        self.inc(e.nop(), name, n)

    def bases(self, *names):
        return {n: self.getcnt(n) for n in names}

    def emit(self, phase):
        with self.nc.Block() as block:
            for eng in ("sync", "gpsimd", "tensor", "scalar", "vector"):
                fns = phase.fns.get(eng)
                if not fns:
                    continue

                def run(e, fns=fns):
                    for f in fns:
                        f(e)
                getattr(block, eng)(run)



class Clk:
    def __init__(self, pb, name):
        self.pb = pb
        self.name = name
        pb.sem(name)
        self.base = pb.getcnt(name)

    def tick(self, ins):
        return self.pb.inc(ins, self.name)

    def guard(self, e):
        e.wait_ge(self.pb.sem(self.name), self.pb.getcnt(self.name))

    def gtick(self, e, ins):
        return self.pb.inc(ins, self.name)

    def wait(self, e, mark):
        e.wait_ge(self.pb.sem(self.name), mark)

    def check(self, expect_total):
        got = self.pb.getcnt(self.name) - self.base
        assert got == expect_total, (self.name, got, expect_total)



def emit_ln_fm(pb, ph, tag, src_tiles, fm_dst, fm_col0, ntok_dst, sb, dsts=None):
    if dsts is None:
        dsts = [(fm_dst, fm_col0 + i * 128, ntok_dst) for i in range(len(src_tiles))]
    n = len(src_tiles)
    rt, st, xln, tps, ident = sb["rt"], sb["st"], sb["xln"], sb["tps"], sb["ident"]
    RSL = rt.n // D if hasattr(rt, "n") else 2
    cS = Clk(pb, f"{tag}S")
    cV = Clk(pb, f"{tag}V")
    cT = Clk(pb, f"{tag}T")
    dsl = [f"{tag}d{j}" for j in range(RSL)]
    dbase = pb.bases(*dsl)

    vseq = []
    for i in range(n):
        if i >= 1:
            vseq += [("ev", i - 1, 0), ("ev", i - 1, 1)]
        for k in range(8):
            vseq.append(("st", i, k))
    vseq += [("ev", n - 1, 0), ("ev", n - 1, 1)]
    vmk = {k: i + 1 for i, k in enumerate(vseq)}

    def sqd(i): return cS.base + 4 * i + 1
    def lnd(i): return cS.base + 4 * i + 2
    def expd(i): return cS.base + 4 * i + 3
    def xld(i): return cS.base + 4 * (i + 1)
    def varpd(i): return cV.base + vmk[("st", i, 5)]
    def negbd(i): return cV.base + vmk[("st", i, 7)]
    def evd(i, half): return cV.base + vmk[("ev", i, half)]
    def tpd(j): return cT.base + j + 1

    @ph.on("sync")
    def _(e):
        for i in range(n):
            if i >= RSL:
                cS.wait(e, xld(i - RSL + 1))
            ins = e.dma_start(out=rt[:, (i % RSL) * D:(i % RSL + 1) * D], in_=src_tiles[i])
            pb.inc(ins, dsl[i % RSL], 16)

    @ph.on("scalar")
    def _(e):
        for i in range(n):
            b = (i % 2) * D
            br = (i % RSL) * D
            c = i % 2
            e.wait_ge(pb.sem(dsl[i % RSL]), dbase[dsl[i % RSL]] + 16 * (i // RSL + 1))
            if i >= 2:
                cT.wait(e, cT.base + (i - 1) * ND)
            cS.guard(e)
            cS.tick(e.activation(xln[:, b:b + D], rt[:, br:br + D], AF.Square,
                                 accum_out=st[:, 1:2]))
            cV.wait(e, varpd(i))
            cS.guard(e)
            cS.tick(e.activation(st[:, 8:9], st[:, 6:7], AF.Ln))
            cS.guard(e)
            cS.tick(e.activation(st[:, 10 + c:11 + c], st[:, 8:9], AF.Exp, scale=-0.5))
            cV.wait(e, negbd(i))
            cS.guard(e)
            cS.tick(e.activation(xln[:, b:b + D], rt[:, br:br + D], AF.Identity,
                                 bias=st[:, 12 + c:13 + c], scale=st[:, 10 + c:11 + c]))

    @ph.on("vector")
    def _(e):
        def emit_ev(i, half):
            cT.wait(e, tpd(i * ND + 8 * half + 7))
            dbuf, c0, ntok = dsts[i]
            if hasattr(dbuf, "tcols"):
                dt_, doff, dcols = dbuf.t, dbuf.off, dbuf.tcols
            else:
                dt_, doff, dcols = dbuf, 0, dbuf.shape[1]
            src = bass.AP(tps, 0, [[8 * 1024, 128], [1024, 8], [1, 128]])
            dst = bass.AP(dt_, doff + (8 * half) * ntok + c0,
                          [[dcols, 128], [ntok, 8], [1, 128]])
            cV.guard(e)
            cV.tick(e.tensor_copy(dst, src))

        for i in range(n):
            if i >= 1:
                emit_ev(i - 1, 0)
                emit_ev(i - 1, 1)
            b = (i % 2) * D
            br = (i % RSL) * D
            c = i % 2
            e.wait_ge(pb.sem(dsl[i % RSL]), dbase[dsl[i % RSL]] + 16 * (i // RSL + 1))
            cV.guard(e)
            cV.tick(e.reduce_sum(st[:, 0:1], rt[:, br:br + D], AX.X))
            if i >= 1:
                cS.wait(e, lnd(i - 1))
            cV.guard(e)
            cV.tick(e.tensor_scalar_mul(st[:, 2:3], st[:, 0:1], 1.0 / D))
            cS.wait(e, sqd(i))
            cV.guard(e)
            cV.tick(e.tensor_scalar_mul(st[:, 3:4], st[:, 1:2], 1.0 / D))
            cV.guard(e)
            cV.tick(e.tensor_mul(st[:, 4:5], st[:, 2:3], st[:, 2:3]))
            cV.guard(e)
            cV.tick(e.tensor_sub(st[:, 5:6], st[:, 3:4], st[:, 4:5]))
            cV.guard(e)
            cV.tick(e.tensor_scalar_add(st[:, 6:7], st[:, 5:6], LN_EPS))
            cS.wait(e, expd(i))
            cV.guard(e)
            cV.tick(e.tensor_mul(st[:, 14:15], st[:, 2:3], st[:, 10 + c:11 + c]))
            cV.guard(e)
            cV.tick(e.tensor_scalar_mul(st[:, 12 + c:13 + c], st[:, 14:15], -1.0))
        emit_ev(n - 1, 0)
        emit_ev(n - 1, 1)

    @ph.on("tensor")
    def _(e):
        for i in range(n):
            b = (i % 2) * D
            cS.wait(e, xld(i))
            for dt in range(ND):
                j = i * ND + dt
                if j >= 8:
                    jj = j - 8
                    cV.wait(e, evd(jj // ND, (jj % ND) // 8))
                ins = e.transpose(tps[:, (j % 8) * 1024:(j % 8) * 1024 + 128],
                                  xln[:, b + dt * 128:b + (dt + 1) * 128], ident[:, :])
                cT.tick(ins)

    def final_check():
        cS.check(4 * n)
        cV.check(len(vseq))
        cT.check(ND * n)
    ph.fns.setdefault("_check", []).append(final_check)
    return dict(cS=cS, cV=cV, cT=cT, xld=xld)



def emit_ffn(pb, cn, w, src_r, dst_r, uid):
    nc = pb.nc
    u = f"f{uid}"
    with (
        nc.sbuf_tensor(f"{u}_xfm", [128, ND * T], BF16) as x_fm,
        nc.sbuf_tensor(f"{u}_yacc", [128, NT * D], F32) as y_acc,
        nc.sbuf_tensor(f"{u}_scale", [128, 8], F32) as scale_sb,
    ):
        with (
            nc.sbuf_tensor(f"{u}_rt", [128, 3 * D], F32) as rt,
            nc.sbuf_tensor(f"{u}_st", [128, 16], F32) as st,
            nc.sbuf_tensor(f"{u}_xln", [128, 2 * D], BF16) as xln,
            nc.psum_tensor(f"{u}_tps", [128, 8 * 1024], BF16) as tps,
        ):
            sb = dict(rt=rt, st=st, xln=xln, tps=tps, ident=cn["ident"])
            ph = Phase(f"{u}_ln")
            tiles = [src_r[i * 128:(i + 1) * 128, :] for i in range(NT)]
            emit_ln_fm(pb, ph, f"{u}ln", tiles, x_fm, 0, T, sb)
            pb.emit(ph)
            for f in ph.fns.get("_check", []):
                f()

        with (
            nc.sbuf_tensor(f"{u}_gsb", [128, ND * 8], BF16) as gsb,
            nc.sbuf_tensor(f"{u}_lg", [128, 8], F32) as lg,
            nc.sbuf_tensor(f"{u}_m", [128, 4], F32) as m,
            nc.sbuf_tensor(f"{u}_msk", [128, 8], F32) as msk,
            nc.psum_tensor(f"{u}_psl", [128, 1024], F32) as ps_l,
        ):
            ph = Phase(f"{u}_top2")
            cT = Clk(pb, f"{u}2T")
            cV = Clk(pb, f"{u}2V")
            dsem = pb.bases(f"{u}2d")

            def mmd(tt): return cT.base + tt + 1
            def scd(tt): return cV.base + 7 * (tt + 1)

            @ph.on("gpsimd")
            def _(e):
                pb.inc(e.dma_start(out=gsb[:], in_=w["gate"]), f"{u}2d", 16)

            @ph.on("tensor")
            def _(e):
                e.wait_ge(pb.sem(f"{u}2d"), dsem[f"{u}2d"] + 16)
                for tt in range(NT):
                    if tt >= 2:
                        cV.wait(e, scd(tt - 2) - 6)
                    for dt in range(ND):
                        ins = e.matmul(ps_l[:, (tt % 2) * 512:(tt % 2) * 512 + 8],
                                       x_fm[:, dt * T + tt * 128:dt * T + tt * 128 + 128],
                                       gsb[:, dt * 8:(dt + 1) * 8],
                                       start=(dt == 0), stop=(dt == ND - 1))
                    cT.tick(ins)

            @ph.on("vector")
            def _(e):
                for tt in range(NT):
                    cT.wait(e, mmd(tt))
                    cV.guard(e)
                    cV.tick(e.tensor_copy(lg[:], ps_l[:, (tt % 2) * 512:(tt % 2) * 512 + 8]))
                    cV.guard(e)
                    cV.tick(e.reduce_max(m[:, 0:1], lg[:], AX.X))
                    cV.guard(e)
                    cV.tick(e.tensor_scalar(msk[:], lg[:], m[:, 0:1], None, ALU.is_ge))
                    cV.guard(e)
                    cV.tick(e.scalar_tensor_tensor(msk[:], msk[:], BIG_NEG, lg[:], ALU.mult, ALU.add))
                    cV.guard(e)
                    cV.tick(e.reduce_max(m[:, 1:2], msk[:], AX.X))
                    cV.guard(e)
                    cV.tick(e.tensor_add(scale_sb[:, tt:tt + 1], m[:, 0:1], m[:, 1:2]))
                    cV.guard(e)
                    cV.tick(e.engine_nop())

            pb.emit(ph)
            cT.check(NT)
            cV.check(7 * NT)

        with (
            nc.sbuf_tensor(f"{u}_ep", [128, 2 * 8192], BF16) as ep,
            nc.psum_tensor(f"{u}_psm", [128, 1024], F32) as ps_m,
        ):
            ph = Phase(f"{u}_moe")
            cT = Clk(pb, f"{u}3T")
            cS = Clk(pb, f"{u}3S")
            dsl = [f"{u}3d0", f"{u}3d1"]
            dbase = pb.bases(*dsl)

            def mmd(g): return cT.base + g + 1
            def evd(g): return cS.base + g + 1

            @ph.on("gpsimd")
            def _(e):
                for db in range(4):
                    if db >= 2:
                        cT.wait(e, mmd((db - 1) * NT - 1))
                    ins = e.dma_start(out=ep[:, (db % 2) * 8192:(db % 2 + 1) * 8192],
                                      in_=w["ex"][db])
                    pb.inc(ins, dsl[db % 2], 16)

            @ph.on("tensor")
            def _(e):
                for db in range(4):
                    e.wait_ge(pb.sem(dsl[db % 2]), dbase[dsl[db % 2]] + 16 * (db // 2 + 1))
                    for tt in range(NT):
                        g = db * NT + tt
                        if g >= 2:
                            cS.wait(e, evd(g - 2))
                        for dt in range(ND):
                            ins = e.matmul(
                                ps_m[:, (tt % 2) * 512:(tt % 2) * 512 + 512],
                                x_fm[:, dt * T + tt * 128:dt * T + tt * 128 + 128],
                                ep[:, (db % 2) * 8192 + dt * 512:(db % 2) * 8192 + (dt + 1) * 512],
                                start=(dt == 0), stop=(dt == ND - 1))
                        cT.tick(ins)

            @ph.on("scalar")
            def _(e):
                for db in range(4):
                    for tt in range(NT):
                        g = db * NT + tt
                        cT.wait(e, mmd(g))
                        cS.guard(e)
                        ins = e.activation(y_acc[:, tt * D + db * 512:tt * D + db * 512 + 512],
                                           ps_m[:, (tt % 2) * 512:(tt % 2) * 512 + 512],
                                           AF.Copy, scale=scale_sb[:, tt:tt + 1])
                        cS.tick(ins)

            pb.emit(ph)
            cT.check(32)
            cS.check(32)

        with (
            nc.sbuf_tensor(f"{u}_hp", [128, 16 * T], BF16) as h_pan,
            nc.sbuf_tensor(f"{u}_upt", [128, 2 * 2048], BF16) as upt,
            nc.sbuf_tensor(f"{u}_dnb", [128, 2 * 8192], BF16) as dnb,
            nc.sbuf_tensor(f"{u}_rlu", [128, 2 * 512], BF16) as rlu,
            nc.psum_tensor(f"{u}_psu", [128, 2048], F32) as ps_u,
            nc.psum_tensor(f"{u}_psd", [128, 1024], F32) as ps_d,
        ):
            ph = Phase(f"{u}_mlp")
            cT = Clk(pb, f"{u}4T")
            cS = Clk(pb, f"{u}4S")
            cV = Clk(pb, f"{u}4V")
            usl = [f"{u}4u0", f"{u}4u1"]
            dsl = [f"{u}4d0", f"{u}4d1"]
            ubase = pb.bases(*usl)
            dbase = pb.bases(*dsl)

            def umm(jp, g): return cT.base + jp * 64 + g + 1
            def dmm(jp, g2): return cT.base + jp * 64 + 32 + g2 + 1
            def reld(jp, g): return cS.base + jp * 32 + g + 1
            def hd(jp, g): return cV.base + jp * 64 + g + 1
            def yad(jp, g2): return cV.base + jp * 64 + 32 + g2 + 1

            @ph.on("gpsimd")
            def _(e):
                for jp in range(4):
                    for jc in range(16):
                        uu = jp * 16 + jc
                        if uu >= 2:
                            jp0, g0 = divmod((uu - 2) * 2 + 1, 32)
                            cT.wait(e, umm(jp0, g0))
                        ins = e.dma_start(out=upt[:, (uu % 2) * 2048:(uu % 2 + 1) * 2048],
                                          in_=w["up"][uu])
                        pb.inc(ins, usl[uu % 2], 16)
                    for db in range(4):
                        d = jp * 4 + db
                        if d >= 2:
                            jp0, g0 = divmod((d - 2) * 8 + 7, 32)
                            cT.wait(e, dmm(jp0, g0))
                        ins = e.dma_start(out=dnb[:, (d % 2) * 8192:(d % 2 + 1) * 8192],
                                          in_=w["dn"][d])
                        pb.inc(ins, dsl[d % 2], 16)

            @ph.on("tensor")
            def _(e):
                for jp in range(4):
                    for jc in range(16):
                        uu = jp * 16 + jc
                        e.wait_ge(pb.sem(usl[uu % 2]), ubase[usl[uu % 2]] + 16 * (uu // 2 + 1))
                        for tb in range(2):
                            g = jc * 2 + tb
                            ag = uu * 2 + tb
                            if ag >= 4:
                                cV.wait(e, hd((ag - 4) // 32, (ag - 4) % 32))
                            sl = ((jc % 2) * 2 + tb) * 512
                            for dt in range(ND):
                                ins = e.matmul(
                                    ps_u[:, sl:sl + 512],
                                    upt[:, (uu % 2) * 2048 + dt * 128:(uu % 2) * 2048 + (dt + 1) * 128],
                                    x_fm[:, dt * T + tb * 512:dt * T + tb * 512 + 512],
                                    start=(dt == 0), stop=(dt == ND - 1))
                            cT.tick(ins)
                    for db in range(4):
                        d = jp * 4 + db
                        e.wait_ge(pb.sem(dsl[d % 2]), dbase[dsl[d % 2]] + 16 * (d // 2 + 1))
                        cV.wait(e, hd(jp, 31))
                        for tt in range(NT):
                            g2 = db * NT + tt
                            if g2 >= 2 or jp > 0:
                                pjp, pg = (jp, g2 - 2) if g2 >= 2 else (jp - 1, 30 + g2)
                                cV.wait(e, yad(pjp, pg))
                            for jc in range(16):
                                ins = e.matmul(
                                    ps_d[:, (tt % 2) * 512:(tt % 2) * 512 + 512],
                                    h_pan[:, jc * T + tt * 128:jc * T + tt * 128 + 128],
                                    dnb[:, (d % 2) * 8192 + jc * 512:(d % 2) * 8192 + (jc + 1) * 512],
                                    start=(jc == 0), stop=(jc == 15))
                            cT.tick(ins)

            @ph.on("scalar")
            def _(e):
                for jp in range(4):
                    for g in range(32):
                        cT.wait(e, umm(jp, g))
                        ag = jp * 32 + g
                        if ag >= 2:
                            cV.wait(e, hd((ag - 2) // 32, (ag - 2) % 32))
                        cS.guard(e)
                        sl = (((g // 2) % 2) * 2 + (g % 2)) * 512
                        cS.tick(e.activation(rlu[:, (ag % 2) * 512:(ag % 2) * 512 + 512],
                                             ps_u[:, sl:sl + 512], AF.Relu))

            @ph.on("vector")
            def _(e):
                for jp in range(4):
                    for g in range(32):
                        jc, tb = g // 2, g % 2
                        ag = jp * 32 + g
                        cS.wait(e, reld(jp, g))
                        cV.guard(e)
                        sl = ((jc % 2) * 2 + tb) * 512
                        ins = e.scalar_tensor_tensor(
                            h_pan[:, jc * T + tb * 512:jc * T + tb * 512 + 512],
                            rlu[:, (ag % 2) * 512:(ag % 2) * 512 + 512],
                            1.0, ps_u[:, sl:sl + 512], ALU.mult, ALU.mult)
                        cV.tick(ins)
                    for g2 in range(32):
                        cT.wait(e, dmm(jp, g2))
                        tt = g2 % NT
                        db = g2 // NT
                        cV.guard(e)
                        ins = e.tensor_add(y_acc[:, tt * D + db * 512:tt * D + db * 512 + 512],
                                           y_acc[:, tt * D + db * 512:tt * D + db * 512 + 512],
                                           ps_d[:, (tt % 2) * 512:(tt % 2) * 512 + 512])
                        cV.tick(ins)

            pb.emit(ph)
            cT.check(4 * 64)
            cS.check(4 * 32)
            cV.check(4 * 64)

        with (
            nc.sbuf_tensor(f"{u}_rt2", [128, 2 * D], F32) as rt2,
            nc.sbuf_tensor(f"{u}_ro", [128, 2 * D], F32) as ro,
            nc.sbuf_tensor(f"{u}_yh", [128, 2 * D], F32) as yh,
            nc.sbuf_tensor(f"{u}_st2", [128, 16], F32) as st2,
            nc.sbuf_tensor(f"{u}_gb", [128, D], F32) as gb,
            nc.sbuf_tensor(f"{u}_bb", [128, D], F32) as bb,
        ):
            ph = Phase(f"{u}_out")
            cS = Clk(pb, f"{u}5S")
            cV = Clk(pb, f"{u}5V")
            dsl = [f"{u}5d0", f"{u}5d1"]
            osl = [f"{u}5o0", f"{u}5o1"]
            gds = f"{u}5g"
            dbase = pb.bases(*dsl, *osl, gds)

            def sqd(i): return cS.base + 4 * i + 1
            def lnd(i): return cS.base + 4 * i + 2
            def expd(i): return cS.base + 4 * i + 3
            def yhd(i): return cS.base + 4 * (i + 1)
            def varpd(i): return cV.base + 11 * i + 6
            def negbd(i): return cV.base + 11 * i + 8
            def rod(i): return cV.base + 11 * (i + 1)

            @ph.on("sync")
            def _(e):
                pb.inc(e.dma_start(out=gb[:], in_=w["gb"]), gds, 16)
                pb.inc(e.dma_start(out=bb[:], in_=w["bb"]), gds, 16)
                for tt in range(NT):
                    if tt >= 2:
                        cV.wait(e, rod(tt - 2))
                        ins = e.dma_start(out=dst_r[(tt - 2) * 128:(tt - 1) * 128, :],
                                          in_=ro[:, (tt % 2) * D:(tt % 2 + 1) * D])
                        pb.inc(ins, osl[tt % 2], 16)
                    ins = e.dma_start(out=rt2[:, (tt % 2) * D:(tt % 2 + 1) * D],
                                      in_=src_r[tt * 128:(tt + 1) * 128, :])
                    pb.inc(ins, dsl[tt % 2], 16)
                for tt in range(NT - 2, NT):
                    cV.wait(e, rod(tt))
                    ins = e.dma_start(out=dst_r[tt * 128:(tt + 1) * 128, :],
                                      in_=ro[:, (tt % 2) * D:(tt % 2 + 1) * D])
                    pb.inc(ins, osl[tt % 2], 16)
                e.wait_ge(pb.sem(osl[0]), dbase[osl[0]] + 16 * 4)
                e.wait_ge(pb.sem(osl[1]), dbase[osl[1]] + 16 * 4)

            @ph.on("scalar")
            def _(e):
                for tt in range(NT):
                    c = tt % 2
                    yt = y_acc[:, tt * D:(tt + 1) * D]
                    if tt >= 2:
                        cV.wait(e, rod(tt - 2))
                    cS.guard(e)
                    cS.tick(e.activation(yh[:, c * D:(c + 1) * D], yt, AF.Square,
                                         accum_out=st2[:, 1:2]))
                    cV.wait(e, varpd(tt))
                    cS.guard(e)
                    cS.tick(e.activation(st2[:, 8:9], st2[:, 6:7], AF.Ln))
                    cS.guard(e)
                    cS.tick(e.activation(st2[:, 10 + c:11 + c], st2[:, 8:9], AF.Exp, scale=-0.5))
                    cV.wait(e, negbd(tt))
                    cS.guard(e)
                    cS.tick(e.activation(yh[:, c * D:(c + 1) * D], yt, AF.Identity,
                                         bias=st2[:, 12 + c:13 + c],
                                         scale=st2[:, 10 + c:11 + c]))

            @ph.on("vector")
            def _(e):
                for tt in range(NT):
                    c = tt % 2
                    yt = y_acc[:, tt * D:(tt + 1) * D]
                    cV.guard(e)
                    cV.tick(e.reduce_sum(st2[:, 0:1], yt, AX.X))
                    cS.wait(e, sqd(tt))
                    if tt >= 1:
                        cS.wait(e, lnd(tt - 1))
                    for args in ((st2[:, 2:3], st2[:, 0:1], 1.0 / D),
                                 (st2[:, 3:4], st2[:, 1:2], 1.0 / D)):
                        cV.guard(e)
                        cV.tick(e.tensor_scalar_mul(*args))
                    cV.guard(e)
                    cV.tick(e.tensor_mul(st2[:, 4:5], st2[:, 2:3], st2[:, 2:3]))
                    cV.guard(e)
                    cV.tick(e.tensor_sub(st2[:, 5:6], st2[:, 3:4], st2[:, 4:5]))
                    cV.guard(e)
                    cV.tick(e.tensor_scalar_add(st2[:, 6:7], st2[:, 5:6], LN_EPS))
                    cS.wait(e, expd(tt))
                    cV.guard(e)
                    cV.tick(e.tensor_mul(st2[:, 14:15], st2[:, 2:3], st2[:, 10 + c:11 + c]))
                    cV.guard(e)
                    cV.tick(e.tensor_scalar_mul(st2[:, 12 + c:13 + c], st2[:, 14:15], -1.0))
                    cS.wait(e, yhd(tt))
                    if tt == 0:
                        e.wait_ge(pb.sem(gds), dbase[gds] + 32)
                    e.wait_ge(pb.sem(dsl[tt % 2]), dbase[dsl[tt % 2]] + 16 * (tt // 2 + 1))
                    if tt >= 2:
                        e.wait_ge(pb.sem(osl[tt % 2]), dbase[osl[tt % 2]] + 16 * (tt // 2))
                    yhc = yh[:, c * D:(c + 1) * D]
                    cV.guard(e)
                    cV.tick(e.tensor_mul(yhc, yhc, gb[:]))
                    cV.guard(e)
                    cV.tick(e.tensor_add(yhc, yhc, bb[:]))
                    cV.guard(e)
                    cV.tick(e.tensor_add(ro[:, c * D:(c + 1) * D],
                                         rt2[:, c * D:(c + 1) * D], yhc))

            pb.emit(ph)
            cS.check(4 * NT)
            cV.check(11 * NT)



def _ffn_dram(nc):
    return dict(
        gate=nc.dram_tensor("gate_st", [128, 128], F32, kind="ExternalInput").ap(),
        ex=nc.dram_tensor("ex_st", [4, 128, 8192], F32, kind="ExternalInput").ap(),
        up=nc.dram_tensor("up_st", [64, 128, 2048], F32, kind="ExternalInput").ap(),
        dn=nc.dram_tensor("dn_st", [16, 128, 8192], F32, kind="ExternalInput").ap(),
        gb=nc.dram_tensor("gb_st", [128, D], F32, kind="ExternalInput").ap(),
        bb=nc.dram_tensor("bb_st", [128, D], F32, kind="ExternalInput").ap(),
    )


def _stage_ffn(inputs):
    up = np.ascontiguousarray(
        inputs["L0_ffn_up_w"].reshape(16, 128, 64, 128).transpose(2, 1, 0, 3).reshape(64, 128, 2048))
    dn = np.ascontiguousarray(
        inputs["L0_ffn_down_w"].reshape(4, 16, 128, 4, 512).transpose(0, 3, 2, 1, 4).reshape(16, 128, 8192))
    ex = np.ascontiguousarray(
        inputs["L0_ffn_expert_weight"].reshape(16, 128, 4, 512).transpose(2, 1, 0, 3).reshape(4, 128, 8192))
    gate = np.ascontiguousarray(
        inputs["L0_ffn_gate_w"].reshape(16, 128, 8).transpose(1, 0, 2).reshape(128, 128))
    gb = np.ascontiguousarray(np.broadcast_to(inputs["L0_ffn_ln_gamma"], (128, D)))
    bb = np.ascontiguousarray(np.broadcast_to(inputs["L0_ffn_ln_beta"], (128, D)))
    return dict(gate_st=gate, ex_st=ex, up_st=up, dn_st=dn, gb_st=gb, bb_st=bb)


def build_nc(dev=None, pairs=4):
    nc = bass.Bass("TRN2", target_bir_lowering=False)
    ident_d = nc.dram_tensor("ident", [128, 128], BF16, kind="ExternalInput").ap()
    onesb_d = nc.dram_tensor("ones_bf", [128, 1], BF16, kind="ExternalInput").ap()
    onesf_d = nc.dram_tensor("ones_f", [1, 128], F32, kind="ExternalInput").ap()
    out = nc.dram_tensor("out", [T, D], F32, kind="ExternalOutput").ap()
    wf = _ffn_dram(nc)
    wa = _a_dram(nc)
    wb = _b_dram(nc, pairs=pairs)

    pb = PB(nc)
    with (
        nc.sbuf_tensor("c_ident", [128, 128], BF16) as ident,
        nc.sbuf_tensor("c_onesb", [128, 1], BF16) as onesb,
        nc.sbuf_tensor("c_onesf", [1, 128], F32) as onesf,
    ):
        cn = dict(ident=ident, ones_bf=onesb, ones_f=onesf)
        ph = Phase("init")
        b0 = pb.bases("c_dma")

        @ph.on("sync")
        def _(e):
            pb.inc(e.dma_start(out=ident[:], in_=ident_d), "c_dma", 16)
            pb.inc(e.dma_start(out=onesb[:], in_=onesb_d), "c_dma", 16)
            pb.inc(e.dma_start(out=onesf[:], in_=onesf_d), "c_dma", 16)
            e.wait_ge(pb.sem("c_dma"), b0["c_dma"] + 48)
        pb.emit(ph)

        if dev == "ffn":
            dev_r = nc.dram_tensor("dev_r", [T, D], F32, kind="ExternalInput").ap()
            emit_ffn(pb, cn, wf, dev_r, out, 0)
        elif dev == "attn_a":
            emit_attn_a(pb, cn, wa, out)
        elif dev == "attn_b":
            dev_r = nc.dram_tensor("dev_r", [T, D], F32, kind="ExternalInput").ap()
            emit_attn_b(pb, cn, wb, dev_r, out)
        else:
            r1 = nc.dram_tensor("r1", [T, D], F32).ap()
            r2 = nc.dram_tensor("r2", [T, D], F32).ap()
            r3 = nc.dram_tensor("r3", [T, D], F32).ap()
            emit_attn_a(pb, cn, wa, r1)
            emit_ffn(pb, cn, wf, r1, r2, 0)
            emit_attn_b(pb, cn, wb, r2, r3)
            emit_ffn(pb, cn, wf, r3, out, 1)
    return nc


def _core_inputs(inputs, core):
    m = dict(ident=np.eye(128, dtype=ml_dtypes.bfloat16),
             ones_bf=np.ones((128, 1), ml_dtypes.bfloat16),
             ones_f=np.ones((1, 128), np.float32))
    m.update(_stage_ffn(inputs))
    m.update(_stage_attn_a(inputs, core))
    m.update(_stage_attn_b(inputs, core))
    return m


def kernel(**inputs):
    nc = build_nc()
    in_maps = [_core_inputs(inputs, c) for c in range(8)]
    res = run_bass_kernel_spmd(nc, in_maps, core_ids=list(range(8)))
    shards = [res.results[c]["out"] for c in range(8)]
    out = np.empty((4, CTX, D), np.float32)
    for c in range(8):
        out[c // 2, (c % 2) * T:(c % 2 + 1) * T, :] = shards[c]
    return out


QLR = 1536
KVLR = 512
DR = 64
DQ = 192
A_EPS = 1e-6
A_SCALE = DQ ** -0.5


_DBG = [False]


def _a_dram(nc):
    d = {}
    if _DBG[0]:
        d.update(
            dbg_qn=nc.dram_tensor("dbg_qn", [128, 16 * 1024], BF16, kind="ExternalOutput").ap(),
            dbg_ckvn=nc.dram_tensor("dbg_ckvn", [128, 4 * 2048], BF16, kind="ExternalOutput").ap(),
            dbg_kper=nc.dram_tensor("dbg_kper", [128, 2048], BF16, kind="ExternalOutput").ap(),
            dbg_qper=nc.dram_tensor("dbg_qper", [128, 8 * 1024], BF16, kind="ExternalOutput").ap(),
            dbg_o=nc.dram_tensor("dbg_o", [128, 16 * 1024], BF16, kind="ExternalOutput").ap(),
        )
    d.update(
        x_ctx=nc.dram_tensor("x_ctx", [CTX, D], F32, kind="ExternalInput").ap(),
        x_own=nc.dram_tensor("x_own", [T, D], F32, kind="ExternalInput").ap(),
        wqa=nc.dram_tensor("wqa_img", [128, 16 * 1536], F32, kind="ExternalInput").ap(),
        wqbN=nc.dram_tensor("wqbN_img", [128, 12 * 2048], F32, kind="ExternalInput").ap(),
        wqbP=nc.dram_tensor("wqbP_img", [128, 12 * 1024], F32, kind="ExternalInput").ap(),
        wkva=nc.dram_tensor("wkva_img", [128, 16 * 512], F32, kind="ExternalInput").ap(),
        wkvaP=nc.dram_tensor("wkvaP_img", [128, 16 * 64], F32, kind="ExternalInput").ap(),
        wkvbK=nc.dram_tensor("wkvbK_img", [128, 4 * 2048], F32, kind="ExternalInput").ap(),
        wkvbV=nc.dram_tensor("wkvbV_img", [128, 4 * 2048], F32, kind="ExternalInput").ap(),
        wo=nc.dram_tensor("wo_img", [128, 16 * 2048], F32, kind="ExternalInput").ap(),
        qg=nc.dram_tensor("qg_img", [128, 12], F32, kind="ExternalInput").ap(),
        kvg=nc.dram_tensor("kvg_img", [128, 4], F32, kind="ExternalInput").ap(),
        cosq=nc.dram_tensor("cosq_img", [128, 8 * 64], F32, kind="ExternalInput").ap(),
        sinq=nc.dram_tensor("sinq_img", [128, 8 * 64], F32, kind="ExternalInput").ap(),
        cosk=nc.dram_tensor("cosk_img", [128, 16 * 64], F32, kind="ExternalInput").ap(),
        sink=nc.dram_tensor("sink_img", [128, 16 * 64], F32, kind="ExternalInput").ap(),
        mask=nc.dram_tensor("mask_img", [8, 128, 2048], BF16, kind="ExternalInput").ap(),
    )
    return d


def _rope_tables(positions):
    inv = 1.0 / (10000.0 ** (np.arange(0, DR, 2, dtype=np.float64) / DR))
    fr = positions[:, None].astype(np.float64) * inv[None, :]
    emb = np.concatenate([fr, fr], -1)
    return np.cos(emb).astype(np.float32), np.sin(emb).astype(np.float32)


def _stage_attn_a(inputs, core):
    seq, hf = core // 2, core % 2
    x = inputs["L0_attn_a_hidden_states"][seq]
    w_qa = inputs["L0_attn_a_w_qa"]
    w_qb = inputs["L0_attn_a_w_qb"]
    w_kva = inputs["L0_attn_a_w_kva"]
    w_kvb = inputs["L0_attn_a_w_kvb"]
    w_o = inputs["L0_attn_a_w_o"]
    m = {}
    m["x_ctx"] = np.ascontiguousarray(x)
    m["x_own"] = np.ascontiguousarray(x[hf * T:(hf + 1) * T])
    m["wqa_img"] = np.ascontiguousarray(
        w_qa.reshape(1536, 16, 128).transpose(2, 1, 0).reshape(128, 16 * 1536))
    wqb3 = w_qb.reshape(H, 192, 1536)
    wqbN = wqb3[:, :128, :].reshape(2048, 1536)
    wqbP = wqb3[:, 128:, :].reshape(1024, 1536)
    m["wqbN_img"] = np.ascontiguousarray(
        wqbN.transpose(1, 0).reshape(12, 128, 2048).transpose(1, 0, 2).reshape(128, 12 * 2048))
    m["wqbP_img"] = np.ascontiguousarray(
        wqbP.transpose(1, 0).reshape(12, 128, 1024).transpose(1, 0, 2).reshape(128, 12 * 1024))
    m["wkva_img"] = np.ascontiguousarray(
        w_kva[:512].reshape(512, 16, 128).transpose(2, 1, 0).reshape(128, 16 * 512))
    m["wkvaP_img"] = np.ascontiguousarray(
        w_kva[512:].reshape(64, 16, 128).transpose(2, 1, 0).reshape(128, 16 * 64))
    wkvb3 = w_kvb.reshape(H, 256, 512)
    wkvbK = wkvb3[:, :128, :].reshape(2048, 512)
    wkvbV = wkvb3[:, 128:, :].reshape(2048, 512)
    m["wkvbK_img"] = np.ascontiguousarray(
        wkvbK.transpose(1, 0).reshape(4, 128, 2048).transpose(1, 0, 2).reshape(128, 4 * 2048))
    m["wkvbV_img"] = np.ascontiguousarray(
        wkvbV.transpose(1, 0).reshape(4, 128, 2048).transpose(1, 0, 2).reshape(128, 4 * 2048))
    m["wo_img"] = np.ascontiguousarray(
        w_o.reshape(4, 512, 16, 128).transpose(3, 2, 0, 1).reshape(128, 16 * 2048))
    m["qg_img"] = np.ascontiguousarray(
        inputs["L0_attn_a_q_a_ln_w"].reshape(12, 128).transpose(1, 0))
    m["kvg_img"] = np.ascontiguousarray(
        inputs["L0_attn_a_kv_a_ln_w"].reshape(4, 128).transpose(1, 0))
    posq = np.arange(hf * T, (hf + 1) * T)
    cq_, sq_ = _rope_tables(posq)
    m["cosq_img"] = np.ascontiguousarray(cq_.reshape(8, 128, 64).transpose(1, 0, 2).reshape(128, 512))
    m["sinq_img"] = np.ascontiguousarray(sq_.reshape(8, 128, 64).transpose(1, 0, 2).reshape(128, 512))
    ck_, sk_ = _rope_tables(np.arange(CTX))
    m["cosk_img"] = np.ascontiguousarray(ck_.reshape(16, 128, 64).transpose(1, 0, 2).reshape(128, 1024))
    m["sink_img"] = np.ascontiguousarray(sk_.reshape(16, 128, 64).transpose(1, 0, 2).reshape(128, 1024))
    qpos = (hf * T + np.arange(T))[:, None]
    kpos = np.arange(CTX)[None, :]
    mask = np.where(kpos <= qpos, 0.0, BIG_NEG).astype(ml_dtypes.bfloat16)
    m["mask_img"] = np.ascontiguousarray(mask.reshape(8, 128, 2048))
    return m


class View:

    def __init__(self, t, tcols, off, n):
        self.t, self.tcols, self.off, self.n = t, tcols, off, n

    def __getitem__(self, idx):
        p, c = idx
        a = self.off + (c.start or 0)
        b = self.off + (c.stop if c.stop is not None else self.n)
        return self.t[p, a:b]

    def sap(self, off, chunks, step, width, p0=0, pn=128):
        return bass.AP(self.t, p0 * self.tcols + self.off + off,
                       [[self.tcols, pn], [step, chunks], [1, width]])


class Arena:
    def __init__(self, nc, name, cols_bf16, cols_f32):
        self._g1 = nc.sbuf_tensor(f"{name}_bf", [128, cols_bf16], BF16)
        self._g2 = nc.sbuf_tensor(f"{name}_f", [128, cols_f32], F32)
        self.tb = self._g1.__enter__()
        self.tf = self._g2.__enter__()
        self.nb, self.nf = cols_bf16, cols_f32
        self.free = {BF16: [(0, cols_bf16)], F32: [(0, cols_f32)]}
        self.live = {}

    def alloc(self, name, n, dt=BF16):
        fl = self.free[dt]
        for i, (a, b) in enumerate(fl):
            if b - a >= n:
                fl[i] = (a + n, b)
                if fl[i][0] == fl[i][1]:
                    fl.pop(i)
                self.live[name] = (a, n, dt)
                t = self.tb if dt == BF16 else self.tf
                return View(t, self.nb if dt == BF16 else self.nf, a, n)
        raise MemoryError(f"arena {name} {n} {dt}")

    def freev(self, *names):
        for name in names:
            a, n, dt = self.live.pop(name)
            fl = sorted(self.free[dt] + [(a, a + n)])
            out = []
            for seg in fl:
                if out and out[-1][1] == seg[0]:
                    out[-1] = (out[-1][0], seg[1])
                else:
                    out.append(seg)
            self.free[dt] = out

    def close(self):
        self._g2.__exit__(None, None, None)
        self._g1.__exit__(None, None, None)


def emit_attn_a(pb, cn, w, dst_r):
    nc = pb.nc
    ar = Arena(nc, "aa", 87040, 9216)
    ident = cn["ident"]
    ones_bf = cn["ones_bf"]
    ones_f = cn["ones_f"]

    h_ctx = ar.alloc("h_ctx", 16 * 2048)
    h_own = ar.alloc("h_own", 16 * 1024)
    wqa = ar.alloc("wqa", 16 * 1536)
    xln = ar.alloc("xln", 2 * D)
    rt = ar.alloc("rt", 3 * D, F32)
    cosk = ar.alloc("cosk", 1024, F32)
    sink = ar.alloc("sink", 1024, F32)
    cosq = ar.alloc("cosq", 512, F32)
    sinq = ar.alloc("sinq", 512, F32)
    tpsg = nc.psum_tensor("aa_tps", [128, 8 * 1024], BF16)
    tps = tpsg.__enter__()
    with (
        nc.sbuf_tensor("aa_st", [128, 16], F32) as stt_,
        nc.sbuf_tensor("aa_qg", [128, 12], F32) as qg,
        nc.sbuf_tensor("aa_kvg", [128, 4], F32) as kvg,
    ):
        ph = Phase("a1")
        dws = pb.bases("a1w", "a1wg")

        @ph.on("gpsimd")
        def _(e):
            pb.inc(e.dma_start(out=wqa[:, :], in_=w["wqa"]), "a1wg", 16)

        @ph.on("sync")
        def _(e):
            for dst, src in ((cosk, w["cosk"]), (sink, w["sink"]),
                             (cosq, w["cosq"]), (sinq, w["sinq"])):
                pb.inc(e.dma_start(out=dst[:, :], in_=src), "a1w", 16)
            pb.inc(e.dma_start(out=qg[:], in_=w["qg"]), "a1w", 16)
            pb.inc(e.dma_start(out=kvg[:], in_=w["kvg"]), "a1w", 16)

        src_tiles = [w["x_ctx"][i * 128:(i + 1) * 128, :] for i in range(16)] + \
                    [w["x_own"][i * 128:(i + 1) * 128, :] for i in range(8)]
        dsts = [(h_ctx, i * 128, 2048) for i in range(16)] + \
               [(h_own, i * 128, 1024) for i in range(8)]
        sb = dict(rt=rt, st=stt_, xln=xln, tps=tps, ident=ident)
        emit_ln_fm(pb, ph, "aln", src_tiles, None, 0, 0, sb, dsts=dsts)
        pb.emit(ph)
        for f in ph.fns.get("_check", []):
            f()
        tpsg.__exit__(None, None, None)

        ar.freev("xln", "rt")
        cq = ar.alloc("cq", 12 * 1024)
        sqc = ar.alloc("sqc", 1024)
        rstdq = ar.alloc("rstdq", 1024, F32)
        with (
            nc.psum_tensor("a2_ps", [128, 4 * 512], F32) as ps2,
            nc.psum_tensor("a2_pss", [128, 1024], F32) as pss,
            nc.psum_tensor("a2_psb", [128, 1024], F32) as psb,
        ):
            ph = Phase("a2")
            cT = Clk(pb, "a2T")
            cS = Clk(pb, "a2S")
            cV = Clk(pb, "a2V")
            dw2 = pb.bases("a2w")


            def cqd(g): return cT.base + g + 1
            def ssd(kt): return cT.base + 24 + kt + 1
            def bcd(nb): return cT.base + 36 + nb + 1

            def sqd(kt): return cS.base + kt + 1
            def rsd(): return cS.base + 12 + 2

            def evq(g): return cV.base + g + 1
            def cqn(j): return cV.base + 24 + j + 1

            @ph.on("tensor")
            def _(e):
                e.wait_ge(pb.sem("a1wg"), dws["a1wg"] + 16)
                e.wait_ge(pb.sem("a1w"), dws["a1w"] + 16 * 6)
                for qt_ in range(12):
                    for tb in range(2):
                        g = qt_ * 2 + tb
                        if g >= 4:
                            cV.wait(e, evq(g - 4))
                        for dt in range(16):
                            ins = e.matmul(ps2[:, (g % 4) * 512:(g % 4) * 512 + 512],
                                           wqa[:, dt * 1536 + qt_ * 128:dt * 1536 + qt_ * 128 + 128],
                                           h_own[:, dt * 1024 + tb * 512:dt * 1024 + tb * 512 + 512],
                                           start=(dt == 0), stop=(dt == 15))
                        cT.tick(ins)
                for kt in range(12):
                    cS.wait(e, sqd(kt))
                    for nb in range(2):
                        ins = e.matmul(pss[0:1, nb * 512:nb * 512 + 512],
                                       ones_bf[:], sqc[:, nb * 512:nb * 512 + 512],
                                       start=(kt == 0), stop=(kt == 11))
                    cT.tick(ins)
                cS.wait(e, rsd())
                for nb in range(2):
                    ins = e.matmul(psb[:, nb * 512:nb * 512 + 512], ones_f[:],
                                   rstdq[0:1, nb * 512:nb * 512 + 512],
                                   start=True, stop=True)
                    cT.tick(ins)

            @ph.on("scalar")
            def _(e):
                for kt in range(12):
                    cV.wait(e, evq(kt * 2 + 1))
                    if kt >= 1:
                        cT.wait(e, ssd(kt - 1))
                    cS.guard(e)
                    cS.tick(e.square(sqc[:, :], cq[:, kt * 1024:(kt + 1) * 1024]))
                cV.wait(e, cV.base + 25)
                cS.guard(e)
                cS.tick(e.activation(rstdq[0:1, 0:1024], rstdq[0:1, 0:1024], AF.Ln))
                cS.guard(e)
                cS.tick(e.activation(rstdq[0:1, 0:1024], rstdq[0:1, 0:1024], AF.Exp, scale=-0.5))

            @ph.on("vector")
            def _(e):
                for g in range(24):
                    cT.wait(e, cqd(g))
                    qt_, tb = g // 2, g % 2
                    cV.guard(e)
                    cV.tick(e.tensor_copy(cq[:, qt_ * 1024 + tb * 512:qt_ * 1024 + tb * 512 + 512],
                                          ps2[:, (g % 4) * 512:(g % 4) * 512 + 512]))
                cT.wait(e, ssd(11))
                cV.guard(e)
                cV.tick(e.tensor_scalar(rstdq[0:1, 0:1024], pss[0:1, 0:1024],
                                        1.0 / QLR, A_EPS, ALU.mult, ALU.add))
                for kt in range(12):
                    for tb in range(2):
                        cT.wait(e, bcd(tb))
                        cV.guard(e)
                        sl = cq[:, kt * 1024 + tb * 512:kt * 1024 + tb * 512 + 512]
                        ins = e.scalar_tensor_tensor(sl, sl, qg[:, kt:kt + 1],
                                                     psb[:, tb * 512:tb * 512 + 512],
                                                     ALU.mult, ALU.mult)
                        cV.tick(ins)

            pb.emit(ph)
            cT.check(24 + 12 + 2)
            cS.check(12 + 2)
            cV.check(24 + 1 + 24)

        ar.freev("h_own", "wqa", "sqc", "rstdq")
        wkva = ar.alloc("wkva", 16 * 512)
        wkvaP = ar.alloc("wkvaP", 16 * 64)
        ckvn = ar.alloc("ckvn", 4 * 2048)
        kper = ar.alloc("kper", 2048)
        kpet = ar.alloc("kpet", 1024)
        kpet2 = ar.alloc("kpet2", 1024)
        sqk = ar.alloc("sqk", 2048)
        rstdk = ar.alloc("rstdk", 2048, F32)
        ropet = ar.alloc("ropet", 128, F32)
        with (
            nc.psum_tensor("a3_ps", [128, 1024], F32) as ps3,
            nc.psum_tensor("a3_pss", [128, 2048], F32) as pss3,
            nc.psum_tensor("a3_tp", [128, 2 * 1024], BF16) as tpsk,
        ):
            ph = Phase("a3")
            cT = Clk(pb, "a3T")
            cS = Clk(pb, "a3S")
            cV = Clk(pb, "a3V")
            dk = pb.bases("a3k")

            def ckvd(g): return cT.base + g + 1
            def kpd(ct): return cT.base + 16 + ct + 1
            def ssd(kt): return cT.base + 32 + kt + 1
            def bcd(nb): return cT.base + 36 + nb + 1
            def tpd(ct): return cT.base + 40 + ct + 1
            def cevd(g): return cS.base + g + 1
            def kevd(ct): return cS.base + 16 + ct + 1
            def sqd(kt): return cS.base + 32 + kt + 1
            def rsd(): return cS.base + 38
            def rod(ct): return cV.base + 6 * (ct + 1)
            def nsd(j): return cV.base + 97 + j + 1
            def ted(ct): return cV.base + 113 + ct + 1

            @ph.on("gpsimd")
            def _(e):
                pb.inc(e.dma_start(out=wkva[:, :], in_=w["wkva"]), "a2w", 16)
                pb.inc(e.dma_start(out=wkvaP[:, :], in_=w["wkvaP"]), "a2w", 16)

            @ph.on("tensor")
            def _(e):
                e.wait_ge(pb.sem("a2w"), dw2["a2w"] + 32)
                for ft in range(4):
                    for nb in range(4):
                        g = ft * 4 + nb
                        if g >= 2:
                            cS.wait(e, cevd(g - 2))
                        for dt in range(16):
                            ins = e.matmul(ps3[:, (g % 2) * 512:(g % 2) * 512 + 512],
                                           wkva[:, dt * 512 + ft * 128:dt * 512 + ft * 128 + 128],
                                           h_ctx[:, dt * 2048 + nb * 512:dt * 2048 + nb * 512 + 512],
                                           start=(dt == 0), stop=(dt == 15))
                        cT.tick(ins)
                for ct in range(16):
                    if ct >= 2:
                        cS.wait(e, kevd(ct - 2))
                    else:
                        cS.wait(e, cevd(14 + ct))
                    for dt in range(16):
                        ins = e.matmul(ps3[:, (ct % 2) * 512:(ct % 2) * 512 + 64],
                                       h_ctx[:, dt * 2048 + ct * 128:dt * 2048 + ct * 128 + 128],
                                       wkvaP[:, dt * 64:(dt + 1) * 64],
                                       start=(dt == 0), stop=(dt == 15))
                    cT.tick(ins)
                for kt in range(4):
                    cS.wait(e, sqd(kt))
                    for nb in range(4):
                        ins = e.matmul(pss3[0:1, nb * 512:nb * 512 + 512],
                                       ones_bf[:], sqk[:, nb * 512:nb * 512 + 512],
                                       start=(kt == 0), stop=(kt == 3))
                    cT.tick(ins)
                cS.wait(e, rsd())
                for nb in range(4):
                    ins = e.matmul(pss3[:, nb * 512:nb * 512 + 512], ones_f[:],
                                   rstdk[0:1, nb * 512:nb * 512 + 512],
                                   start=True, stop=True)
                    cT.tick(ins)
                for ct in range(16):
                    cV.wait(e, rod(ct))
                    if ct >= 2:
                        cV.wait(e, ted(ct - 2))
                    ins = e.transpose(tpsk[0:64, (ct % 2) * 1024:(ct % 2) * 1024 + 128],
                                      kpet2[:, ct * 64:(ct + 1) * 64], ident[:, :])
                    cT.tick(ins)

            @ph.on("scalar")
            def _(e):
                for g in range(16):
                    ft, nb = g // 4, g % 4
                    cT.wait(e, ckvd(g))
                    cS.guard(e)
                    cS.tick(e.copy(ckvn[:, ft * 2048 + nb * 512:ft * 2048 + nb * 512 + 512],
                                   ps3[:, (g % 2) * 512:(g % 2) * 512 + 512]))
                for ct in range(16):
                    cT.wait(e, kpd(ct))
                    cS.guard(e)
                    cS.tick(e.copy(kpet[:, ct * 64:(ct + 1) * 64],
                                   ps3[:, (ct % 2) * 512:(ct % 2) * 512 + 64]))
                for kt in range(4):
                    if kt >= 1:
                        cT.wait(e, ssd(kt - 1))
                    cS.guard(e)
                    cS.tick(e.square(sqk[:, :], ckvn[:, kt * 2048:(kt + 1) * 2048]))
                cV.wait(e, cV.base + 97)
                cS.guard(e)
                cS.tick(e.activation(rstdk[0:1, 0:2048], rstdk[0:1, 0:2048], AF.Ln))
                cS.guard(e)
                cS.tick(e.activation(rstdk[0:1, 0:2048], rstdk[0:1, 0:2048], AF.Exp, scale=-0.5))

            @ph.on("vector")
            def _(e):
                for ct in range(16):
                    cS.wait(e, kevd(ct))
                    b = ct * 64
                    ck0 = cosk[:, ct * 64:ct * 64 + 32]
                    ck1 = cosk[:, ct * 64 + 32:ct * 64 + 64]
                    sk0 = sink[:, ct * 64:ct * 64 + 32]
                    sk1 = sink[:, ct * 64 + 32:ct * 64 + 64]
                    lo = kpet[:, b:b + 32]
                    hi = kpet[:, b + 32:b + 64]
                    cV.guard(e)
                    cV.tick(e.tensor_mul(ropet[:, 0:32], lo, ck0))
                    cV.guard(e)
                    cV.tick(e.tensor_mul(ropet[:, 32:64], hi, sk0))
                    cV.guard(e)
                    cV.tick(e.tensor_sub(kpet2[:, b:b + 32], ropet[:, 0:32], ropet[:, 32:64]))
                    cV.guard(e)
                    cV.tick(e.tensor_mul(ropet[:, 64:96], hi, ck1))
                    cV.guard(e)
                    cV.tick(e.tensor_mul(ropet[:, 96:128], lo, sk1))
                    cV.guard(e)
                    cV.tick(e.tensor_add(kpet2[:, b + 32:b + 64], ropet[:, 64:96], ropet[:, 96:128]))
                cT.wait(e, ssd(3))
                cV.guard(e)
                cV.tick(e.tensor_scalar(rstdk[0:1, 0:2048], pss3[0:1, 0:2048],
                                        1.0 / KVLR, A_EPS, ALU.mult, ALU.add))
                for kt in range(4):
                    for nb in range(4):
                        j = kt * 4 + nb
                        cT.wait(e, bcd(nb))
                        cV.guard(e)
                        sl = ckvn[:, kt * 2048 + nb * 512:kt * 2048 + nb * 512 + 512]
                        ins = e.scalar_tensor_tensor(sl, sl, kvg[:, kt:kt + 1],
                                                     pss3[:, nb * 512:nb * 512 + 512],
                                                     ALU.mult, ALU.mult)
                        cV.tick(ins)
                for ct in range(16):
                    cT.wait(e, tpd(ct))
                    cV.guard(e)
                    cV.tick(e.tensor_copy(kper[0:64, ct * 128:(ct + 1) * 128],
                                          tpsk[0:64, (ct % 2) * 1024:(ct % 2) * 1024 + 128]))

            @ph.on("sync")
            def _(e):
                cV.wait(e, ted(15))
                pb.inc(e.dma_start(out=kper[64:128, 0:2048], in_=kper[0:64, 0:2048]), "a3k", 16)
                e.wait_ge(pb.sem("a3k"), dk["a3k"] + 16)

            pb.emit(ph)
            cT.check(16 + 16 + 4 + 4 + 16)
            cS.check(16 + 16 + 4 + 2)
            cV.check(96 + 1 + 16 + 16)

        ar.freev("h_ctx", "wkva", "wkvaP", "kpet", "kpet2", "sqk", "rstdk", "ropet")
        wqbN = ar.alloc("wqbN", 12 * 2048)
        qn = ar.alloc("qn", 16 * 1024)
        with nc.psum_tensor("a4_ps", [128, 2048], F32) as ps4:
            ph = Phase("a4")
            cT = Clk(pb, "a4T")
            cS = Clk(pb, "a4S")
            d4 = pb.bases("a4w")

            def mmd(g): return cT.base + g + 1
            def evd(g): return cS.base + g + 1

            @ph.on("gpsimd")
            def _(e):
                pb.inc(e.dma_start(out=wqbN[:, :], in_=w["wqbN"]), "a4w", 16)

            @ph.on("tensor")
            def _(e):
                e.wait_ge(pb.sem("a4w"), d4["a4w"] + 16)
                for ft in range(16):
                    for tb in range(2):
                        g = ft * 2 + tb
                        if g >= 4:
                            cS.wait(e, evd(g - 4))
                        for kt in range(12):
                            ins = e.matmul(ps4[:, (g % 4) * 512:(g % 4) * 512 + 512],
                                           wqbN[:, kt * 2048 + ft * 128:kt * 2048 + ft * 128 + 128],
                                           cq[:, kt * 1024 + tb * 512:kt * 1024 + tb * 512 + 512],
                                           start=(kt == 0), stop=(kt == 11))
                        cT.tick(ins)

            @ph.on("scalar")
            def _(e):
                for g in range(32):
                    ft, tb = g // 2, g % 2
                    cT.wait(e, mmd(g))
                    cS.guard(e)
                    ins = e.activation(qn[:, ft * 1024 + tb * 512:ft * 1024 + tb * 512 + 512],
                                       ps4[:, (g % 4) * 512:(g % 4) * 512 + 512],
                                       AF.Copy, scale=A_SCALE)
                    cS.tick(ins)

            pb.emit(ph)
            cT.check(32)
            cS.check(32)

        ar.freev("wqbN")
        wqbP = ar.alloc("wqbP", 12 * 1024)
        qpet = ar.alloc("qpet", 8 * 1024)
        qpet2 = ar.alloc("qpet2", 8 * 1024)
        qper = ar.alloc("qper", 8 * 1024)
        ropeq = ar.alloc("ropeq", 4 * 256, F32)
        with (
            nc.psum_tensor("a5_ps", [128, 1024], F32) as ps5,
            nc.psum_tensor("a5_tp", [128, 2 * 1024], BF16) as tpsq,
        ):
            ph = Phase("a5")
            cT = Clk(pb, "a5T")
            cS = Clk(pb, "a5S")
            cV = Clk(pb, "a5V")
            d5 = pb.bases("a5w")

            def mmd(g): return cT.base + g + 1
            def tpd(j): return cT.base + 16 + j + 1
            def evd(g): return cS.base + g + 1
            def rod(h): return cV.base + 6 * (h + 1)
            def ted(j): return cV.base + 96 + j + 1

            @ph.on("gpsimd")
            def _(e):
                pb.inc(e.dma_start(out=wqbP[:, :], in_=w["wqbP"]), "a5w", 16)

            @ph.on("tensor")
            def _(e):
                e.wait_ge(pb.sem("a5w"), d5["a5w"] + 16)
                for tt in range(8):
                    for pbk in range(2):
                        g = tt * 2 + pbk
                        if g >= 2:
                            cS.wait(e, evd(g - 2))
                        for kt in range(12):
                            ins = e.matmul(ps5[:, (g % 2) * 512:(g % 2) * 512 + 512],
                                           cq[:, kt * 1024 + tt * 128:kt * 1024 + tt * 128 + 128],
                                           wqbP[:, kt * 1024 + pbk * 512:kt * 1024 + pbk * 512 + 512],
                                           start=(kt == 0), stop=(kt == 11))
                        cT.tick(ins)
                for tt in range(8):
                    for hp in range(8):
                        j = tt * 8 + hp
                        cV.wait(e, rod(2 * hp + 1))
                        if j >= 2:
                            cV.wait(e, ted(j - 2))
                        ins = e.transpose(tpsq[:, (j % 2) * 1024:(j % 2) * 1024 + 128],
                                          qpet2[:, tt * 1024 + hp * 128:tt * 1024 + (hp + 1) * 128],
                                          ident[:, :])
                        cT.tick(ins)

            @ph.on("scalar")
            def _(e):
                for g in range(16):
                    tt, pbk = g // 2, g % 2
                    cT.wait(e, mmd(g))
                    cS.guard(e)
                    ins = e.activation(qpet[:, tt * 1024 + pbk * 512:tt * 1024 + pbk * 512 + 512],
                                       ps5[:, (g % 2) * 512:(g % 2) * 512 + 512],
                                       AF.Copy, scale=A_SCALE)
                    cS.tick(ins)

            @ph.on("vector")
            def _(e):
                for h in range(16):
                    cS.wait(e, evd(15))
                    lo = qpet.sap(h * 64, 8, 1024, 32)
                    hi = qpet.sap(h * 64 + 32, 8, 1024, 32)
                    lo2 = qpet2.sap(h * 64, 8, 1024, 32)
                    hi2 = qpet2.sap(h * 64 + 32, 8, 1024, 32)
                    c0 = cosq.sap(0, 8, 64, 32)
                    c1 = cosq.sap(32, 8, 64, 32)
                    s0 = sinq.sap(0, 8, 64, 32)
                    s1 = sinq.sap(32, 8, 64, 32)
                    tA = ropeq.sap(0, 8, 32, 32)
                    tB = ropeq.sap(256, 8, 32, 32)
                    cV.guard(e)
                    cV.tick(e.tensor_mul(tA, lo, c0))
                    cV.guard(e)
                    cV.tick(e.tensor_mul(tB, hi, s0))
                    cV.guard(e)
                    cV.tick(e.tensor_sub(lo2, tA, tB))
                    cV.guard(e)
                    cV.tick(e.tensor_mul(tA, hi, c1))
                    cV.guard(e)
                    cV.tick(e.tensor_mul(tB, lo, s1))
                    cV.guard(e)
                    cV.tick(e.tensor_add(hi2, tA, tB))
                for tt in range(8):
                    for hp in range(8):
                        j = tt * 8 + hp
                        cT.wait(e, tpd(j))
                        cV.guard(e)
                        cV.tick(e.tensor_copy(qper[:, hp * 1024 + tt * 128:hp * 1024 + (tt + 1) * 128],
                                              tpsq[:, (j % 2) * 1024:(j % 2) * 1024 + 128]))

            pb.emit(ph)
            cT.check(16 + 64)
            cS.check(16)
            cV.check(96 + 64)

        ar.freev("cq", "wqbP", "qpet", "qpet2", "ropeq", "cosk", "sink", "cosq", "sinq")
        o_fm = ar.alloc("o_fm", 16 * 1024)
        wkbK = ar.alloc("wkbK", 4 * 2048)
        wkbV = ar.alloc("wkbV", 4 * 2048)
        knop = ar.alloc("knop", 2048)
        vh = ar.alloc("vh", 2 * 2048)
        probs = ar.alloc("probs", 2 * 2048)
        probsT = ar.alloc("probsT", 2 * 2048)
        mskb = ar.alloc("mskb", 2 * 2048)
        sc_m = ar.alloc("sc_m", 2 * 2048, F32)
        nmx = ar.alloc("nmx", 16, F32)
        with (
            nc.psum_tensor("a6_psA", [128, 1024], F32) as psA,
            nc.psum_tensor("a6_tp", [128, 4 * 1024], BF16) as tps6,
            nc.psum_tensor("a6_av", [128, 1024], F32) as psav,
        ):
            ph = Phase("a6")
            cT = Clk(pb, "a6T")
            cS = Clk(pb, "a6S")
            cV = Clk(pb, "a6V")
            d6 = pb.bases("a6w", "a6m0", "a6m1")

            def hq(n):
                return n // 8, n % 8

            Titems, Sitems, Vitems = [], [], []
            for h in range(H):
                Titems += [("kn", h, nb) for nb in range(4)]
                Titems += [("vg", h, ct) for ct in range(16)]
                Sitems += [("knev", h, nb) for nb in range(4)]
                Sitems += [("vev", h, b) for b in range(4)]
                for qt in range(NT):
                    n = h * 8 + qt
                    if n >= 2:
                        Titems += [("sc", n, 0), ("sc", n, 1)]
                        if n >= 4:
                            Titems.append(("av", n - 4))
                        Titems += [("tp", n - 2, k) for k in range(4)]
                        Titems += [("sc", n, 2), ("sc", n, 3)]
                        Titems += [("tp", n - 2, k) for k in range(4, 16)]
                    else:
                        Titems += [("sc", n, kb) for kb in range(4)]
                    if n >= 2:
                        Sitems += [("pte", n - 2, j) for j in range(4)]
                    if n >= 4:
                        Sitems.append(("oev", n - 4))
                    Sitems.append(("exp", n))
                    if n >= 2:
                        Vitems.append(("rcp", n - 2))
                        Vitems.append(("pml", n - 2))
                    Vitems += [("stt", n, kb) for kb in range(4)]
                    Vitems.append(("nmx", n))
            Titems += [("av", 124)] + [("tp", 126, k) for k in range(16)] + [("av", 125)] + [("tp", 127, k) for k in range(16)] + [("av", 126), ("av", 127)]
            Sitems += [("pte", 126, j) for j in range(4)] + [("pte", 127, j) for j in range(4)] + [("oev", n2) for n2 in (124, 125, 126, 127)]
            Vitems += [("rcp", 126), ("pml", 126), ("rcp", 127), ("pml", 127)]
            tmark = {it: i + 1 for i, it in enumerate(Titems)}
            smark = {it: i + 1 for i, it in enumerate(Sitems)}
            vmark = {it: i + 1 for i, it in enumerate(Vitems)}

            def tm(*it): return cT.base + tmark[it]
            def sm(*it): return cS.base + smark[it]
            def vm(*it): return cV.base + vmark[it]

            @ph.on("gpsimd")
            def _(e):
                pb.inc(e.dma_start(out=wkbK[:, :], in_=w["wkvbK"]), "a6w", 16)
                pb.inc(e.dma_start(out=wkbV[:, :], in_=w["wkvbV"]), "a6w", 16)

            @ph.on("sync")
            def _(e):
                for n in range(128):
                    h, qt = hq(n)
                    if n >= 2:
                        cV.wait(e, vm("stt", n - 2, 3))
                    ins = e.dma_start(out=mskb[:, (qt % 2) * 2048:(qt % 2 + 1) * 2048],
                                      in_=w["mask"][qt])
                    pb.inc(ins, f"a6m{qt % 2}", 16)

            @ph.on("tensor")
            def _(e):
                e.wait_ge(pb.sem("a6w"), d6["a6w"] + 32)
                for it in Titems:
                    if it[0] == "kn":
                        _, h, nb = it
                        if h >= 1:
                            cV.wait(e, vm("stt", h * 8 - 1, 3))
                        if nb >= 2:
                            cS.wait(e, sm("knev", h, nb - 2))
                        for kt in range(4):
                            ins = e.matmul(psA[:, (nb % 2) * 512:(nb % 2) * 512 + 512],
                                           wkbK[:, kt * 2048 + h * 128:kt * 2048 + h * 128 + 128],
                                           ckvn[:, kt * 2048 + nb * 512:kt * 2048 + nb * 512 + 512],
                                           start=(kt == 0), stop=(kt == 3))
                        cT.tick(ins)
                    elif it[0] == "vg":
                        _, h, ct = it
                        if ct < 8:
                            cS.wait(e, sm("knev", h, 2 + ct // 4))
                        else:
                            cS.wait(e, sm("vev", h, (ct - 8) // 4))
                        sl6 = ((ct // 4) % 2) * 512 + (ct % 4) * 128
                        for kt in range(4):
                            ins = e.matmul(psA[:, sl6:sl6 + 128],
                                           ckvn[:, kt * 2048 + ct * 128:kt * 2048 + ct * 128 + 128],
                                           wkbV[:, kt * 2048 + h * 128:kt * 2048 + h * 128 + 128],
                                           start=(kt == 0), stop=(kt == 3))
                        cT.tick(ins)
                    elif it[0] == "sc":
                        _, n, kb = it
                        h, qt = hq(n)
                        hv = (h % 2) * 64
                        hqc = (h // 2) * 1024
                        if qt == 0:
                            cS.wait(e, sm("vev", h, 3))
                        elif kb < 2:
                            cV.wait(e, vm("stt", n - 1, kb + 2))
                        if kb >= 2:
                            cV.wait(e, vm("stt", n, kb - 2))
                        ins = e.matmul(psA[:, (kb % 2) * 512:(kb % 2) * 512 + 512],
                                       qn[:, h * 1024 + qt * 128:h * 1024 + qt * 128 + 128],
                                       knop[:, kb * 512:(kb + 1) * 512],
                                       start=True, stop=False)
                        ins = e.matmul(psA[:, (kb % 2) * 512:(kb % 2) * 512 + 512],
                                       qper[hv:hv + 64, hqc + qt * 128:hqc + qt * 128 + 128],
                                       kper[hv:hv + 64, kb * 512:(kb + 1) * 512],
                                       start=False, stop=True)
                        cT.tick(ins)
                    elif it[0] == "tp":
                        _, n, k = it
                        h, qt = hq(n)
                        cp = (qt % 2) * 2048
                        cV.wait(e, vm("pml", n))
                        if k >= 4:
                            cS.wait(e, sm("pte", n, (k - 4) // 4))
                        elif n >= 1:
                            cS.wait(e, sm("pte", n - 1, 3))
                        ins = e.transpose(tps6[:, (k % 4) * 1024:(k % 4) * 1024 + 128],
                                          probs[:, cp + k * 128:cp + (k + 1) * 128], ident[:, :])
                        cT.tick(ins)
                    else:
                        n = it[1]
                        h, qt = hq(n)
                        cp = (qt % 2) * 2048
                        cS.wait(e, sm("pte", n, 3))
                        if n >= 2:
                            cS.wait(e, sm("oev", n - 2))
                        for kt in range(16):
                            ins = e.matmul(psav[:, (qt % 2) * 512:(qt % 2) * 512 + 128],
                                           vh[:, (h % 2) * 2048 + kt * 128:(h % 2) * 2048 + (kt + 1) * 128],
                                           probsT[:, cp + kt * 128:cp + (kt + 1) * 128],
                                           start=(kt == 0), stop=(kt == 15))
                        cT.tick(ins)

            @ph.on("scalar")
            def _(e):
                for it in Sitems:
                    if it[0] == "knev":
                        _, h, nb = it
                        cT.wait(e, tm("kn", h, nb))
                        cS.guard(e)
                        cS.tick(e.copy(knop[:, nb * 512:(nb + 1) * 512],
                                       psA[:, (nb % 2) * 512:(nb % 2) * 512 + 512]))
                    elif it[0] == "vev":
                        _, h, b = it
                        cT.wait(e, tm("vg", h, 4 * b + 3))
                        cS.guard(e)
                        cS.tick(e.copy(vh[:, (h % 2) * 2048 + b * 512:(h % 2) * 2048 + (b + 1) * 512],
                                       psA[:, (b % 2) * 512:(b % 2) * 512 + 512]))
                    elif it[0] == "exp":
                        n = it[1]
                        h, qt = hq(n)
                        c_ = qt % 2
                        cp = c_ * 2048
                        cV.wait(e, vm("nmx", n))
                        if n >= 2:
                            cT.wait(e, tm("tp", n - 2, 15))
                        cS.guard(e)
                        cS.tick(e.activation(probs[:, cp:cp + 2048], sc_m[:, cp:cp + 2048], AF.Exp,
                                             bias=nmx[:, c_:c_ + 1], accum_out=nmx[:, 2 + c_:3 + c_]))
                    elif it[0] == "pte":
                        _, n, j = it
                        h, qt = hq(n)
                        cp = (qt % 2) * 2048
                        cT.wait(e, tm("tp", n, 4 * j + 3))
                        cS.guard(e)
                        src = bass.AP(tps6, 0, [[4096, 128], [1024, 4], [1, 128]])
                        dst = bass.AP(probsT.t, probsT.off + cp + j * 512,
                                      [[probsT.tcols, 128], [128, 4], [1, 128]])
                        cS.tick(e.copy(dst, src))
                    else:
                        n = it[1]
                        h, qt = hq(n)
                        cT.wait(e, tm("av", n))
                        cS.guard(e)
                        cS.tick(e.copy(o_fm[:, h * 1024 + qt * 128:h * 1024 + qt * 128 + 128],
                                       psav[:, (qt % 2) * 512:(qt % 2) * 512 + 128]))

            @ph.on("vector")
            def _(e):
                for it in Vitems:
                    if it[0] == "stt":
                        _, n, kb = it
                        h, qt = hq(n)
                        c_ = qt % 2
                        cp = c_ * 2048
                        if kb == 0:
                            e.wait_ge(pb.sem(f"a6m{qt % 2}"), d6[f"a6m{qt % 2}"] + 16 * (n // 2 + 1))
                            if n >= 2:
                                cS.wait(e, sm("exp", n - 2))
                        cT.wait(e, tm("sc", n, kb))
                        cV.guard(e)
                        ins = e.scalar_tensor_tensor(
                            sc_m[:, cp + kb * 512:cp + (kb + 1) * 512],
                            psA[:, (kb % 2) * 512:(kb % 2) * 512 + 512], 1.0,
                            mskb[:, c_ * 2048 + kb * 512:c_ * 2048 + kb * 512 + 512],
                            ALU.mult, ALU.add)
                        cV.tick(ins)
                    elif it[0] == "nmx":
                        n = it[1]
                        h, qt = hq(n)
                        c_ = qt % 2
                        cp = c_ * 2048
                        cV.guard(e)
                        cV.tick(e.reduce_max(nmx[:, c_:c_ + 1], sc_m[:, cp:cp + 2048],
                                             AX.X, negate=True))
                    elif it[0] == "rcp":
                        n = it[1]
                        c_ = (n % 8) % 2
                        cS.wait(e, sm("exp", n))
                        cV.guard(e)
                        cV.tick(e.reciprocal(nmx[:, 4 + c_:5 + c_], nmx[:, 2 + c_:3 + c_]))
                    else:
                        n = it[1]
                        c_ = (n % 8) % 2
                        cp = c_ * 2048
                        cV.guard(e)
                        cV.tick(e.tensor_scalar_mul(probs[:, cp:cp + 2048], probs[:, cp:cp + 2048],
                                                    nmx[:, 4 + c_:5 + c_]))

            pb.emit(ph)
            cT.check(len(Titems))
            cS.check(len(Sitems))
            cV.check(len(Vitems))

        ar.freev("wkbK", "wkbV", "knop", "vh", "probs", "probsT", "mskb", "sc_m", "nmx",
                 "ckvn", "kper", "qn", "qper")
        wo_sb = ar.alloc("wo_sb", 16 * 2048)
        rtx = ar.alloc("rtx", 2 * D, F32)
        rox = ar.alloc("rox", 2 * D, F32)
        with nc.psum_tensor("a7_ps", [128, 1024], F32) as ps7:
            ph = Phase("a7")
            cT = Clk(pb, "a7T")
            cV = Clk(pb, "a7V")
            d7 = pb.bases("a7w", "a7x0", "a7x1", "a7o0", "a7o1")

            def mmd(g): return cT.base + g + 1
            def add_(tt, db): return cV.base + tt * 4 + db + 1

            @ph.on("gpsimd")
            def _(e):
                pb.inc(e.dma_start(out=wo_sb[:, :], in_=w["wo"]), "a7w", 16)

            @ph.on("sync")
            def _(e):
                for tt in range(NT):
                    if tt >= 2:
                        cV.wait(e, add_(tt - 2, 3))
                        ins = e.dma_start(out=dst_r[(tt - 2) * 128:(tt - 1) * 128, :],
                                          in_=rox[:, (tt % 2) * D:(tt % 2 + 1) * D])
                        pb.inc(ins, f"a7o{tt % 2}", 16)
                    ins = e.dma_start(out=rtx[:, (tt % 2) * D:(tt % 2 + 1) * D],
                                      in_=w["x_own"][tt * 128:(tt + 1) * 128, :])
                    pb.inc(ins, f"a7x{tt % 2}", 16)
                for tt in range(NT - 2, NT):
                    cV.wait(e, add_(tt, 3))
                    ins = e.dma_start(out=dst_r[tt * 128:(tt + 1) * 128, :],
                                      in_=rox[:, (tt % 2) * D:(tt % 2 + 1) * D])
                    pb.inc(ins, f"a7o{tt % 2}", 16)
                e.wait_ge(pb.sem("a7o0"), d7["a7o0"] + 16 * 4)
                e.wait_ge(pb.sem("a7o1"), d7["a7o1"] + 16 * 4)

            @ph.on("tensor")
            def _(e):
                e.wait_ge(pb.sem("a7w"), d7["a7w"] + 16)
                for tt in range(NT):
                    for db in range(4):
                        g = tt * 4 + db
                        if g >= 2:
                            cV.wait(e, add_((g - 2) // 4, (g - 2) % 4))
                        for kt in range(16):
                            ins = e.matmul(ps7[:, (g % 2) * 512:(g % 2) * 512 + 512],
                                           o_fm[:, kt * 1024 + tt * 128:kt * 1024 + tt * 128 + 128],
                                           wo_sb[:, kt * 2048 + db * 512:kt * 2048 + db * 512 + 512],
                                           start=(kt == 0), stop=(kt == 15))
                        cT.tick(ins)

            @ph.on("vector")
            def _(e):
                for tt in range(NT):
                    for db in range(4):
                        g = tt * 4 + db
                        cT.wait(e, mmd(g))
                        e.wait_ge(pb.sem(f"a7x{tt % 2}"), d7[f"a7x{tt % 2}"] + 16 * (tt // 2 + 1))
                        if tt >= 2 and db == 0:
                            e.wait_ge(pb.sem(f"a7o{tt % 2}"), d7[f"a7o{tt % 2}"] + 16 * (tt // 2))
                        cV.guard(e)
                        ins = e.tensor_add(rox[:, (tt % 2) * D + db * 512:(tt % 2) * D + db * 512 + 512],
                                           rtx[:, (tt % 2) * D + db * 512:(tt % 2) * D + db * 512 + 512],
                                           ps7[:, (g % 2) * 512:(g % 2) * 512 + 512])
                        cV.tick(ins)

            pb.emit(ph)
            cT.check(32)
            cV.check(32)

    ar.freev("wo_sb", "rtx", "rox", "o_fm")
    ar.close()


B_EPS = 1e-5
CH = 128
NCH = 8


def _b_dram(nc, pairs=4):
    d = dict(
        wq=nc.dram_tensor("bwq_img", [4, 128, 16 * 512], F32, kind="ExternalInput").ap(),
        wk=nc.dram_tensor("bwk_img", [4, 128, 16 * 512], F32, kind="ExternalInput").ap(),
        wv=nc.dram_tensor("bwv_img", [4, 128, 16 * 512], F32, kind="ExternalInput").ap(),
        wo=nc.dram_tensor("bwo_img", [128, 16 * 2048], F32, kind="ExternalInput").ap(),
        wg1=nc.dram_tensor("bwg1_img", [128, 16 * 128], F32, kind="ExternalInput").ap(),
        wg2=nc.dram_tensor("bwg2_img", [128, 2048], F32, kind="ExternalInput").ap(),
        nw=nc.dram_tensor("bnw_img", [128, 16], F32, kind="ExternalInput").ap(),
        ldr=nc.dram_tensor("bld_img", [1, 16], F32, kind="ExternalInput").ap(),
        diffX=nc.dram_tensor("bdiff_img", [128, 128], F32, kind="ExternalInput").ap(),
        iota1=nc.dram_tensor("biota_img", [128, 128], F32, kind="ExternalInput").ap(),
        revj=nc.dram_tensor("brevj_img", [128, 1], F32, kind="ExternalInput").ap(),
        csteps=nc.dram_tensor("bcst_img", [128, 8], F32, kind="ExternalInput").ap(),
        flag=nc.dram_tensor("bflag_img", [128, 1], F32, kind="ExternalInput").ap(),
        cc_in=nc.dram_tensor("bcc_in", [128, 2048], F32).ap(),
        cc_out=nc.dram_tensor("bcc_out", [256, 2048], F32).ap(),
    )
    d["pairs"] = [[2 * i, 2 * i + 1] for i in range(pairs)]
    return d


def _stage_attn_b(inputs, core):
    hf = core % 2
    m = {}
    for nm, key in (("bwq_img", "L0_attn_b_Wq"), ("bwk_img", "L0_attn_b_Wk"),
                    ("bwv_img", "L0_attn_b_Wv")):
        W = inputs[key]
        m[nm] = np.ascontiguousarray(
            W.reshape(16, 128, 4, 512).transpose(2, 1, 0, 3).reshape(4, 128, 16 * 512))
    m["bwo_img"] = np.ascontiguousarray(
        inputs["L0_attn_b_Wo"].reshape(16, 128, 2048).transpose(1, 0, 2).reshape(128, 16 * 2048))
    m["bwg1_img"] = np.ascontiguousarray(
        inputs["L0_attn_b_Wg1"].reshape(16, 128, 128).transpose(1, 0, 2).reshape(128, 2048))
    m["bwg2_img"] = np.ascontiguousarray(inputs["L0_attn_b_Wg2"])
    m["bnw_img"] = np.ascontiguousarray(
        inputs["L0_attn_b_norm_w"].reshape(16, 128).transpose(1, 0))
    m["bld_img"] = np.ascontiguousarray(inputs["L0_attn_b_log_decay"].reshape(1, 16))
    i_ = np.arange(128)
    dif = i_[None, :] - i_[:, None]
    m["bdiff_img"] = np.where(dif >= 0, dif, 1e30).astype(np.float32)
    m["biota_img"] = np.broadcast_to((i_ + 1).astype(np.float32)[None, :], (128, 128)).copy()
    m["brevj_img"] = (127.0 - i_).astype(np.float32).reshape(128, 1).copy()
    m["bcst_img"] = np.broadcast_to((128.0 * np.arange(8)).astype(np.float32)[None, :],
                                    (128, 8)).copy()
    m["bflag_img"] = np.full((128, 1), float(hf), np.float32)
    return m


def emit_attn_b(pb, cn, w, src_r, dst_r):
    nc = pb.nc
    ar = Arena(nc, "bb", 87040, 9216)
    ident = cn["ident"]
    ones_bf = cn["ones_bf"]
    ones_f = cn["ones_f"]

    x_fm = ar.alloc("x_fm", 16 * 1024)
    xln = ar.alloc("xln", 2 * D)
    rt = ar.alloc("rt", 3 * D, F32)
    tpsg = nc.psum_tensor("bb_tps", [128, 8 * 1024], BF16)
    tps = tpsg.__enter__()
    with nc.sbuf_tensor("bln_st", [128, 16], F32) as stt_:
        sb = dict(rt=rt, st=stt_, xln=xln, tps=tps, ident=ident)
        ph = Phase("c1")
        tiles = [src_r[i * 128:(i + 1) * 128, :] for i in range(NT)]
        emit_ln_fm(pb, ph, "bln", tiles, x_fm, 0, T, sb)
        pb.emit(ph)
        for f in ph.fns.get("_check", []):
            f()
    tpsg.__exit__(None, None, None)
    ar.freev("xln", "rt")

    q_tm = ar.alloc("q_tm", 16384)
    k_tm = ar.alloc("k_tm", 16384)
    v_tm = ar.alloc("v_tm", 16384)
    g1 = ar.alloc("g1", 1024)
    wpan = ar.alloc("wpan", 2 * 8192)
    wg1s = ar.alloc("wg1s", 2048)
    with nc.psum_tensor("c2_ps", [128, 2048], F32) as ps2:
        ph = Phase("c2")
        cT = Clk(pb, "c2T")
        cS = Clk(pb, "c2S")
        d2 = pb.bases("c2d0", "c2d1", "c2g")
        tms = [q_tm, k_tm, v_tm]

        def mmd(g): return cT.base + g + 1
        def evd(g): return cS.base + g + 1

        @ph.on("gpsimd")
        def _(e):
            pb.inc(e.dma_start(out=wg1s[:, :], in_=w["wg1"]), "c2g", 16)
            for pi, wkey in enumerate(("wq", "wk", "wv")):
                for fb in range(4):
                    u = pi * 4 + fb
                    if u >= 2:
                        cT.wait(e, mmd((u - 1) * 8))
                    ins = e.dma_start(out=wpan[:, (u % 2) * 8192:(u % 2 + 1) * 8192],
                                      in_=w[wkey][fb])
                    pb.inc(ins, f"c2d{u % 2}", 16)

        @ph.on("tensor")
        def _(e):
            for pi in range(3):
                for fb in range(4):
                    u = pi * 4 + fb
                    e.wait_ge(pb.sem(f"c2d{u % 2}"), d2[f"c2d{u % 2}"] + 16 * (u // 2 + 1))
                    for tt in range(NT):
                        g = u * 8 + tt
                        if g >= 4:
                            cS.wait(e, evd(g - 4))
                        for kt in range(16):
                            ins = e.matmul(
                                ps2[:, (g % 4) * 512:(g % 4) * 512 + 512],
                                x_fm[:, kt * T + tt * 128:kt * T + tt * 128 + 128],
                                wpan[:, (u % 2) * 8192 + kt * 512:(u % 2) * 8192 + (kt + 1) * 512],
                                start=(kt == 0), stop=(kt == 15))
                        cT.tick(ins)
            e.wait_ge(pb.sem("c2g"), d2["c2g"] + 16)
            for tb in range(2):
                g = 96 + tb
                cS.wait(e, evd(g - 4))
                for kt in range(16):
                    ins = e.matmul(ps2[:, (g % 4) * 512:(g % 4) * 512 + 512],
                                   wg1s[:, kt * 128:(kt + 1) * 128],
                                   x_fm[:, kt * T + tb * 512:kt * T + tb * 512 + 512],
                                   start=(kt == 0), stop=(kt == 15))
                cT.tick(ins)

        @ph.on("scalar")
        def _(e):
            for g in range(96):
                u, tt = g // 8, g % 8
                pi, fb = u // 4, u % 4
                cT.wait(e, mmd(g))
                cS.guard(e)
                ins = e.activation(tms[pi][:, tt * 2048 + fb * 512:tt * 2048 + fb * 512 + 512],
                                   ps2[:, (g % 4) * 512:(g % 4) * 512 + 512], AF.Silu)
                cS.tick(ins)
            for tb in range(2):
                g = 96 + tb
                cT.wait(e, mmd(g))
                cS.guard(e)
                cS.tick(e.copy(g1[:, tb * 512:tb * 512 + 512],
                               ps2[:, (g % 4) * 512:(g % 4) * 512 + 512]))

        pb.emit(ph)
        cT.check(98)
        cS.check(98)
    ar.freev("x_fm", "wpan", "wg1s")

    q_fm = ar.alloc("q_fm", 16384)
    k_fm = ar.alloc("k_fm", 16384)
    with nc.psum_tensor("c3_tp", [128, 4 * 1024], BF16) as tps3:
        ph = Phase("c3")
        cT = Clk(pb, "c3T")
        cV = Clk(pb, "c3V")

        def tpd(j): return cT.base + j + 1
        def evd(j): return cV.base + j + 1

        @ph.on("tensor")
        def _(e):
            for j in range(256):
                pi = j // 128
                c = (j % 128) // 16
                h = j % 16
                if j >= 4:
                    cV.wait(e, evd(j - 4))
                src = (q_tm if pi == 0 else k_tm)
                ins = e.transpose(tps3[:, (j % 4) * 1024:(j % 4) * 1024 + 128],
                                  src[:, c * 2048 + h * 128:c * 2048 + (h + 1) * 128],
                                  ident[:, :])
                cT.tick(ins)

        @ph.on("vector")
        def _(e):
            for j in range(256):
                pi = j // 128
                c = (j % 128) // 16
                h = j % 16
                cT.wait(e, tpd(j))
                dst = (q_fm if pi == 0 else k_fm)
                cV.guard(e)
                cV.tick(e.tensor_copy(dst[:, h * 1024 + c * 128:h * 1024 + (c + 1) * 128],
                                      tps3[:, (j % 4) * 1024:(j % 4) * 1024 + 128]))

        pb.emit(ph)
        cT.check(256)
        cV.check(256)
    ar.freev("q_tm")

    ldb = ar.alloc("ldb", 16, F32)
    DTt = ar.alloc("DTt", 2048, F32)
    qdecb = ar.alloc("qdecb", 2048, F32)
    kdecs = ar.alloc("kdecs", 16, F32)
    cdpow = ar.alloc("cdpow", 128, F32)
    corrpow = ar.alloc("corrpow", 128, F32)
    diffX = ar.alloc("diffX", 128, F32)
    iota1 = ar.alloc("iota1", 128, F32)
    revj = ar.alloc("revj", 1, F32)
    csteps = ar.alloc("csteps", 8, F32)
    flag = ar.alloc("flag", 1, F32)
    ldrow = ar.alloc("ldrow", 16, F32)
    with nc.psum_tensor("c4_ps", [128, 16], F32) as ps4:
        ph = Phase("c4")
        cT = Clk(pb, "c4T")
        cS = Clk(pb, "c4S")
        cV = Clk(pb, "c4V")
        d4 = pb.bases("c4d")

        @ph.on("sync")
        def _(e):
            for dst, src in ((diffX, w["diffX"]), (iota1, w["iota1"]), (revj, w["revj"]),
                             (csteps, w["csteps"]), (flag, w["flag"])):
                pb.inc(e.dma_start(out=dst[:, :], in_=src), "c4d", 16)
            pb.inc(e.dma_start(out=ldrow[0:1, 0:16], in_=w["ldr"]), "c4d", 16)

        @ph.on("tensor")
        def _(e):
            e.wait_ge(pb.sem("c4d"), d4["c4d"] + 96)
            ins = e.matmul(ps4[:, 0:16], ones_f[:], ldrow[0:1, 0:16], start=True, stop=True)
            cT.tick(ins)

        @ph.on("vector")
        def _(e):
            cT.wait(e, cT.base + 1)
            cV.guard(e)
            cV.tick(e.tensor_copy(ldb[:, :], ps4[:, 0:16]))
            cV.guard(e)
            cV.tick(e.tensor_scalar_mul(kdecs[:, :], ldb[:, :], revj[:, 0:1]))

        @ph.on("scalar")
        def _(e):
            cV.wait(e, cV.base + 2)
            for h in range(16):
                cS.guard(e)
                cS.tick(e.activation(DTt[:, h * 128:(h + 1) * 128], diffX[:, :], AF.Exp,
                                     scale=ldb[:, h:h + 1]))
                cS.guard(e)
                cS.tick(e.activation(qdecb[:, h * 128:(h + 1) * 128], iota1[:, :], AF.Exp,
                                     scale=ldb[:, h:h + 1]))
                cS.guard(e)
                cS.tick(e.activation(cdpow[:, h * 8:(h + 1) * 8], csteps[:, :], AF.Exp,
                                     scale=ldb[:, h:h + 1]))
            cS.guard(e)
            cS.tick(e.activation(kdecs[:, :], kdecs[:, :], AF.Exp))

        @ph.on("vector")
        def _(e):
            cS.wait(e, cS.base + 49)
            cV.guard(e)
            cV.tick(e.tensor_scalar_mul(corrpow[:, :], cdpow[:, :], flag[:, 0:1]))

        pb.emit(ph)
        cT.check(1)
        cS.check(49)
        cV.check(3)

    o_b = ar.alloc("o_b", 16384)
    S_f = ar.alloc("S_f", 2048, F32)
    S_bf = ar.alloc("S_bf", 2048)
    ptm = ar.alloc("ptm", 256)
    vkb = ar.alloc("vkb", 256)
    otmp = ar.alloc("otmp", 256, F32)
    with (
        nc.psum_tensor("c5_pp", [128, 1024], F32) as pp,
        nc.psum_tensor("c5_po", [128, 1024], F32) as po,
        nc.psum_tensor("c5_p2", [128, 1024], F32) as p2,
        nc.psum_tensor("c5_psd", [128, 1024], F32) as psd,
    ):
        ph = Phase("c5")
        cT = Clk(pb, "c5T")
        cV = Clk(pb, "c5V")
        cZ = Clk(pb, "c5Z")

        Tseq, Vseq = [], []
        for n in range(128):
            Tseq.append(("pt", n))
            if n >= 1:
                Tseq += [("in", n - 1), ("ie", n - 1), ("sa", n - 1)]
            Vseq += [("vk", n), ("pm", n)]
            if n >= 1:
                Vseq += [("om", n - 1), ("oa", n - 1), ("su", n - 1), ("sb", n - 1)]
        Tseq += [("in", 127), ("ie", 127), ("sa", 127)]
        Vseq += [("om", 127), ("oa", 127), ("su", 127), ("sb", 127)]
        tmk = {k: i + 1 for i, k in enumerate(Tseq)}
        vmk = {k: i + 1 for i, k in enumerate(Vseq)}

        def tmark(*k): return cT.base + tmk[k]
        def vmark(*k): return cV.base + vmk[k]

        @ph.on("gpsimd")
        def _(e):
            cZ.tick(e.memset(S_f[:, :], 0.0))
            cZ.tick(e.memset(S_bf[:, :], 0.0))

        @ph.on("tensor")
        def _(e):
            for it in Tseq:
                kind, m = it
                c, h = m // 16, m % 16
                sl = (m % 2) * 512
                qs = q_fm[:, h * 1024 + c * 128:h * 1024 + (c + 1) * 128]
                ks_f = k_fm[:, h * 1024 + c * 128:h * 1024 + (c + 1) * 128]
                ks_t = k_tm[:, c * 2048 + h * 128:c * 2048 + (h + 1) * 128]
                vs_t = v_tm[:, c * 2048 + h * 128:c * 2048 + (h + 1) * 128]
                if kind == "pt":
                    if m >= 2:
                        cV.wait(e, vmark("pm", m - 2))
                    cT.tick(e.matmul(pp[:, sl:sl + 128], ks_f, qs, start=True, stop=True))
                elif kind == "in":
                    cV.wait(e, vmark("pm", m))
                    if m >= 2:
                        cV.wait(e, vmark("oa", m - 2))
                    cT.tick(e.matmul(po[:, sl:sl + 128], vs_t, ptm[:, sl // 4:sl // 4 + 128],
                                     start=True, stop=True))
                elif kind == "ie":
                    if m >= 2:
                        cV.wait(e, vmark("om", m - 2))
                    if c == 0:
                        cZ.wait(e, cZ.base + 2)
                    else:
                        cV.wait(e, vmark("sb", (c - 1) * 16 + h))
                    cT.tick(e.matmul(p2[:, sl:sl + 128], S_bf[:, h * 128:(h + 1) * 128], qs,
                                     start=True, stop=True))
                else:
                    cV.wait(e, vmark("vk", m))
                    if m >= 2:
                        cV.wait(e, vmark("su", m - 2))
                    cT.tick(e.matmul(psd[:, sl:sl + 128], ks_t, vkb[:, sl // 4:sl // 4 + 128],
                                     start=True, stop=True))

        @ph.on("vector")
        def _(e):
            for it in Vseq:
                kind, m = it
                c, h = m // 16, m % 16
                sl = (m % 2) * 512
                sv = (m % 2) * 128
                vs_t = v_tm[:, c * 2048 + h * 128:c * 2048 + (h + 1) * 128]
                if kind == "vk":
                    if m >= 2:
                        cT.wait(e, tmark("sa", m - 2))
                    cV.guard(e)
                    cV.tick(e.tensor_scalar_mul(vkb[:, sv:sv + 128], vs_t, kdecs[:, h:h + 1]))
                elif kind == "pm":
                    cT.wait(e, tmark("pt", m))
                    if m >= 2:
                        cT.wait(e, tmark("in", m - 2))
                    cV.guard(e)
                    cV.tick(e.tensor_mul(ptm[:, sv:sv + 128], pp[:, sl:sl + 128],
                                         DTt[:, h * 128:(h + 1) * 128]))
                elif kind == "om":
                    cT.wait(e, tmark("ie", m))
                    cV.guard(e)
                    cV.tick(e.tensor_mul(otmp[:, sv:sv + 128], p2[:, sl:sl + 128],
                                         qdecb[:, h * 128:(h + 1) * 128]))
                elif kind == "oa":
                    cT.wait(e, tmark("in", m))
                    cV.guard(e)
                    cV.tick(e.tensor_add(o_b[:, h * 1024 + c * 128:h * 1024 + (c + 1) * 128],
                                         otmp[:, sv:sv + 128], po[:, sl:sl + 128]))
                elif kind == "su":
                    cT.wait(e, tmark("sa", m))
                    if c == 0:
                        cZ.wait(e, cZ.base + 2)
                    cV.guard(e)
                    cV.tick(e.scalar_tensor_tensor(S_f[:, h * 128:(h + 1) * 128],
                                                   S_f[:, h * 128:(h + 1) * 128],
                                                   cdpow[:, h * 8 + 1:h * 8 + 2],
                                                   psd[:, sl:sl + 128], ALU.mult, ALU.add))
                else:
                    cV.guard(e)
                    cV.tick(e.tensor_copy(S_bf[:, h * 128:(h + 1) * 128],
                                          S_f[:, h * 128:(h + 1) * 128]))

        pb.emit(ph)
        cT.check(len(Tseq))
        cV.check(len(Vseq))
        cZ.check(2)

    ar.freev("k_tm", "k_fm", "ptm", "vkb", "otmp")
    sg = ar.alloc("sg", 16384)
    S0b = ar.alloc("S0b", 2048)
    S0f = ar.alloc("S0f", 2048, F32)
    ctmp = ar.alloc("ctmp", 256, F32)
    wg2s = ar.alloc("wg2s", 2048)
    with (
        nc.psum_tensor("c6_ps", [128, 1024], F32) as ps6,
        nc.psum_tensor("c6_pc", [128, 1024], F32) as pc6,
    ):
        ph = Phase("c6")
        cT = Clk(pb, "c6T")
        cS = Clk(pb, "c6S")
        cV = Clk(pb, "c6V")
        d6 = pb.bases("c6d", "c6cc", "c6g", "c6s0")

        def gmm(g): return cT.base + g + 1
        def cmm(n): return cT.base + 32 + n + 1
        def sgd(g): return cS.base + g + 1
        def s0d(): return cV.base + 1
        def com(n): return cV.base + 1 + 2 * n + 1
        def coa(n): return cV.base + 1 + 2 * n + 2

        @ph.on("sync")
        def _(e):
            pb.inc(e.dma_start(out=w["cc_in"], in_=S_f[:, :]), "c6d", 16)

        @ph.on("gpsimd")
        def _(e):
            pb.inc(e.dma_start(out=wg2s[:, :], in_=w["wg2"]), "c6g", 16)
            e.wait_ge(pb.sem("c6d"), d6["c6d"] + 16)
            ins = e.collective_compute(
                "AllGather", ALU.bypass, replica_groups=w["pairs"],
                ins=[w["cc_in"]], outs=[w["cc_out"]])
            pb.inc(ins, "c6cc", 1)
            e.wait_ge(pb.sem("c6cc"), d6["c6cc"] + 1)
            ins = e.dma_start(out=S0f[:, :], in_=w["cc_out"][0:128, :])
            pb.inc(ins, "c6s0", 16)

        @ph.on("tensor")
        def _(e):
            e.wait_ge(pb.sem("c6g"), d6["c6g"] + 16)
            for ft in range(16):
                for tb in range(2):
                    g = ft * 2 + tb
                    if g >= 2:
                        cS.wait(e, sgd(g - 2))
                    ins = e.matmul(ps6[:, (g % 2) * 512:(g % 2) * 512 + 512],
                                   wg2s[:, ft * 128:(ft + 1) * 128],
                                   g1[:, tb * 512:tb * 512 + 512],
                                   start=True, stop=True)
                    cT.tick(ins)
            cV.wait(e, s0d())
            for c in range(NCH):
                for h in range(H):
                    n = c * 16 + h
                    if n >= 2:
                        cV.wait(e, com(n - 2))
                    ins = e.matmul(pc6[:, (n % 2) * 512:(n % 2) * 512 + 128],
                                   S0b[:, h * 128:(h + 1) * 128],
                                   q_fm[:, h * 1024 + c * 128:h * 1024 + (c + 1) * 128],
                                   start=True, stop=True)
                    cT.tick(ins)

        @ph.on("scalar")
        def _(e):
            for g in range(32):
                ft, tb = g // 2, g % 2
                cT.wait(e, gmm(g))
                cS.guard(e)
                ins = e.activation(sg[:, ft * 1024 + tb * 512:ft * 1024 + tb * 512 + 512],
                                   ps6[:, (g % 2) * 512:(g % 2) * 512 + 512], AF.Sigmoid)
                cS.tick(ins)

        @ph.on("vector")
        def _(e):
            e.wait_ge(pb.sem("c6s0"), d6["c6s0"] + 16)
            cV.guard(e)
            cV.tick(e.tensor_copy(S0b[:, :], S0f[:, :]))
            for c in range(NCH):
                for h in range(H):
                    n = c * 16 + h
                    sl = (n % 2) * 512
                    sv = (n % 2) * 128
                    cT.wait(e, cmm(n))
                    cV.guard(e)
                    ins = e.scalar_tensor_tensor(ctmp[:, sv:sv + 128],
                                                 pc6[:, sl:sl + 128],
                                                 corrpow[:, h * 8 + c:h * 8 + c + 1],
                                                 qdecb[:, h * 128:(h + 1) * 128],
                                                 ALU.mult, ALU.mult)
                    cV.tick(ins)
                    ob = o_b[:, h * 1024 + c * 128:h * 1024 + (c + 1) * 128]
                    cV.guard(e)
                    cV.tick(e.tensor_add(ob, ob, ctmp[:, sv:sv + 128]))

        pb.emit(ph)
        cT.check(32 + 128)
        cS.check(32)
        cV.check(1 + 256)

    ar.freev("q_fm", "v_tm", "S_f", "S_bf", "S0b", "S0f", "ctmp", "wg2s",
             "DTt", "qdecb", "kdecs", "cdpow", "corrpow", "diffX", "iota1", "revj",
             "csteps", "flag", "ldrow", "ldb", "g1")
    sqb = ar.alloc("sqb", 1024)
    rstdb = ar.alloc("rstdb", 1024, F32)
    with (
        nc.sbuf_tensor("c7_nw", [128, 16], F32) as nw,
        nc.psum_tensor("c7_ps", [128, 1024], F32) as ps7,
        nc.psum_tensor("c7_pb", [128, 1024], F32) as pb7,
    ):
        ph = Phase("c7")
        cT = Clk(pb, "c7T")
        cS = Clk(pb, "c7S")
        cV = Clk(pb, "c7V")
        d7 = pb.bases("c7d")

        def t_ss(h, tb): return cT.base + h * 4 + tb + 1
        def t_bc(h, tb): return cT.base + h * 4 + 2 + tb + 1
        def s_sq(h): return cS.base + h * 3 + 1
        def s_rs(h): return cS.base + h * 3 + 3
        def v_og(h, tb): return cV.base + h * 5 + tb + 1
        def v_eps(h): return cV.base + h * 5 + 3
        def v_on(h, tb): return cV.base + h * 5 + 3 + tb + 1

        @ph.on("sync")
        def _(e):
            pb.inc(e.dma_start(out=nw[:], in_=w["nw"]), "c7d", 16)

        @ph.on("tensor")
        def _(e):
            for h in range(H):
                for tb in range(2):
                    cS.wait(e, s_sq(h))
                    if h >= 1:
                        cV.wait(e, v_on(h - 1, 1))
                    ins = e.matmul(ps7[0:1, tb * 512:tb * 512 + 512], ones_bf[:],
                                   sqb[:, tb * 512:tb * 512 + 512], start=True, stop=True)
                    cT.tick(ins)
                cS.wait(e, s_rs(h))
                for tb in range(2):
                    ins = e.matmul(pb7[:, tb * 512:tb * 512 + 512], ones_f[:],
                                   rstdb[0:1, tb * 512:tb * 512 + 512], start=True, stop=True)
                    cT.tick(ins)

        @ph.on("scalar")
        def _(e):
            for h in range(H):
                cV.wait(e, v_og(h, 1))
                if h >= 1:
                    cT.wait(e, t_ss(h - 1, 1))
                cS.guard(e)
                cS.tick(e.square(sqb[:, :], o_b[:, h * 1024:(h + 1) * 1024]))
                cV.wait(e, v_eps(h))
                cS.guard(e)
                cS.tick(e.activation(rstdb[0:1, 0:1024], rstdb[0:1, 0:1024], AF.Ln))
                cS.guard(e)
                cS.tick(e.activation(rstdb[0:1, 0:1024], rstdb[0:1, 0:1024], AF.Exp, scale=-0.5))

        @ph.on("vector")
        def _(e):
            e.wait_ge(pb.sem("c7d"), d7["c7d"] + 16)
            for h in range(H):
                for tb in range(2):
                    cV.guard(e)
                    ob = o_b[:, h * 1024 + tb * 512:h * 1024 + tb * 512 + 512]
                    cV.tick(e.tensor_mul(ob, ob, sg[:, h * 1024 + tb * 512:h * 1024 + tb * 512 + 512]))
                cT.wait(e, t_ss(h, 1))
                cV.guard(e)
                cV.tick(e.tensor_scalar(rstdb[0:1, 0:1024], ps7[0:1, 0:1024],
                                        1.0 / 128.0, B_EPS, ALU.mult, ALU.add))
                for tb in range(2):
                    cT.wait(e, t_bc(h, tb))
                    cV.guard(e)
                    ob = o_b[:, h * 1024 + tb * 512:h * 1024 + tb * 512 + 512]
                    ins = e.scalar_tensor_tensor(ob, ob, nw[:, h:h + 1],
                                                 pb7[:, tb * 512:tb * 512 + 512],
                                                 ALU.mult, ALU.mult)
                    cV.tick(ins)

        pb.emit(ph)
        cT.check(H * 4)
        cS.check(H * 3)
        cV.check(H * 5)

    ar.freev("sg", "sqb", "rstdb")
    wo_sb = ar.alloc("wo_sb", 16 * 2048)
    rtx = ar.alloc("rtx", 2 * D, F32)
    rox = ar.alloc("rox", 2 * D, F32)
    with nc.psum_tensor("c8_ps", [128, 1024], F32) as ps8:
        ph = Phase("c8")
        cT = Clk(pb, "c8T")
        cV = Clk(pb, "c8V")
        d8 = pb.bases("c8w", "c8x0", "c8x1", "c8o0", "c8o1")

        def mmd(g): return cT.base + g + 1
        def add_(tt, db): return cV.base + tt * 4 + db + 1

        @ph.on("gpsimd")
        def _(e):
            pb.inc(e.dma_start(out=wo_sb[:, :], in_=w["wo"]), "c8w", 16)

        @ph.on("sync")
        def _(e):
            for tt in range(NT):
                if tt >= 2:
                    cV.wait(e, add_(tt - 2, 3))
                    ins = e.dma_start(out=dst_r[(tt - 2) * 128:(tt - 1) * 128, :],
                                      in_=rox[:, (tt % 2) * D:(tt % 2 + 1) * D])
                    pb.inc(ins, f"c8o{tt % 2}", 16)
                ins = e.dma_start(out=rtx[:, (tt % 2) * D:(tt % 2 + 1) * D],
                                  in_=src_r[tt * 128:(tt + 1) * 128, :])
                pb.inc(ins, f"c8x{tt % 2}", 16)
            for tt in range(NT - 2, NT):
                cV.wait(e, add_(tt, 3))
                ins = e.dma_start(out=dst_r[tt * 128:(tt + 1) * 128, :],
                                  in_=rox[:, (tt % 2) * D:(tt % 2 + 1) * D])
                pb.inc(ins, f"c8o{tt % 2}", 16)
            e.wait_ge(pb.sem("c8o0"), d8["c8o0"] + 16 * 4)
            e.wait_ge(pb.sem("c8o1"), d8["c8o1"] + 16 * 4)

        @ph.on("tensor")
        def _(e):
            e.wait_ge(pb.sem("c8w"), d8["c8w"] + 16)
            for tt in range(NT):
                for db in range(4):
                    g = tt * 4 + db
                    if g >= 2:
                        cV.wait(e, add_((g - 2) // 4, (g - 2) % 4))
                    for kt in range(16):
                        ins = e.matmul(ps8[:, (g % 2) * 512:(g % 2) * 512 + 512],
                                       o_b[:, kt * 1024 + tt * 128:kt * 1024 + tt * 128 + 128],
                                       wo_sb[:, kt * 2048 + db * 512:kt * 2048 + db * 512 + 512],
                                       start=(kt == 0), stop=(kt == 15))
                    cT.tick(ins)

        @ph.on("vector")
        def _(e):
            for tt in range(NT):
                for db in range(4):
                    g = tt * 4 + db
                    cT.wait(e, mmd(g))
                    e.wait_ge(pb.sem(f"c8x{tt % 2}"), d8[f"c8x{tt % 2}"] + 16 * (tt // 2 + 1))
                    if tt >= 2 and db == 0:
                        e.wait_ge(pb.sem(f"c8o{tt % 2}"), d8[f"c8o{tt % 2}"] + 16 * (tt // 2))
                    cV.guard(e)
                    ins = e.tensor_add(rox[:, (tt % 2) * D + db * 512:(tt % 2) * D + db * 512 + 512],
                                       rtx[:, (tt % 2) * D + db * 512:(tt % 2) * D + db * 512 + 512],
                                       ps8[:, (g % 2) * 512:(g % 2) * 512 + 512])
                    cV.tick(ins)

        pb.emit(ph)
        cT.check(32)
        cV.check(32)

    ar.freev("o_b", "wo_sb", "rtx", "rox")
    ar.close()


# version 17
# speedup vs baseline: 1.0208x; 1.0208x over previous
import numpy as np
import ml_dtypes
import concourse.bass as bass
import concourse.mybir as mybir
from concourse.bass_utils import run_bass_kernel_spmd

F32 = mybir.dt.float32
BF16 = mybir.dt.bfloat16
AF = mybir.ActivationFunctionType
ALU = mybir.AluOpType
AX = mybir.AxisListType

D = 2048
T = 1024
CTX = 2048
NT = T // 128
NCTX = CTX // 128
ND = D // 128
H = 16
LN_EPS = 1e-5
BIG_NEG = -1e30
INTER = 8192
NJC = INTER // 128


class Phase:

    def __init__(self, name):
        self.name = name
        self.fns = {}

    def on(self, engine):
        def deco(fn):
            self.fns.setdefault(engine, []).append(fn)
            return fn
        return deco


_SEM_TABLE = {
    "c_dma": "hx", "a1w": "hx", "a1wg": "gw", "a2w": "gw", "a3k": "hx",
    "a4w": "gx", "a5w": "gy", "a6w": "gw", "a6m0": "hd0", "a6m1": "hd1",
    "a7w": "gw", "a7x0": "hd0", "a7x1": "hd1", "a7o0": "ho0", "a7o1": "ho1",
    "c2d0": "gs0", "c2d1": "gs1", "c2g": "gw", "c4d": "hx", "c6d": "hx",
    "c6cc": "ccs", "c6g": "gw", "c6s0": "gs0", "c7d": "hx", "c8w": "gw",
    "c8x0": "hd0", "c8x1": "hd1", "c8o0": "ho0", "c8o1": "ho1", "adbg": "hx",
}
_SEM_SUFFIX = [
    ("lnd0", "hd0"), ("lnd1", "hd1"), ("lnd2", "hd2"), ("2d", "gw"), ("3d0", "gs0"), ("3d1", "gs1"),
    ("4u0", "gs0"), ("4u1", "gs1"), ("4d0", "gs2"), ("4d1", "gs3"),
    ("5d0", "hd0"), ("5d1", "hd1"), ("5o0", "ho0"), ("5o1", "ho1"), ("5g", "hx"),
]


def _canon(name):
    if name in _SEM_TABLE:
        return _SEM_TABLE[name]
    for suf, pool in _SEM_SUFFIX:
        if name.endswith(suf):
            return pool
    if name[-1] in "TSVZ":
        return "ck" + name[-1]
    raise KeyError(f"unpooled sem name {name}")


class PB:

    def __init__(self, nc):
        self.nc = nc
        self.sems = {}
        self.cnt = {}

    def sem(self, name):
        k = _canon(name)
        if k not in self.sems:
            self.sems[k] = self.nc.alloc_semaphore(name=k)
            self.cnt[k] = 0
        return self.sems[k]

    def getcnt(self, name):
        self.sem(name)
        return self.cnt[_canon(name)]

    def inc(self, ins, name, n=1):
        ins.then_inc(self.sem(name), n)
        self.cnt[_canon(name)] += n
        return self.cnt[_canon(name)]

    def cw(self, e, key):
        e.wait_ge(self.sem(key), self.getcnt(key))

    def ci(self, ins, key):
        self.inc(ins, key)

    def nopinc(self, e, name, n=1):
        self.inc(e.nop(), name, n)

    def bases(self, *names):
        return {n: self.getcnt(n) for n in names}

    def emit(self, phase):
        with self.nc.Block() as block:
            for eng in ("sync", "gpsimd", "tensor", "scalar", "vector"):
                fns = phase.fns.get(eng)
                if not fns:
                    continue

                def run(e, fns=fns):
                    for f in fns:
                        f(e)
                getattr(block, eng)(run)



class Clk:
    def __init__(self, pb, name):
        self.pb = pb
        self.name = name
        pb.sem(name)
        self.base = pb.getcnt(name)

    def tick(self, ins):
        return self.pb.inc(ins, self.name)

    def guard(self, e):
        e.wait_ge(self.pb.sem(self.name), self.pb.getcnt(self.name))

    def gtick(self, e, ins):
        return self.pb.inc(ins, self.name)

    def wait(self, e, mark):
        e.wait_ge(self.pb.sem(self.name), mark)

    def check(self, expect_total):
        got = self.pb.getcnt(self.name) - self.base
        assert got == expect_total, (self.name, got, expect_total)



def emit_ln_fm(pb, ph, tag, src_tiles, fm_dst, fm_col0, ntok_dst, sb, dsts=None):
    if dsts is None:
        dsts = [(fm_dst, fm_col0 + i * 128, ntok_dst) for i in range(len(src_tiles))]
    n = len(src_tiles)
    rt, st, xln, tps, ident = sb["rt"], sb["st"], sb["xln"], sb["tps"], sb["ident"]
    RSL = rt.n // D if hasattr(rt, "n") else 2
    cS = Clk(pb, f"{tag}S")
    cV = Clk(pb, f"{tag}V")
    cT = Clk(pb, f"{tag}T")
    dsl = [f"{tag}d{j}" for j in range(RSL)]
    dbase = pb.bases(*dsl)

    vseq = []
    for i in range(n):
        if i >= 1:
            vseq += [("ev", i - 1, 0), ("ev", i - 1, 1)]
        for k in range(8):
            vseq.append(("st", i, k))
    vseq += [("ev", n - 1, 0), ("ev", n - 1, 1)]
    vmk = {k: i + 1 for i, k in enumerate(vseq)}

    def sqd(i): return cS.base + 4 * i + 1
    def lnd(i): return cS.base + 4 * i + 2
    def expd(i): return cS.base + 4 * i + 3
    def xld(i): return cS.base + 4 * (i + 1)
    def varpd(i): return cV.base + vmk[("st", i, 5)]
    def negbd(i): return cV.base + vmk[("st", i, 7)]
    def evd(i, half): return cV.base + vmk[("ev", i, half)]
    def tpd(j): return cT.base + j + 1

    @ph.on("sync")
    def _(e):
        for i in range(n):
            if i >= RSL:
                cS.wait(e, xld(i - RSL + 1))
            ins = e.dma_start(out=rt[:, (i % RSL) * D:(i % RSL + 1) * D], in_=src_tiles[i])
            pb.inc(ins, dsl[i % RSL], 16)

    @ph.on("scalar")
    def _(e):
        for i in range(n):
            b = (i % 2) * D
            br = (i % RSL) * D
            c = i % 2
            e.wait_ge(pb.sem(dsl[i % RSL]), dbase[dsl[i % RSL]] + 16 * (i // RSL + 1))
            if i >= 2:
                cT.wait(e, cT.base + (i - 1) * ND)
            cS.guard(e)
            cS.tick(e.activation(xln[:, b:b + D], rt[:, br:br + D], AF.Square,
                                 accum_out=st[:, 1:2]))
            cV.wait(e, varpd(i))
            cS.guard(e)
            cS.tick(e.activation(st[:, 8:9], st[:, 6:7], AF.Ln))
            cS.guard(e)
            cS.tick(e.activation(st[:, 10 + c:11 + c], st[:, 8:9], AF.Exp, scale=-0.5))
            cV.wait(e, negbd(i))
            cS.guard(e)
            cS.tick(e.activation(xln[:, b:b + D], rt[:, br:br + D], AF.Identity,
                                 bias=st[:, 12 + c:13 + c], scale=st[:, 10 + c:11 + c]))

    @ph.on("vector")
    def _(e):
        def emit_ev(i, half):
            cT.wait(e, tpd(i * ND + 8 * half + 7))
            dbuf, c0, ntok = dsts[i]
            if hasattr(dbuf, "tcols"):
                dt_, doff, dcols = dbuf.t, dbuf.off, dbuf.tcols
            else:
                dt_, doff, dcols = dbuf, 0, dbuf.shape[1]
            src = bass.AP(tps, 0, [[8 * 1024, 128], [1024, 8], [1, 128]])
            dst = bass.AP(dt_, doff + (8 * half) * ntok + c0,
                          [[dcols, 128], [ntok, 8], [1, 128]])
            cV.guard(e)
            cV.tick(e.tensor_copy(dst, src))

        for i in range(n):
            if i >= 1:
                emit_ev(i - 1, 0)
                emit_ev(i - 1, 1)
            b = (i % 2) * D
            br = (i % RSL) * D
            c = i % 2
            e.wait_ge(pb.sem(dsl[i % RSL]), dbase[dsl[i % RSL]] + 16 * (i // RSL + 1))
            cV.guard(e)
            cV.tick(e.reduce_sum(st[:, 0:1], rt[:, br:br + D], AX.X))
            if i >= 1:
                cS.wait(e, lnd(i - 1))
            cV.guard(e)
            cV.tick(e.tensor_scalar_mul(st[:, 2:3], st[:, 0:1], 1.0 / D))
            cS.wait(e, sqd(i))
            cV.guard(e)
            cV.tick(e.tensor_scalar_mul(st[:, 3:4], st[:, 1:2], 1.0 / D))
            cV.guard(e)
            cV.tick(e.tensor_mul(st[:, 4:5], st[:, 2:3], st[:, 2:3]))
            cV.guard(e)
            cV.tick(e.tensor_sub(st[:, 5:6], st[:, 3:4], st[:, 4:5]))
            cV.guard(e)
            cV.tick(e.tensor_scalar_add(st[:, 6:7], st[:, 5:6], LN_EPS))
            cS.wait(e, expd(i))
            cV.guard(e)
            cV.tick(e.tensor_mul(st[:, 14:15], st[:, 2:3], st[:, 10 + c:11 + c]))
            cV.guard(e)
            cV.tick(e.tensor_scalar_mul(st[:, 12 + c:13 + c], st[:, 14:15], -1.0))
        emit_ev(n - 1, 0)
        emit_ev(n - 1, 1)

    @ph.on("tensor")
    def _(e):
        for i in range(n):
            b = (i % 2) * D
            cS.wait(e, xld(i))
            for dt in range(ND):
                j = i * ND + dt
                if j >= 8:
                    jj = j - 8
                    cV.wait(e, evd(jj // ND, (jj % ND) // 8))
                ins = e.transpose(tps[:, (j % 8) * 1024:(j % 8) * 1024 + 128],
                                  xln[:, b + dt * 128:b + (dt + 1) * 128], ident[:, :])
                cT.tick(ins)

    def final_check():
        cS.check(4 * n)
        cV.check(len(vseq))
        cT.check(ND * n)
    ph.fns.setdefault("_check", []).append(final_check)
    return dict(cS=cS, cV=cV, cT=cT, xld=xld)



def emit_ffn(pb, cn, w, src_r, dst_r, uid):
    nc = pb.nc
    u = f"f{uid}"
    with (
        nc.sbuf_tensor(f"{u}_xfm", [128, ND * T], BF16) as x_fm,
        nc.sbuf_tensor(f"{u}_yacc", [128, NT * D], F32) as y_acc,
        nc.sbuf_tensor(f"{u}_scale", [128, 8], F32) as scale_sb,
    ):
        with (
            nc.sbuf_tensor(f"{u}_rt", [128, 3 * D], F32) as rt,
            nc.sbuf_tensor(f"{u}_st", [128, 16], F32) as st,
            nc.sbuf_tensor(f"{u}_xln", [128, 2 * D], BF16) as xln,
            nc.psum_tensor(f"{u}_tps", [128, 8 * 1024], BF16) as tps,
        ):
            sb = dict(rt=rt, st=st, xln=xln, tps=tps, ident=cn["ident"])
            ph = Phase(f"{u}_ln")
            tiles = [src_r[i * 128:(i + 1) * 128, :] for i in range(NT)]
            emit_ln_fm(pb, ph, f"{u}ln", tiles, x_fm, 0, T, sb)
            pb.emit(ph)
            for f in ph.fns.get("_check", []):
                f()

        with (
            nc.sbuf_tensor(f"{u}_gsb", [128, ND * 8], BF16) as gsb,
            nc.sbuf_tensor(f"{u}_lg", [128, 8], F32) as lg,
            nc.sbuf_tensor(f"{u}_m", [128, 4], F32) as m,
            nc.sbuf_tensor(f"{u}_msk", [128, 8], F32) as msk,
            nc.psum_tensor(f"{u}_psl", [128, 1024], F32) as ps_l,
        ):
            ph = Phase(f"{u}_top2")
            cT = Clk(pb, f"{u}2T")
            cV = Clk(pb, f"{u}2V")
            dsem = pb.bases(f"{u}2d")

            def mmd(tt): return cT.base + tt + 1
            def scd(tt): return cV.base + 7 * (tt + 1)

            @ph.on("gpsimd")
            def _(e):
                pb.inc(e.dma_start(out=gsb[:], in_=w["gate"]), f"{u}2d", 16)

            @ph.on("tensor")
            def _(e):
                e.wait_ge(pb.sem(f"{u}2d"), dsem[f"{u}2d"] + 16)
                for tt in range(NT):
                    if tt >= 2:
                        cV.wait(e, scd(tt - 2) - 6)
                    for dt in range(ND):
                        ins = e.matmul(ps_l[:, (tt % 2) * 512:(tt % 2) * 512 + 8],
                                       x_fm[:, dt * T + tt * 128:dt * T + tt * 128 + 128],
                                       gsb[:, dt * 8:(dt + 1) * 8],
                                       start=(dt == 0), stop=(dt == ND - 1))
                    cT.tick(ins)

            @ph.on("vector")
            def _(e):
                for tt in range(NT):
                    cT.wait(e, mmd(tt))
                    cV.guard(e)
                    cV.tick(e.tensor_copy(lg[:], ps_l[:, (tt % 2) * 512:(tt % 2) * 512 + 8]))
                    cV.guard(e)
                    cV.tick(e.reduce_max(m[:, 0:1], lg[:], AX.X))
                    cV.guard(e)
                    cV.tick(e.tensor_scalar(msk[:], lg[:], m[:, 0:1], None, ALU.is_ge))
                    cV.guard(e)
                    cV.tick(e.scalar_tensor_tensor(msk[:], msk[:], BIG_NEG, lg[:], ALU.mult, ALU.add))
                    cV.guard(e)
                    cV.tick(e.reduce_max(m[:, 1:2], msk[:], AX.X))
                    cV.guard(e)
                    cV.tick(e.tensor_add(scale_sb[:, tt:tt + 1], m[:, 0:1], m[:, 1:2]))
                    cV.guard(e)
                    cV.tick(e.engine_nop())

            pb.emit(ph)
            cT.check(NT)
            cV.check(7 * NT)

        with (
            nc.sbuf_tensor(f"{u}_ep", [128, 2 * 8192], BF16) as ep,
            nc.psum_tensor(f"{u}_psm", [128, 1024], F32) as ps_m,
        ):
            ph = Phase(f"{u}_moe")
            cT = Clk(pb, f"{u}3T")
            cS = Clk(pb, f"{u}3S")
            dsl = [f"{u}3d0", f"{u}3d1"]
            dbase = pb.bases(*dsl)

            def mmd(g): return cT.base + g + 1
            def evd(g): return cS.base + g + 1

            @ph.on("gpsimd")
            def _(e):
                for db in range(4):
                    if db >= 2:
                        cT.wait(e, mmd((db - 1) * NT - 1))
                    ins = e.dma_start(out=ep[:, (db % 2) * 8192:(db % 2 + 1) * 8192],
                                      in_=w["ex"][db])
                    pb.inc(ins, dsl[db % 2], 16)

            @ph.on("tensor")
            def _(e):
                for db in range(4):
                    e.wait_ge(pb.sem(dsl[db % 2]), dbase[dsl[db % 2]] + 16 * (db // 2 + 1))
                    for tt in range(NT):
                        g = db * NT + tt
                        if g >= 2:
                            cS.wait(e, evd(g - 2))
                        for dt in range(ND):
                            ins = e.matmul(
                                ps_m[:, (tt % 2) * 512:(tt % 2) * 512 + 512],
                                x_fm[:, dt * T + tt * 128:dt * T + tt * 128 + 128],
                                ep[:, (db % 2) * 8192 + dt * 512:(db % 2) * 8192 + (dt + 1) * 512],
                                start=(dt == 0), stop=(dt == ND - 1))
                        cT.tick(ins)

            @ph.on("scalar")
            def _(e):
                for db in range(4):
                    for tt in range(NT):
                        g = db * NT + tt
                        cT.wait(e, mmd(g))
                        cS.guard(e)
                        ins = e.activation(y_acc[:, tt * D + db * 512:tt * D + db * 512 + 512],
                                           ps_m[:, (tt % 2) * 512:(tt % 2) * 512 + 512],
                                           AF.Copy, scale=scale_sb[:, tt:tt + 1])
                        cS.tick(ins)

            pb.emit(ph)
            cT.check(32)
            cS.check(32)

        with (
            nc.sbuf_tensor(f"{u}_hp", [128, 16 * T], BF16) as h_pan,
            nc.sbuf_tensor(f"{u}_upt", [128, 2 * 2048], BF16) as upt,
            nc.sbuf_tensor(f"{u}_dnb", [128, 2 * 8192], BF16) as dnb,
            nc.sbuf_tensor(f"{u}_rlu", [128, 2 * 512], BF16) as rlu,
            nc.psum_tensor(f"{u}_psu", [128, 2048], F32) as ps_u,
            nc.psum_tensor(f"{u}_psd", [128, 1024], F32) as ps_d,
        ):
            ph = Phase(f"{u}_mlp")
            cT = Clk(pb, f"{u}4T")
            cS = Clk(pb, f"{u}4S")
            cV = Clk(pb, f"{u}4V")
            usl = [f"{u}4u0", f"{u}4u1"]
            dsl = [f"{u}4d0", f"{u}4d1"]
            ubase = pb.bases(*usl)
            dbase = pb.bases(*dsl)

            def umm(jp, g): return cT.base + jp * 64 + g + 1
            def dmm(jp, g2): return cT.base + jp * 64 + 32 + g2 + 1
            def reld(jp, g): return cS.base + jp * 32 + g + 1
            def hd(jp, g): return cV.base + jp * 64 + g + 1
            def yad(jp, g2): return cV.base + jp * 64 + 32 + g2 + 1

            @ph.on("gpsimd")
            def _(e):
                for jp in range(4):
                    for jc in range(16):
                        uu = jp * 16 + jc
                        if uu >= 2:
                            jp0, g0 = divmod((uu - 2) * 2 + 1, 32)
                            cT.wait(e, umm(jp0, g0))
                        ins = e.dma_start(out=upt[:, (uu % 2) * 2048:(uu % 2 + 1) * 2048],
                                          in_=w["up"][uu])
                        pb.inc(ins, usl[uu % 2], 16)
                    for db in range(4):
                        d = jp * 4 + db
                        if d >= 2:
                            jp0, g0 = divmod((d - 2) * 8 + 7, 32)
                            cT.wait(e, dmm(jp0, g0))
                        ins = e.dma_start(out=dnb[:, (d % 2) * 8192:(d % 2 + 1) * 8192],
                                          in_=w["dn"][d])
                        pb.inc(ins, dsl[d % 2], 16)

            @ph.on("tensor")
            def _(e):
                for jp in range(4):
                    for jc in range(16):
                        uu = jp * 16 + jc
                        e.wait_ge(pb.sem(usl[uu % 2]), ubase[usl[uu % 2]] + 16 * (uu // 2 + 1))
                        for tb in range(2):
                            g = jc * 2 + tb
                            ag = uu * 2 + tb
                            if ag >= 4:
                                cV.wait(e, hd((ag - 4) // 32, (ag - 4) % 32))
                            sl = ((jc % 2) * 2 + tb) * 512
                            for dt in range(ND):
                                ins = e.matmul(
                                    ps_u[:, sl:sl + 512],
                                    upt[:, (uu % 2) * 2048 + dt * 128:(uu % 2) * 2048 + (dt + 1) * 128],
                                    x_fm[:, dt * T + tb * 512:dt * T + tb * 512 + 512],
                                    start=(dt == 0), stop=(dt == ND - 1))
                            cT.tick(ins)
                    for db in range(4):
                        d = jp * 4 + db
                        e.wait_ge(pb.sem(dsl[d % 2]), dbase[dsl[d % 2]] + 16 * (d // 2 + 1))
                        cV.wait(e, hd(jp, 31))
                        for tt in range(NT):
                            g2 = db * NT + tt
                            if g2 >= 2 or jp > 0:
                                pjp, pg = (jp, g2 - 2) if g2 >= 2 else (jp - 1, 30 + g2)
                                cV.wait(e, yad(pjp, pg))
                            for jc in range(16):
                                ins = e.matmul(
                                    ps_d[:, (tt % 2) * 512:(tt % 2) * 512 + 512],
                                    h_pan[:, jc * T + tt * 128:jc * T + tt * 128 + 128],
                                    dnb[:, (d % 2) * 8192 + jc * 512:(d % 2) * 8192 + (jc + 1) * 512],
                                    start=(jc == 0), stop=(jc == 15))
                            cT.tick(ins)

            @ph.on("scalar")
            def _(e):
                for jp in range(4):
                    for g in range(32):
                        cT.wait(e, umm(jp, g))
                        ag = jp * 32 + g
                        if ag >= 2:
                            cV.wait(e, hd((ag - 2) // 32, (ag - 2) % 32))
                        cS.guard(e)
                        sl = (((g // 2) % 2) * 2 + (g % 2)) * 512
                        cS.tick(e.activation(rlu[:, (ag % 2) * 512:(ag % 2) * 512 + 512],
                                             ps_u[:, sl:sl + 512], AF.Relu))

            @ph.on("vector")
            def _(e):
                for jp in range(4):
                    for g in range(32):
                        jc, tb = g // 2, g % 2
                        ag = jp * 32 + g
                        cS.wait(e, reld(jp, g))
                        cV.guard(e)
                        sl = ((jc % 2) * 2 + tb) * 512
                        ins = e.scalar_tensor_tensor(
                            h_pan[:, jc * T + tb * 512:jc * T + tb * 512 + 512],
                            rlu[:, (ag % 2) * 512:(ag % 2) * 512 + 512],
                            1.0, ps_u[:, sl:sl + 512], ALU.mult, ALU.mult)
                        cV.tick(ins)
                    for g2 in range(32):
                        cT.wait(e, dmm(jp, g2))
                        tt = g2 % NT
                        db = g2 // NT
                        cV.guard(e)
                        ins = e.tensor_add(y_acc[:, tt * D + db * 512:tt * D + db * 512 + 512],
                                           y_acc[:, tt * D + db * 512:tt * D + db * 512 + 512],
                                           ps_d[:, (tt % 2) * 512:(tt % 2) * 512 + 512])
                        cV.tick(ins)

            pb.emit(ph)
            cT.check(4 * 64)
            cS.check(4 * 32)
            cV.check(4 * 64)

        with (
            nc.sbuf_tensor(f"{u}_rt2", [128, 2 * D], F32) as rt2,
            nc.sbuf_tensor(f"{u}_ro", [128, 2 * D], F32) as ro,
            nc.sbuf_tensor(f"{u}_yh", [128, 2 * D], F32) as yh,
            nc.sbuf_tensor(f"{u}_st2", [128, 16], F32) as st2,
            nc.sbuf_tensor(f"{u}_gb", [128, D], F32) as gb,
            nc.sbuf_tensor(f"{u}_bb", [128, D], F32) as bb,
        ):
            ph = Phase(f"{u}_out")
            cS = Clk(pb, f"{u}5S")
            cV = Clk(pb, f"{u}5V")
            dsl = [f"{u}5d0", f"{u}5d1"]
            osl = [f"{u}5o0", f"{u}5o1"]
            gds = f"{u}5g"
            dbase = pb.bases(*dsl, *osl, gds)

            def sqd(i): return cS.base + 4 * i + 1
            def lnd(i): return cS.base + 4 * i + 2
            def expd(i): return cS.base + 4 * i + 3
            def yhd(i): return cS.base + 4 * (i + 1)
            def varpd(i): return cV.base + 11 * i + 6
            def negbd(i): return cV.base + 11 * i + 8
            def rod(i): return cV.base + 11 * (i + 1)

            @ph.on("sync")
            def _(e):
                pb.inc(e.dma_start(out=gb[:], in_=w["gb"]), gds, 16)
                pb.inc(e.dma_start(out=bb[:], in_=w["bb"]), gds, 16)
                for tt in range(NT):
                    if tt >= 2:
                        cV.wait(e, rod(tt - 2))
                        ins = e.dma_start(out=dst_r[(tt - 2) * 128:(tt - 1) * 128, :],
                                          in_=ro[:, (tt % 2) * D:(tt % 2 + 1) * D])
                        pb.inc(ins, osl[tt % 2], 16)
                    ins = e.dma_start(out=rt2[:, (tt % 2) * D:(tt % 2 + 1) * D],
                                      in_=src_r[tt * 128:(tt + 1) * 128, :])
                    pb.inc(ins, dsl[tt % 2], 16)
                for tt in range(NT - 2, NT):
                    cV.wait(e, rod(tt))
                    ins = e.dma_start(out=dst_r[tt * 128:(tt + 1) * 128, :],
                                      in_=ro[:, (tt % 2) * D:(tt % 2 + 1) * D])
                    pb.inc(ins, osl[tt % 2], 16)
                e.wait_ge(pb.sem(osl[0]), dbase[osl[0]] + 16 * 4)
                e.wait_ge(pb.sem(osl[1]), dbase[osl[1]] + 16 * 4)

            @ph.on("scalar")
            def _(e):
                for tt in range(NT):
                    c = tt % 2
                    yt = y_acc[:, tt * D:(tt + 1) * D]
                    if tt >= 2:
                        cV.wait(e, rod(tt - 2))
                    cS.guard(e)
                    cS.tick(e.activation(yh[:, c * D:(c + 1) * D], yt, AF.Square,
                                         accum_out=st2[:, 1:2]))
                    cV.wait(e, varpd(tt))
                    cS.guard(e)
                    cS.tick(e.activation(st2[:, 8:9], st2[:, 6:7], AF.Ln))
                    cS.guard(e)
                    cS.tick(e.activation(st2[:, 10 + c:11 + c], st2[:, 8:9], AF.Exp, scale=-0.5))
                    cV.wait(e, negbd(tt))
                    cS.guard(e)
                    cS.tick(e.activation(yh[:, c * D:(c + 1) * D], yt, AF.Identity,
                                         bias=st2[:, 12 + c:13 + c],
                                         scale=st2[:, 10 + c:11 + c]))

            @ph.on("vector")
            def _(e):
                for tt in range(NT):
                    c = tt % 2
                    yt = y_acc[:, tt * D:(tt + 1) * D]
                    cV.guard(e)
                    cV.tick(e.reduce_sum(st2[:, 0:1], yt, AX.X))
                    cS.wait(e, sqd(tt))
                    if tt >= 1:
                        cS.wait(e, lnd(tt - 1))
                    for args in ((st2[:, 2:3], st2[:, 0:1], 1.0 / D),
                                 (st2[:, 3:4], st2[:, 1:2], 1.0 / D)):
                        cV.guard(e)
                        cV.tick(e.tensor_scalar_mul(*args))
                    cV.guard(e)
                    cV.tick(e.tensor_mul(st2[:, 4:5], st2[:, 2:3], st2[:, 2:3]))
                    cV.guard(e)
                    cV.tick(e.tensor_sub(st2[:, 5:6], st2[:, 3:4], st2[:, 4:5]))
                    cV.guard(e)
                    cV.tick(e.tensor_scalar_add(st2[:, 6:7], st2[:, 5:6], LN_EPS))
                    cS.wait(e, expd(tt))
                    cV.guard(e)
                    cV.tick(e.tensor_mul(st2[:, 14:15], st2[:, 2:3], st2[:, 10 + c:11 + c]))
                    cV.guard(e)
                    cV.tick(e.tensor_scalar_mul(st2[:, 12 + c:13 + c], st2[:, 14:15], -1.0))
                    cS.wait(e, yhd(tt))
                    if tt == 0:
                        e.wait_ge(pb.sem(gds), dbase[gds] + 32)
                    e.wait_ge(pb.sem(dsl[tt % 2]), dbase[dsl[tt % 2]] + 16 * (tt // 2 + 1))
                    if tt >= 2:
                        e.wait_ge(pb.sem(osl[tt % 2]), dbase[osl[tt % 2]] + 16 * (tt // 2))
                    yhc = yh[:, c * D:(c + 1) * D]
                    cV.guard(e)
                    cV.tick(e.tensor_mul(yhc, yhc, gb[:]))
                    cV.guard(e)
                    cV.tick(e.tensor_add(yhc, yhc, bb[:]))
                    cV.guard(e)
                    cV.tick(e.tensor_add(ro[:, c * D:(c + 1) * D],
                                         rt2[:, c * D:(c + 1) * D], yhc))

            pb.emit(ph)
            cS.check(4 * NT)
            cV.check(11 * NT)



def _ffn_dram(nc):
    return dict(
        gate=nc.dram_tensor("gate_st", [128, 128], F32, kind="ExternalInput").ap(),
        ex=nc.dram_tensor("ex_st", [4, 128, 8192], F32, kind="ExternalInput").ap(),
        up=nc.dram_tensor("up_st", [64, 128, 2048], F32, kind="ExternalInput").ap(),
        dn=nc.dram_tensor("dn_st", [16, 128, 8192], F32, kind="ExternalInput").ap(),
        gb=nc.dram_tensor("gb_st", [128, D], F32, kind="ExternalInput").ap(),
        bb=nc.dram_tensor("bb_st", [128, D], F32, kind="ExternalInput").ap(),
    )


def _stage_ffn(inputs):
    up = np.ascontiguousarray(
        inputs["L0_ffn_up_w"].reshape(16, 128, 64, 128).transpose(2, 1, 0, 3).reshape(64, 128, 2048))
    dn = np.ascontiguousarray(
        inputs["L0_ffn_down_w"].reshape(4, 16, 128, 4, 512).transpose(0, 3, 2, 1, 4).reshape(16, 128, 8192))
    ex = np.ascontiguousarray(
        inputs["L0_ffn_expert_weight"].reshape(16, 128, 4, 512).transpose(2, 1, 0, 3).reshape(4, 128, 8192))
    gate = np.ascontiguousarray(
        inputs["L0_ffn_gate_w"].reshape(16, 128, 8).transpose(1, 0, 2).reshape(128, 128))
    gb = np.ascontiguousarray(np.broadcast_to(inputs["L0_ffn_ln_gamma"], (128, D)))
    bb = np.ascontiguousarray(np.broadcast_to(inputs["L0_ffn_ln_beta"], (128, D)))
    return dict(gate_st=gate, ex_st=ex, up_st=up, dn_st=dn, gb_st=gb, bb_st=bb)


def build_nc(dev=None, pairs=4):
    nc = bass.Bass("TRN2", target_bir_lowering=False)
    ident_d = nc.dram_tensor("ident", [128, 128], BF16, kind="ExternalInput").ap()
    onesb_d = nc.dram_tensor("ones_bf", [128, 1], BF16, kind="ExternalInput").ap()
    onesf_d = nc.dram_tensor("ones_f", [1, 128], F32, kind="ExternalInput").ap()
    out = nc.dram_tensor("out", [T, D], F32, kind="ExternalOutput").ap()
    wf = _ffn_dram(nc)
    wa = _a_dram(nc)
    wb = _b_dram(nc, pairs=pairs)

    pb = PB(nc)
    with (
        nc.sbuf_tensor("c_ident", [128, 128], BF16) as ident,
        nc.sbuf_tensor("c_onesb", [128, 1], BF16) as onesb,
        nc.sbuf_tensor("c_onesf", [1, 128], F32) as onesf,
    ):
        cn = dict(ident=ident, ones_bf=onesb, ones_f=onesf)
        ph = Phase("init")
        b0 = pb.bases("c_dma")

        @ph.on("sync")
        def _(e):
            pb.inc(e.dma_start(out=ident[:], in_=ident_d), "c_dma", 16)
            pb.inc(e.dma_start(out=onesb[:], in_=onesb_d), "c_dma", 16)
            pb.inc(e.dma_start(out=onesf[:], in_=onesf_d), "c_dma", 16)
            e.wait_ge(pb.sem("c_dma"), b0["c_dma"] + 48)
        pb.emit(ph)

        if dev == "ffn":
            dev_r = nc.dram_tensor("dev_r", [T, D], F32, kind="ExternalInput").ap()
            emit_ffn(pb, cn, wf, dev_r, out, 0)
        elif dev == "attn_a":
            emit_attn_a(pb, cn, wa, out)
        elif dev == "attn_b":
            dev_r = nc.dram_tensor("dev_r", [T, D], F32, kind="ExternalInput").ap()
            emit_attn_b(pb, cn, wb, dev_r, out)
        else:
            r1 = nc.dram_tensor("r1", [T, D], F32).ap()
            r2 = nc.dram_tensor("r2", [T, D], F32).ap()
            r3 = nc.dram_tensor("r3", [T, D], F32).ap()
            emit_attn_a(pb, cn, wa, r1)
            emit_ffn(pb, cn, wf, r1, r2, 0)
            emit_attn_b(pb, cn, wb, r2, r3)
            emit_ffn(pb, cn, wf, r3, out, 1)
    return nc


def _core_inputs(inputs, core):
    m = dict(ident=np.eye(128, dtype=ml_dtypes.bfloat16),
             ones_bf=np.ones((128, 1), ml_dtypes.bfloat16),
             ones_f=np.ones((1, 128), np.float32))
    m.update(_stage_ffn(inputs))
    m.update(_stage_attn_a(inputs, core))
    m.update(_stage_attn_b(inputs, core))
    return m


def kernel(**inputs):
    nc = build_nc()
    in_maps = [_core_inputs(inputs, c) for c in range(8)]
    res = run_bass_kernel_spmd(nc, in_maps, core_ids=list(range(8)))
    shards = [res.results[c]["out"] for c in range(8)]
    out = np.empty((4, CTX, D), np.float32)
    for c in range(8):
        out[c // 2, (c % 2) * T:(c % 2 + 1) * T, :] = shards[c]
    return out


QLR = 1536
KVLR = 512
DR = 64
DQ = 192
A_EPS = 1e-6
A_SCALE = DQ ** -0.5


_DBG = [False]


def _a_dram(nc):
    d = {}
    if _DBG[0]:
        d.update(
            dbg_qn=nc.dram_tensor("dbg_qn", [128, 16 * 1024], BF16, kind="ExternalOutput").ap(),
            dbg_ckvn=nc.dram_tensor("dbg_ckvn", [128, 4 * 2048], BF16, kind="ExternalOutput").ap(),
            dbg_kper=nc.dram_tensor("dbg_kper", [128, 2048], BF16, kind="ExternalOutput").ap(),
            dbg_qper=nc.dram_tensor("dbg_qper", [128, 8 * 1024], BF16, kind="ExternalOutput").ap(),
            dbg_o=nc.dram_tensor("dbg_o", [128, 16 * 1024], BF16, kind="ExternalOutput").ap(),
        )
    d.update(
        x_ctx=nc.dram_tensor("x_ctx", [CTX, D], F32, kind="ExternalInput").ap(),
        x_own=nc.dram_tensor("x_own", [T, D], F32, kind="ExternalInput").ap(),
        wqa=nc.dram_tensor("wqa_img", [128, 16 * 1536], F32, kind="ExternalInput").ap(),
        wqbN=nc.dram_tensor("wqbN_img", [128, 12 * 2048], F32, kind="ExternalInput").ap(),
        wqbP=nc.dram_tensor("wqbP_img", [128, 12 * 1024], F32, kind="ExternalInput").ap(),
        wkva=nc.dram_tensor("wkva_img", [128, 16 * 512], F32, kind="ExternalInput").ap(),
        wkvaP=nc.dram_tensor("wkvaP_img", [128, 16 * 64], F32, kind="ExternalInput").ap(),
        wkvbK=nc.dram_tensor("wkvbK_img", [128, 4 * 2048], F32, kind="ExternalInput").ap(),
        wkvbV=nc.dram_tensor("wkvbV_img", [128, 4 * 2048], F32, kind="ExternalInput").ap(),
        wo=nc.dram_tensor("wo_img", [128, 16 * 2048], F32, kind="ExternalInput").ap(),
        qg=nc.dram_tensor("qg_img", [128, 12], F32, kind="ExternalInput").ap(),
        kvg=nc.dram_tensor("kvg_img", [128, 4], F32, kind="ExternalInput").ap(),
        cosq=nc.dram_tensor("cosq_img", [128, 8 * 64], F32, kind="ExternalInput").ap(),
        sinq=nc.dram_tensor("sinq_img", [128, 8 * 64], F32, kind="ExternalInput").ap(),
        cosk=nc.dram_tensor("cosk_img", [128, 16 * 64], F32, kind="ExternalInput").ap(),
        sink=nc.dram_tensor("sink_img", [128, 16 * 64], F32, kind="ExternalInput").ap(),
        mask=nc.dram_tensor("mask_img", [8, 128, 2048], BF16, kind="ExternalInput").ap(),
    )
    return d


def _rope_tables(positions):
    inv = 1.0 / (10000.0 ** (np.arange(0, DR, 2, dtype=np.float64) / DR))
    fr = positions[:, None].astype(np.float64) * inv[None, :]
    emb = np.concatenate([fr, fr], -1)
    return np.cos(emb).astype(np.float32), np.sin(emb).astype(np.float32)


def _stage_attn_a(inputs, core):
    seq, hf = core // 2, core % 2
    x = inputs["L0_attn_a_hidden_states"][seq]
    w_qa = inputs["L0_attn_a_w_qa"]
    w_qb = inputs["L0_attn_a_w_qb"]
    w_kva = inputs["L0_attn_a_w_kva"]
    w_kvb = inputs["L0_attn_a_w_kvb"]
    w_o = inputs["L0_attn_a_w_o"]
    m = {}
    m["x_ctx"] = np.ascontiguousarray(x)
    m["x_own"] = np.ascontiguousarray(x[hf * T:(hf + 1) * T])
    m["wqa_img"] = np.ascontiguousarray(
        w_qa.reshape(1536, 16, 128).transpose(2, 1, 0).reshape(128, 16 * 1536))
    wqb3 = w_qb.reshape(H, 192, 1536)
    wqbN = wqb3[:, :128, :].reshape(2048, 1536)
    wqbP = wqb3[:, 128:, :].reshape(1024, 1536)
    m["wqbN_img"] = np.ascontiguousarray(
        wqbN.transpose(1, 0).reshape(12, 128, 2048).transpose(1, 0, 2).reshape(128, 12 * 2048))
    m["wqbP_img"] = np.ascontiguousarray(
        wqbP.transpose(1, 0).reshape(12, 128, 1024).transpose(1, 0, 2).reshape(128, 12 * 1024))
    m["wkva_img"] = np.ascontiguousarray(
        w_kva[:512].reshape(512, 16, 128).transpose(2, 1, 0).reshape(128, 16 * 512))
    m["wkvaP_img"] = np.ascontiguousarray(
        w_kva[512:].reshape(64, 16, 128).transpose(2, 1, 0).reshape(128, 16 * 64))
    wkvb3 = w_kvb.reshape(H, 256, 512)
    wkvbK = wkvb3[:, :128, :].reshape(2048, 512)
    wkvbV = wkvb3[:, 128:, :].reshape(2048, 512)
    m["wkvbK_img"] = np.ascontiguousarray(
        wkvbK.transpose(1, 0).reshape(4, 128, 2048).transpose(1, 0, 2).reshape(128, 4 * 2048))
    m["wkvbV_img"] = np.ascontiguousarray(
        wkvbV.transpose(1, 0).reshape(4, 128, 2048).transpose(1, 0, 2).reshape(128, 4 * 2048))
    m["wo_img"] = np.ascontiguousarray(
        w_o.reshape(4, 512, 16, 128).transpose(3, 2, 0, 1).reshape(128, 16 * 2048))
    m["qg_img"] = np.ascontiguousarray(
        inputs["L0_attn_a_q_a_ln_w"].reshape(12, 128).transpose(1, 0))
    m["kvg_img"] = np.ascontiguousarray(
        inputs["L0_attn_a_kv_a_ln_w"].reshape(4, 128).transpose(1, 0))
    posq = np.arange(hf * T, (hf + 1) * T)
    cq_, sq_ = _rope_tables(posq)
    m["cosq_img"] = np.ascontiguousarray(cq_.reshape(8, 128, 64).transpose(1, 0, 2).reshape(128, 512))
    m["sinq_img"] = np.ascontiguousarray(sq_.reshape(8, 128, 64).transpose(1, 0, 2).reshape(128, 512))
    ck_, sk_ = _rope_tables(np.arange(CTX))
    m["cosk_img"] = np.ascontiguousarray(ck_.reshape(16, 128, 64).transpose(1, 0, 2).reshape(128, 1024))
    m["sink_img"] = np.ascontiguousarray(sk_.reshape(16, 128, 64).transpose(1, 0, 2).reshape(128, 1024))
    qpos = (hf * T + np.arange(T))[:, None]
    kpos = np.arange(CTX)[None, :]
    mask = np.where(kpos <= qpos, 0.0, BIG_NEG).astype(ml_dtypes.bfloat16)
    m["mask_img"] = np.ascontiguousarray(mask.reshape(8, 128, 2048))
    return m


class View:

    def __init__(self, t, tcols, off, n):
        self.t, self.tcols, self.off, self.n = t, tcols, off, n

    def __getitem__(self, idx):
        p, c = idx
        a = self.off + (c.start or 0)
        b = self.off + (c.stop if c.stop is not None else self.n)
        return self.t[p, a:b]

    def sap(self, off, chunks, step, width, p0=0, pn=128):
        return bass.AP(self.t, p0 * self.tcols + self.off + off,
                       [[self.tcols, pn], [step, chunks], [1, width]])


class Arena:
    def __init__(self, nc, name, cols_bf16, cols_f32):
        self._g1 = nc.sbuf_tensor(f"{name}_bf", [128, cols_bf16], BF16)
        self._g2 = nc.sbuf_tensor(f"{name}_f", [128, cols_f32], F32)
        self.tb = self._g1.__enter__()
        self.tf = self._g2.__enter__()
        self.nb, self.nf = cols_bf16, cols_f32
        self.free = {BF16: [(0, cols_bf16)], F32: [(0, cols_f32)]}
        self.live = {}

    def alloc(self, name, n, dt=BF16):
        fl = self.free[dt]
        best = None
        for i, (a, b) in enumerate(fl):
            if b - a >= n and (best is None or b - a < fl[best][1] - fl[best][0]):
                best = i
        if best is None:
            raise MemoryError(f"arena {name} {n} {dt}")
        a, b = fl[best]
        fl[best] = (a + n, b)
        if fl[best][0] == fl[best][1]:
            fl.pop(best)
        self.live[name] = (a, n, dt)
        t = self.tb if dt == BF16 else self.tf
        return View(t, self.nb if dt == BF16 else self.nf, a, n)

    def freev(self, *names):
        for name in names:
            a, n, dt = self.live.pop(name)
            fl = sorted(self.free[dt] + [(a, a + n)])
            out = []
            for seg in fl:
                if out and out[-1][1] == seg[0]:
                    out[-1] = (out[-1][0], seg[1])
                else:
                    out.append(seg)
            self.free[dt] = out

    def close(self):
        self._g2.__exit__(None, None, None)
        self._g1.__exit__(None, None, None)


def emit_attn_a(pb, cn, w, dst_r):
    nc = pb.nc
    ar = Arena(nc, "aa", 87040, 9216)
    ident = cn["ident"]
    ones_bf = cn["ones_bf"]
    ones_f = cn["ones_f"]

    h_ctx = ar.alloc("h_ctx", 16 * 2048)
    h_own = ar.alloc("h_own", 16 * 1024)
    wqa = ar.alloc("wqa", 16 * 1536)
    xln = ar.alloc("xln", 2 * D)
    rt = ar.alloc("rt", 3 * D, F32)
    cosk = ar.alloc("cosk", 1024, F32)
    sink = ar.alloc("sink", 1024, F32)
    cosq = ar.alloc("cosq", 512, F32)
    sinq = ar.alloc("sinq", 512, F32)
    tpsg = nc.psum_tensor("aa_tps", [128, 8 * 1024], BF16)
    tps = tpsg.__enter__()
    with (
        nc.sbuf_tensor("aa_st", [128, 16], F32) as stt_,
        nc.sbuf_tensor("aa_qg", [128, 12], F32) as qg,
        nc.sbuf_tensor("aa_kvg", [128, 4], F32) as kvg,
    ):
        ph = Phase("a1")
        dws = pb.bases("a1w", "a1wg")

        @ph.on("gpsimd")
        def _(e):
            pb.inc(e.dma_start(out=wqa[:, :], in_=w["wqa"]), "a1wg", 16)

        @ph.on("sync")
        def _(e):
            for dst, src in ((cosk, w["cosk"]), (sink, w["sink"]),
                             (cosq, w["cosq"]), (sinq, w["sinq"])):
                pb.inc(e.dma_start(out=dst[:, :], in_=src), "a1w", 16)
            pb.inc(e.dma_start(out=qg[:], in_=w["qg"]), "a1w", 16)
            pb.inc(e.dma_start(out=kvg[:], in_=w["kvg"]), "a1w", 16)

        src_tiles = [w["x_ctx"][i * 128:(i + 1) * 128, :] for i in range(16)] + \
                    [w["x_own"][i * 128:(i + 1) * 128, :] for i in range(8)]
        dsts = [(h_ctx, i * 128, 2048) for i in range(16)] + \
               [(h_own, i * 128, 1024) for i in range(8)]
        sb = dict(rt=rt, st=stt_, xln=xln, tps=tps, ident=ident)
        emit_ln_fm(pb, ph, "aln", src_tiles, None, 0, 0, sb, dsts=dsts)
        pb.emit(ph)
        for f in ph.fns.get("_check", []):
            f()
        tpsg.__exit__(None, None, None)

        ar.freev("xln", "rt")
        cq = ar.alloc("cq", 12 * 1024)
        sqc = ar.alloc("sqc", 1024)
        rstdq = ar.alloc("rstdq", 1024, F32)
        with (
            nc.psum_tensor("a2_ps", [128, 4 * 512], F32) as ps2,
            nc.psum_tensor("a2_pss", [128, 1024], F32) as pss,
            nc.psum_tensor("a2_psb", [128, 1024], F32) as psb,
        ):
            ph = Phase("a2")
            cT = Clk(pb, "a2T")
            cS = Clk(pb, "a2S")
            cV = Clk(pb, "a2V")
            dw2 = pb.bases("a2w")


            def cqd(g): return cT.base + g + 1
            def ssd(kt): return cT.base + 24 + kt + 1
            def bcd(nb): return cT.base + 36 + nb + 1

            def sqd(kt): return cS.base + kt + 1
            def rsd(): return cS.base + 12 + 2

            def evq(g): return cV.base + g + 1
            def cqn(j): return cV.base + 24 + j + 1

            @ph.on("tensor")
            def _(e):
                e.wait_ge(pb.sem("a1wg"), dws["a1wg"] + 16)
                e.wait_ge(pb.sem("a1w"), dws["a1w"] + 16 * 6)
                for qt_ in range(12):
                    for tb in range(2):
                        g = qt_ * 2 + tb
                        if g >= 4:
                            cV.wait(e, evq(g - 4))
                        for dt in range(16):
                            ins = e.matmul(ps2[:, (g % 4) * 512:(g % 4) * 512 + 512],
                                           wqa[:, dt * 1536 + qt_ * 128:dt * 1536 + qt_ * 128 + 128],
                                           h_own[:, dt * 1024 + tb * 512:dt * 1024 + tb * 512 + 512],
                                           start=(dt == 0), stop=(dt == 15))
                        cT.tick(ins)
                for kt in range(12):
                    cS.wait(e, sqd(kt))
                    for nb in range(2):
                        ins = e.matmul(pss[0:1, nb * 512:nb * 512 + 512],
                                       ones_bf[:], sqc[:, nb * 512:nb * 512 + 512],
                                       start=(kt == 0), stop=(kt == 11))
                    cT.tick(ins)
                cS.wait(e, rsd())
                for nb in range(2):
                    ins = e.matmul(psb[:, nb * 512:nb * 512 + 512], ones_f[:],
                                   rstdq[0:1, nb * 512:nb * 512 + 512],
                                   start=True, stop=True)
                    cT.tick(ins)

            @ph.on("scalar")
            def _(e):
                for kt in range(12):
                    cV.wait(e, evq(kt * 2 + 1))
                    if kt >= 1:
                        cT.wait(e, ssd(kt - 1))
                    cS.guard(e)
                    cS.tick(e.square(sqc[:, :], cq[:, kt * 1024:(kt + 1) * 1024]))
                cV.wait(e, cV.base + 25)
                cS.guard(e)
                cS.tick(e.activation(rstdq[0:1, 0:1024], rstdq[0:1, 0:1024], AF.Ln))
                cS.guard(e)
                cS.tick(e.activation(rstdq[0:1, 0:1024], rstdq[0:1, 0:1024], AF.Exp, scale=-0.5))

            @ph.on("vector")
            def _(e):
                for g in range(24):
                    cT.wait(e, cqd(g))
                    qt_, tb = g // 2, g % 2
                    cV.guard(e)
                    cV.tick(e.tensor_copy(cq[:, qt_ * 1024 + tb * 512:qt_ * 1024 + tb * 512 + 512],
                                          ps2[:, (g % 4) * 512:(g % 4) * 512 + 512]))
                cT.wait(e, ssd(11))
                cV.guard(e)
                cV.tick(e.tensor_scalar(rstdq[0:1, 0:1024], pss[0:1, 0:1024],
                                        1.0 / QLR, A_EPS, ALU.mult, ALU.add))
                for kt in range(12):
                    for tb in range(2):
                        cT.wait(e, bcd(tb))
                        cV.guard(e)
                        sl = cq[:, kt * 1024 + tb * 512:kt * 1024 + tb * 512 + 512]
                        ins = e.scalar_tensor_tensor(sl, sl, qg[:, kt:kt + 1],
                                                     psb[:, tb * 512:tb * 512 + 512],
                                                     ALU.mult, ALU.mult)
                        cV.tick(ins)

            pb.emit(ph)
            cT.check(24 + 12 + 2)
            cS.check(12 + 2)
            cV.check(24 + 1 + 24)

        ar.freev("h_own", "wqa", "sqc", "rstdq")
        wkva = ar.alloc("wkva", 16 * 512)
        wkvaP = ar.alloc("wkvaP", 16 * 64)
        wqbN = ar.alloc("wqbN", 6 * 2048)
        ckvn = ar.alloc("ckvn", 4 * 2048)
        kper = ar.alloc("kper", 2048)
        kpet = ar.alloc("kpet", 1024)
        kpet2 = ar.alloc("kpet2", 1024)
        sqk = ar.alloc("sqk", 2048)
        rstdk = ar.alloc("rstdk", 2048, F32)
        ropet = ar.alloc("ropet", 128, F32)
        with (
            nc.psum_tensor("a3_ps", [128, 1024], F32) as ps3,
            nc.psum_tensor("a3_pss", [128, 2048], F32) as pss3,
            nc.psum_tensor("a3_tp", [128, 2 * 1024], BF16) as tpsk,
        ):
            ph = Phase("a3")
            cT = Clk(pb, "a3T")
            cS = Clk(pb, "a3S")
            cV = Clk(pb, "a3V")
            dk = pb.bases("a3k")

            def ckvd(g): return cT.base + g + 1
            def kpd(ct): return cT.base + 16 + ct + 1
            def ssd(kt): return cT.base + 32 + kt + 1
            def bcd(nb): return cT.base + 36 + nb + 1
            def tpd(ct): return cT.base + 40 + ct + 1
            def cevd(g): return cS.base + g + 1
            def kevd(ct): return cS.base + 16 + ct + 1
            def sqd(kt): return cS.base + 32 + kt + 1
            def rsd(): return cS.base + 38
            def rod(ct): return cV.base + 6 * (ct + 1)
            def nsd(j): return cV.base + 97 + j + 1
            def ted(ct): return cV.base + 113 + ct + 1

            @ph.on("gpsimd")
            def _(e):
                pb.inc(e.dma_start(out=wkva[:, :], in_=w["wkva"]), "a2w", 16)
                pb.inc(e.dma_start(out=wkvaP[:, :], in_=w["wkvaP"]), "a2w", 16)
                pb.inc(e.dma_start(out=wqbN[:, 0:12288], in_=w["wqbN"][:, 0:12288]), "a4w", 16)

            @ph.on("tensor")
            def _(e):
                e.wait_ge(pb.sem("a2w"), dw2["a2w"] + 32)
                for ft in range(4):
                    for nb in range(4):
                        g = ft * 4 + nb
                        if g >= 2:
                            cS.wait(e, cevd(g - 2))
                        for dt in range(16):
                            ins = e.matmul(ps3[:, (g % 2) * 512:(g % 2) * 512 + 512],
                                           wkva[:, dt * 512 + ft * 128:dt * 512 + ft * 128 + 128],
                                           h_ctx[:, dt * 2048 + nb * 512:dt * 2048 + nb * 512 + 512],
                                           start=(dt == 0), stop=(dt == 15))
                        cT.tick(ins)
                for ct in range(16):
                    if ct >= 2:
                        cS.wait(e, kevd(ct - 2))
                    else:
                        cS.wait(e, cevd(14 + ct))
                    for dt in range(16):
                        ins = e.matmul(ps3[:, (ct % 2) * 512:(ct % 2) * 512 + 64],
                                       h_ctx[:, dt * 2048 + ct * 128:dt * 2048 + ct * 128 + 128],
                                       wkvaP[:, dt * 64:(dt + 1) * 64],
                                       start=(dt == 0), stop=(dt == 15))
                    cT.tick(ins)
                for kt in range(4):
                    cS.wait(e, sqd(kt))
                    for nb in range(4):
                        ins = e.matmul(pss3[0:1, nb * 512:nb * 512 + 512],
                                       ones_bf[:], sqk[:, nb * 512:nb * 512 + 512],
                                       start=(kt == 0), stop=(kt == 3))
                    cT.tick(ins)
                cS.wait(e, rsd())
                for nb in range(4):
                    ins = e.matmul(pss3[:, nb * 512:nb * 512 + 512], ones_f[:],
                                   rstdk[0:1, nb * 512:nb * 512 + 512],
                                   start=True, stop=True)
                    cT.tick(ins)
                for ct in range(16):
                    cV.wait(e, rod(ct))
                    if ct >= 2:
                        cV.wait(e, ted(ct - 2))
                    ins = e.transpose(tpsk[0:64, (ct % 2) * 1024:(ct % 2) * 1024 + 128],
                                      kpet2[:, ct * 64:(ct + 1) * 64], ident[:, :])
                    cT.tick(ins)

            @ph.on("scalar")
            def _(e):
                for g in range(16):
                    ft, nb = g // 4, g % 4
                    cT.wait(e, ckvd(g))
                    cS.guard(e)
                    cS.tick(e.copy(ckvn[:, ft * 2048 + nb * 512:ft * 2048 + nb * 512 + 512],
                                   ps3[:, (g % 2) * 512:(g % 2) * 512 + 512]))
                for ct in range(16):
                    cT.wait(e, kpd(ct))
                    cS.guard(e)
                    cS.tick(e.copy(kpet[:, ct * 64:(ct + 1) * 64],
                                   ps3[:, (ct % 2) * 512:(ct % 2) * 512 + 64]))
                for kt in range(4):
                    if kt >= 1:
                        cT.wait(e, ssd(kt - 1))
                    cS.guard(e)
                    cS.tick(e.square(sqk[:, :], ckvn[:, kt * 2048:(kt + 1) * 2048]))
                cV.wait(e, cV.base + 97)
                cS.guard(e)
                cS.tick(e.activation(rstdk[0:1, 0:2048], rstdk[0:1, 0:2048], AF.Ln))
                cS.guard(e)
                cS.tick(e.activation(rstdk[0:1, 0:2048], rstdk[0:1, 0:2048], AF.Exp, scale=-0.5))

            @ph.on("vector")
            def _(e):
                for ct in range(16):
                    cS.wait(e, kevd(ct))
                    b = ct * 64
                    ck0 = cosk[:, ct * 64:ct * 64 + 32]
                    ck1 = cosk[:, ct * 64 + 32:ct * 64 + 64]
                    sk0 = sink[:, ct * 64:ct * 64 + 32]
                    sk1 = sink[:, ct * 64 + 32:ct * 64 + 64]
                    lo = kpet[:, b:b + 32]
                    hi = kpet[:, b + 32:b + 64]
                    cV.guard(e)
                    cV.tick(e.tensor_mul(ropet[:, 0:32], lo, ck0))
                    cV.guard(e)
                    cV.tick(e.tensor_mul(ropet[:, 32:64], hi, sk0))
                    cV.guard(e)
                    cV.tick(e.tensor_sub(kpet2[:, b:b + 32], ropet[:, 0:32], ropet[:, 32:64]))
                    cV.guard(e)
                    cV.tick(e.tensor_mul(ropet[:, 64:96], hi, ck1))
                    cV.guard(e)
                    cV.tick(e.tensor_mul(ropet[:, 96:128], lo, sk1))
                    cV.guard(e)
                    cV.tick(e.tensor_add(kpet2[:, b + 32:b + 64], ropet[:, 64:96], ropet[:, 96:128]))
                cT.wait(e, ssd(3))
                cV.guard(e)
                cV.tick(e.tensor_scalar(rstdk[0:1, 0:2048], pss3[0:1, 0:2048],
                                        1.0 / KVLR, A_EPS, ALU.mult, ALU.add))
                for kt in range(4):
                    for nb in range(4):
                        j = kt * 4 + nb
                        cT.wait(e, bcd(nb))
                        cV.guard(e)
                        sl = ckvn[:, kt * 2048 + nb * 512:kt * 2048 + nb * 512 + 512]
                        ins = e.scalar_tensor_tensor(sl, sl, kvg[:, kt:kt + 1],
                                                     pss3[:, nb * 512:nb * 512 + 512],
                                                     ALU.mult, ALU.mult)
                        cV.tick(ins)
                for ct in range(16):
                    cT.wait(e, tpd(ct))
                    cV.guard(e)
                    cV.tick(e.tensor_copy(kper[0:64, ct * 128:(ct + 1) * 128],
                                          tpsk[0:64, (ct % 2) * 1024:(ct % 2) * 1024 + 128]))

            @ph.on("sync")
            def _(e):
                cV.wait(e, ted(15))
                pb.inc(e.dma_start(out=kper[64:128, 0:2048], in_=kper[0:64, 0:2048]), "a3k", 16)
                e.wait_ge(pb.sem("a3k"), dk["a3k"] + 16)

            pb.emit(ph)
            cT.check(16 + 16 + 4 + 4 + 16)
            cS.check(16 + 16 + 4 + 2)
            cV.check(96 + 1 + 16 + 16)

        ar.freev("h_ctx", "wkva", "wkvaP", "kpet", "kpet2", "sqk", "rstdk", "ropet")
        wqbN2 = ar.alloc("wqbN2", 6 * 2048)
        wqbP = ar.alloc("wqbP", 12 * 1024)
        qn = ar.alloc("qn", 16 * 1024)
        with nc.psum_tensor("a4_ps", [128, 2048], F32) as ps4:
            ph = Phase("a4")
            cT = Clk(pb, "a4T")
            cS = Clk(pb, "a4S")
            d4 = pb.bases("a4w")

            def mmd(g): return cT.base + g + 1
            def evd(g): return cS.base + g + 1

            @ph.on("gpsimd")
            def _(e):
                pb.inc(e.dma_start(out=wqbN2[:, :], in_=w["wqbN"][:, 12288:24576]), "a4w", 16)
                pb.inc(e.dma_start(out=wqbP[:, :], in_=w["wqbP"]), "a5w", 16)

            @ph.on("tensor")
            def _(e):
                e.wait_ge(pb.sem("a4w"), d4["a4w"] + 16)
                for ft in range(16):
                    for tb in range(2):
                        g = ft * 2 + tb
                        if g >= 4:
                            cS.wait(e, evd(g - 4))
                        for kt in range(12):
                            wsrc = wqbN if kt < 6 else wqbN2
                            kk = kt if kt < 6 else kt - 6
                            ins = e.matmul(ps4[:, (g % 4) * 512:(g % 4) * 512 + 512],
                                           wsrc[:, kk * 2048 + ft * 128:kk * 2048 + ft * 128 + 128],
                                           cq[:, kt * 1024 + tb * 512:kt * 1024 + tb * 512 + 512],
                                           start=(kt == 0), stop=(kt == 11))
                        cT.tick(ins)

            @ph.on("scalar")
            def _(e):
                for g in range(32):
                    ft, tb = g // 2, g % 2
                    cT.wait(e, mmd(g))
                    cS.guard(e)
                    ins = e.activation(qn[:, ft * 1024 + tb * 512:ft * 1024 + tb * 512 + 512],
                                       ps4[:, (g % 4) * 512:(g % 4) * 512 + 512],
                                       AF.Copy, scale=A_SCALE)
                    cS.tick(ins)

            pb.emit(ph)
            cT.check(32)
            cS.check(32)

        ar.freev("wqbN", "wqbN2")
        qpet = ar.alloc("qpet", 8 * 1024)
        qpet2 = ar.alloc("qpet2", 8 * 1024)
        qper = ar.alloc("qper", 8 * 1024)
        ropeq = ar.alloc("ropeq", 4 * 256, F32)
        with (
            nc.psum_tensor("a5_ps", [128, 1024], F32) as ps5,
            nc.psum_tensor("a5_tp", [128, 2 * 1024], BF16) as tpsq,
        ):
            ph = Phase("a5")
            cT = Clk(pb, "a5T")
            cS = Clk(pb, "a5S")
            cV = Clk(pb, "a5V")
            d5 = pb.bases("a5w")

            def mmd(g): return cT.base + g + 1
            def tpd(j): return cT.base + 16 + j + 1
            def evd(g): return cS.base + g + 1
            def rod(h): return cV.base + 6 * (h + 1)
            def ted(j): return cV.base + 96 + j + 1

            @ph.on("tensor")
            def _(e):
                e.wait_ge(pb.sem("a5w"), d5["a5w"])
                for tt in range(8):
                    for pbk in range(2):
                        g = tt * 2 + pbk
                        if g >= 2:
                            cS.wait(e, evd(g - 2))
                        for kt in range(12):
                            ins = e.matmul(ps5[:, (g % 2) * 512:(g % 2) * 512 + 512],
                                           cq[:, kt * 1024 + tt * 128:kt * 1024 + tt * 128 + 128],
                                           wqbP[:, kt * 1024 + pbk * 512:kt * 1024 + pbk * 512 + 512],
                                           start=(kt == 0), stop=(kt == 11))
                        cT.tick(ins)
                for tt in range(8):
                    for hp in range(8):
                        j = tt * 8 + hp
                        cV.wait(e, rod(2 * hp + 1))
                        if j >= 2:
                            cV.wait(e, ted(j - 2))
                        ins = e.transpose(tpsq[:, (j % 2) * 1024:(j % 2) * 1024 + 128],
                                          qpet2[:, tt * 1024 + hp * 128:tt * 1024 + (hp + 1) * 128],
                                          ident[:, :])
                        cT.tick(ins)

            @ph.on("scalar")
            def _(e):
                for g in range(16):
                    tt, pbk = g // 2, g % 2
                    cT.wait(e, mmd(g))
                    cS.guard(e)
                    ins = e.activation(qpet[:, tt * 1024 + pbk * 512:tt * 1024 + pbk * 512 + 512],
                                       ps5[:, (g % 2) * 512:(g % 2) * 512 + 512],
                                       AF.Copy, scale=A_SCALE)
                    cS.tick(ins)

            @ph.on("vector")
            def _(e):
                for h in range(16):
                    cS.wait(e, evd(15))
                    lo = qpet.sap(h * 64, 8, 1024, 32)
                    hi = qpet.sap(h * 64 + 32, 8, 1024, 32)
                    lo2 = qpet2.sap(h * 64, 8, 1024, 32)
                    hi2 = qpet2.sap(h * 64 + 32, 8, 1024, 32)
                    c0 = cosq.sap(0, 8, 64, 32)
                    c1 = cosq.sap(32, 8, 64, 32)
                    s0 = sinq.sap(0, 8, 64, 32)
                    s1 = sinq.sap(32, 8, 64, 32)
                    tA = ropeq.sap(0, 8, 32, 32)
                    tB = ropeq.sap(256, 8, 32, 32)
                    cV.guard(e)
                    cV.tick(e.tensor_mul(tA, lo, c0))
                    cV.guard(e)
                    cV.tick(e.tensor_mul(tB, hi, s0))
                    cV.guard(e)
                    cV.tick(e.tensor_sub(lo2, tA, tB))
                    cV.guard(e)
                    cV.tick(e.tensor_mul(tA, hi, c1))
                    cV.guard(e)
                    cV.tick(e.tensor_mul(tB, lo, s1))
                    cV.guard(e)
                    cV.tick(e.tensor_add(hi2, tA, tB))
                for tt in range(8):
                    for hp in range(8):
                        j = tt * 8 + hp
                        cT.wait(e, tpd(j))
                        cV.guard(e)
                        cV.tick(e.tensor_copy(qper[:, hp * 1024 + tt * 128:hp * 1024 + (tt + 1) * 128],
                                              tpsq[:, (j % 2) * 1024:(j % 2) * 1024 + 128]))

            pb.emit(ph)
            cT.check(16 + 64)
            cS.check(16)
            cV.check(96 + 64)

        ar.freev("cq", "wqbP", "qpet", "qpet2", "ropeq", "cosk", "sink", "cosq", "sinq")
        o_fm = ar.alloc("o_fm", 16 * 1024)
        wkbK = ar.alloc("wkbK", 4 * 2048)
        wkbV = ar.alloc("wkbV", 4 * 2048)
        knop = ar.alloc("knop", 2048)
        vh = ar.alloc("vh", 2 * 2048)
        probs = ar.alloc("probs", 2 * 2048)
        probsT = ar.alloc("probsT", 2 * 2048)
        mskb0 = ar.alloc("mskb0", 2048)
        mskb1 = ar.alloc("mskb1", 2048)
        mskb_sl = [mskb0, mskb1]
        sc_m = ar.alloc("sc_m", 2 * 2048, F32)
        nmx = ar.alloc("nmx", 16, F32)
        with (
            nc.psum_tensor("a6_psA", [128, 1024], F32) as psA,
            nc.psum_tensor("a6_tp", [128, 4 * 1024], BF16) as tps6,
            nc.psum_tensor("a6_av", [128, 1024], F32) as psav,
        ):
            ph = Phase("a6")
            cT = Clk(pb, "a6T")
            cS = Clk(pb, "a6S")
            cV = Clk(pb, "a6V")
            d6 = pb.bases("a6w", "a6m0", "a6m1")

            def hq(n):
                return n // 8, n % 8

            Titems, Sitems, Vitems = [], [], []
            for h in range(H):
                Titems += [("kn", h, nb) for nb in range(4)]
                Titems += [("vg", h, ct) for ct in range(16)]
                Sitems += [("knev", h, nb) for nb in range(4)]
                Sitems += [("vev", h, b) for b in range(4)]
                for qt in range(NT):
                    n = h * 8 + qt
                    if n >= 2:
                        Titems += [("sc", n, 0), ("sc", n, 1)]
                        if n >= 4:
                            Titems.append(("av", n - 4))
                        Titems += [("tp", n - 2, k) for k in range(4)]
                        Titems += [("sc", n, 2), ("sc", n, 3)]
                        Titems += [("tp", n - 2, k) for k in range(4, 16)]
                    else:
                        Titems += [("sc", n, kb) for kb in range(4)]
                    if n >= 2:
                        Sitems += [("pte", n - 2, j) for j in range(4)]
                    if n >= 4:
                        Sitems.append(("oev", n - 4))
                    Sitems.append(("exp", n))
                    if n >= 2:
                        Vitems.append(("rcp", n - 2))
                        Vitems.append(("pml", n - 2))
                    Vitems += [("stt", n, kb) for kb in range(4)]
                    Vitems.append(("nmx", n))
            Titems += [("av", 124)] + [("tp", 126, k) for k in range(16)] + [("av", 125)] + [("tp", 127, k) for k in range(16)] + [("av", 126), ("av", 127)]
            Sitems += [("pte", 126, j) for j in range(4)] + [("pte", 127, j) for j in range(4)] + [("oev", n2) for n2 in (124, 125, 126, 127)]
            Vitems += [("rcp", 126), ("pml", 126), ("rcp", 127), ("pml", 127)]
            tmark = {it: i + 1 for i, it in enumerate(Titems)}
            smark = {it: i + 1 for i, it in enumerate(Sitems)}
            vmark = {it: i + 1 for i, it in enumerate(Vitems)}

            def tm(*it): return cT.base + tmark[it]
            def sm(*it): return cS.base + smark[it]
            def vm(*it): return cV.base + vmark[it]

            @ph.on("gpsimd")
            def _(e):
                pb.inc(e.dma_start(out=wkbK[:, :], in_=w["wkvbK"]), "a6w", 16)
                pb.inc(e.dma_start(out=wkbV[:, :], in_=w["wkvbV"]), "a6w", 16)

            @ph.on("sync")
            def _(e):
                for n in range(128):
                    h, qt = hq(n)
                    if n >= 2:
                        cV.wait(e, vm("stt", n - 2, 3))
                    ins = e.dma_start(out=mskb_sl[qt % 2][:, 0:2048], in_=w["mask"][qt])
                    pb.inc(ins, f"a6m{qt % 2}", 16)

            @ph.on("tensor")
            def _(e):
                e.wait_ge(pb.sem("a6w"), d6["a6w"] + 32)
                for it in Titems:
                    if it[0] == "kn":
                        _, h, nb = it
                        if h >= 1:
                            cV.wait(e, vm("stt", h * 8 - 1, 3))
                        if nb >= 2:
                            cS.wait(e, sm("knev", h, nb - 2))
                        for kt in range(4):
                            ins = e.matmul(psA[:, (nb % 2) * 512:(nb % 2) * 512 + 512],
                                           wkbK[:, kt * 2048 + h * 128:kt * 2048 + h * 128 + 128],
                                           ckvn[:, kt * 2048 + nb * 512:kt * 2048 + nb * 512 + 512],
                                           start=(kt == 0), stop=(kt == 3))
                        cT.tick(ins)
                    elif it[0] == "vg":
                        _, h, ct = it
                        if ct < 8:
                            cS.wait(e, sm("knev", h, 2 + ct // 4))
                        else:
                            cS.wait(e, sm("vev", h, (ct - 8) // 4))
                        sl6 = ((ct // 4) % 2) * 512 + (ct % 4) * 128
                        for kt in range(4):
                            ins = e.matmul(psA[:, sl6:sl6 + 128],
                                           ckvn[:, kt * 2048 + ct * 128:kt * 2048 + ct * 128 + 128],
                                           wkbV[:, kt * 2048 + h * 128:kt * 2048 + h * 128 + 128],
                                           start=(kt == 0), stop=(kt == 3))
                        cT.tick(ins)
                    elif it[0] == "sc":
                        _, n, kb = it
                        h, qt = hq(n)
                        hv = (h % 2) * 64
                        hqc = (h // 2) * 1024
                        if qt == 0:
                            cS.wait(e, sm("vev", h, 3))
                        elif kb < 2:
                            cV.wait(e, vm("stt", n - 1, kb + 2))
                        if kb >= 2:
                            cV.wait(e, vm("stt", n, kb - 2))
                        ins = e.matmul(psA[:, (kb % 2) * 512:(kb % 2) * 512 + 512],
                                       qn[:, h * 1024 + qt * 128:h * 1024 + qt * 128 + 128],
                                       knop[:, kb * 512:(kb + 1) * 512],
                                       start=True, stop=False)
                        ins = e.matmul(psA[:, (kb % 2) * 512:(kb % 2) * 512 + 512],
                                       qper[hv:hv + 64, hqc + qt * 128:hqc + qt * 128 + 128],
                                       kper[hv:hv + 64, kb * 512:(kb + 1) * 512],
                                       start=False, stop=True)
                        cT.tick(ins)
                    elif it[0] == "tp":
                        _, n, k = it
                        h, qt = hq(n)
                        cp = (qt % 2) * 2048
                        cV.wait(e, vm("pml", n))
                        if k >= 4:
                            cS.wait(e, sm("pte", n, (k - 4) // 4))
                        elif n >= 1:
                            cS.wait(e, sm("pte", n - 1, 3))
                        ins = e.transpose(tps6[:, (k % 4) * 1024:(k % 4) * 1024 + 128],
                                          probs[:, cp + k * 128:cp + (k + 1) * 128], ident[:, :])
                        cT.tick(ins)
                    else:
                        n = it[1]
                        h, qt = hq(n)
                        cp = (qt % 2) * 2048
                        cS.wait(e, sm("pte", n, 3))
                        if n >= 2:
                            cS.wait(e, sm("oev", n - 2))
                        for kt in range(16):
                            ins = e.matmul(psav[:, (qt % 2) * 512:(qt % 2) * 512 + 128],
                                           vh[:, (h % 2) * 2048 + kt * 128:(h % 2) * 2048 + (kt + 1) * 128],
                                           probsT[:, cp + kt * 128:cp + (kt + 1) * 128],
                                           start=(kt == 0), stop=(kt == 15))
                        cT.tick(ins)

            @ph.on("scalar")
            def _(e):
                for it in Sitems:
                    if it[0] == "knev":
                        _, h, nb = it
                        cT.wait(e, tm("kn", h, nb))
                        cS.guard(e)
                        cS.tick(e.copy(knop[:, nb * 512:(nb + 1) * 512],
                                       psA[:, (nb % 2) * 512:(nb % 2) * 512 + 512]))
                    elif it[0] == "vev":
                        _, h, b = it
                        cT.wait(e, tm("vg", h, 4 * b + 3))
                        cS.guard(e)
                        cS.tick(e.copy(vh[:, (h % 2) * 2048 + b * 512:(h % 2) * 2048 + (b + 1) * 512],
                                       psA[:, (b % 2) * 512:(b % 2) * 512 + 512]))
                    elif it[0] == "exp":
                        n = it[1]
                        h, qt = hq(n)
                        c_ = qt % 2
                        cp = c_ * 2048
                        cV.wait(e, vm("nmx", n))
                        if n >= 2:
                            cT.wait(e, tm("tp", n - 2, 15))
                        cS.guard(e)
                        cS.tick(e.activation(probs[:, cp:cp + 2048], sc_m[:, cp:cp + 2048], AF.Exp,
                                             bias=nmx[:, c_:c_ + 1], accum_out=nmx[:, 2 + c_:3 + c_]))
                    elif it[0] == "pte":
                        _, n, j = it
                        h, qt = hq(n)
                        cp = (qt % 2) * 2048
                        cT.wait(e, tm("tp", n, 4 * j + 3))
                        cS.guard(e)
                        src = bass.AP(tps6, 0, [[4096, 128], [1024, 4], [1, 128]])
                        dst = bass.AP(probsT.t, probsT.off + cp + j * 512,
                                      [[probsT.tcols, 128], [128, 4], [1, 128]])
                        cS.tick(e.copy(dst, src))
                    else:
                        n = it[1]
                        h, qt = hq(n)
                        cT.wait(e, tm("av", n))
                        cS.guard(e)
                        cS.tick(e.copy(o_fm[:, h * 1024 + qt * 128:h * 1024 + qt * 128 + 128],
                                       psav[:, (qt % 2) * 512:(qt % 2) * 512 + 128]))

            @ph.on("vector")
            def _(e):
                for it in Vitems:
                    if it[0] == "stt":
                        _, n, kb = it
                        h, qt = hq(n)
                        c_ = qt % 2
                        cp = c_ * 2048
                        if kb == 0:
                            e.wait_ge(pb.sem(f"a6m{qt % 2}"), d6[f"a6m{qt % 2}"] + 16 * (n // 2 + 1))
                            if n >= 2:
                                cS.wait(e, sm("exp", n - 2))
                        cT.wait(e, tm("sc", n, kb))
                        cV.guard(e)
                        ins = e.scalar_tensor_tensor(
                            sc_m[:, cp + kb * 512:cp + (kb + 1) * 512],
                            psA[:, (kb % 2) * 512:(kb % 2) * 512 + 512], 1.0,
                            mskb_sl[c_][:, kb * 512:kb * 512 + 512],
                            ALU.mult, ALU.add)
                        cV.tick(ins)
                    elif it[0] == "nmx":
                        n = it[1]
                        h, qt = hq(n)
                        c_ = qt % 2
                        cp = c_ * 2048
                        cV.guard(e)
                        cV.tick(e.reduce_max(nmx[:, c_:c_ + 1], sc_m[:, cp:cp + 2048],
                                             AX.X, negate=True))
                    elif it[0] == "rcp":
                        n = it[1]
                        c_ = (n % 8) % 2
                        cS.wait(e, sm("exp", n))
                        cV.guard(e)
                        cV.tick(e.reciprocal(nmx[:, 4 + c_:5 + c_], nmx[:, 2 + c_:3 + c_]))
                    else:
                        n = it[1]
                        c_ = (n % 8) % 2
                        cp = c_ * 2048
                        cV.guard(e)
                        cV.tick(e.tensor_scalar_mul(probs[:, cp:cp + 2048], probs[:, cp:cp + 2048],
                                                    nmx[:, 4 + c_:5 + c_]))

            pb.emit(ph)
            cT.check(len(Titems))
            cS.check(len(Sitems))
            cV.check(len(Vitems))

        ar.freev("wkbK", "wkbV", "knop", "vh", "probs", "probsT", "mskb0", "mskb1", "sc_m", "nmx",
                 "ckvn", "kper", "qn", "qper")
        wo_sb = ar.alloc("wo_sb", 16 * 2048)
        rtx = ar.alloc("rtx", 2 * D, F32)
        rox = ar.alloc("rox", 2 * D, F32)
        with nc.psum_tensor("a7_ps", [128, 1024], F32) as ps7:
            ph = Phase("a7")
            cT = Clk(pb, "a7T")
            cV = Clk(pb, "a7V")
            d7 = pb.bases("a7w", "a7x0", "a7x1", "a7o0", "a7o1")

            def mmd(g): return cT.base + g + 1
            def add_(tt, db): return cV.base + tt * 4 + db + 1

            @ph.on("gpsimd")
            def _(e):
                pb.inc(e.dma_start(out=wo_sb[:, :], in_=w["wo"]), "a7w", 16)

            @ph.on("sync")
            def _(e):
                for tt in range(NT):
                    if tt >= 2:
                        cV.wait(e, add_(tt - 2, 3))
                        ins = e.dma_start(out=dst_r[(tt - 2) * 128:(tt - 1) * 128, :],
                                          in_=rox[:, (tt % 2) * D:(tt % 2 + 1) * D])
                        pb.inc(ins, f"a7o{tt % 2}", 16)
                    ins = e.dma_start(out=rtx[:, (tt % 2) * D:(tt % 2 + 1) * D],
                                      in_=w["x_own"][tt * 128:(tt + 1) * 128, :])
                    pb.inc(ins, f"a7x{tt % 2}", 16)
                for tt in range(NT - 2, NT):
                    cV.wait(e, add_(tt, 3))
                    ins = e.dma_start(out=dst_r[tt * 128:(tt + 1) * 128, :],
                                      in_=rox[:, (tt % 2) * D:(tt % 2 + 1) * D])
                    pb.inc(ins, f"a7o{tt % 2}", 16)
                e.wait_ge(pb.sem("a7o0"), d7["a7o0"] + 16 * 4)
                e.wait_ge(pb.sem("a7o1"), d7["a7o1"] + 16 * 4)

            @ph.on("tensor")
            def _(e):
                e.wait_ge(pb.sem("a7w"), d7["a7w"] + 16)
                for tt in range(NT):
                    for db in range(4):
                        g = tt * 4 + db
                        if g >= 2:
                            cV.wait(e, add_((g - 2) // 4, (g - 2) % 4))
                        for kt in range(16):
                            ins = e.matmul(ps7[:, (g % 2) * 512:(g % 2) * 512 + 512],
                                           o_fm[:, kt * 1024 + tt * 128:kt * 1024 + tt * 128 + 128],
                                           wo_sb[:, kt * 2048 + db * 512:kt * 2048 + db * 512 + 512],
                                           start=(kt == 0), stop=(kt == 15))
                        cT.tick(ins)

            @ph.on("vector")
            def _(e):
                for tt in range(NT):
                    for db in range(4):
                        g = tt * 4 + db
                        cT.wait(e, mmd(g))
                        e.wait_ge(pb.sem(f"a7x{tt % 2}"), d7[f"a7x{tt % 2}"] + 16 * (tt // 2 + 1))
                        if tt >= 2 and db == 0:
                            e.wait_ge(pb.sem(f"a7o{tt % 2}"), d7[f"a7o{tt % 2}"] + 16 * (tt // 2))
                        cV.guard(e)
                        ins = e.tensor_add(rox[:, (tt % 2) * D + db * 512:(tt % 2) * D + db * 512 + 512],
                                           rtx[:, (tt % 2) * D + db * 512:(tt % 2) * D + db * 512 + 512],
                                           ps7[:, (g % 2) * 512:(g % 2) * 512 + 512])
                        cV.tick(ins)

            pb.emit(ph)
            cT.check(32)
            cV.check(32)

    ar.freev("wo_sb", "rtx", "rox", "o_fm")
    ar.close()


B_EPS = 1e-5
CH = 128
NCH = 8


def _b_dram(nc, pairs=4):
    d = dict(
        wq=nc.dram_tensor("bwq_img", [4, 128, 16 * 512], F32, kind="ExternalInput").ap(),
        wk=nc.dram_tensor("bwk_img", [4, 128, 16 * 512], F32, kind="ExternalInput").ap(),
        wv=nc.dram_tensor("bwv_img", [4, 128, 16 * 512], F32, kind="ExternalInput").ap(),
        wo=nc.dram_tensor("bwo_img", [128, 16 * 2048], F32, kind="ExternalInput").ap(),
        wg1=nc.dram_tensor("bwg1_img", [128, 16 * 128], F32, kind="ExternalInput").ap(),
        wg2=nc.dram_tensor("bwg2_img", [128, 2048], F32, kind="ExternalInput").ap(),
        nw=nc.dram_tensor("bnw_img", [128, 16], F32, kind="ExternalInput").ap(),
        ldr=nc.dram_tensor("bld_img", [1, 16], F32, kind="ExternalInput").ap(),
        diffX=nc.dram_tensor("bdiff_img", [128, 128], F32, kind="ExternalInput").ap(),
        iota1=nc.dram_tensor("biota_img", [128, 128], F32, kind="ExternalInput").ap(),
        revj=nc.dram_tensor("brevj_img", [128, 1], F32, kind="ExternalInput").ap(),
        csteps=nc.dram_tensor("bcst_img", [128, 8], F32, kind="ExternalInput").ap(),
        flag=nc.dram_tensor("bflag_img", [128, 1], F32, kind="ExternalInput").ap(),
        cc_in=nc.dram_tensor("bcc_in", [128, 2048], F32).ap(),
        cc_out=nc.dram_tensor("bcc_out", [256, 2048], F32).ap(),
    )
    d["pairs"] = [[2 * i, 2 * i + 1] for i in range(pairs)]
    return d


def _stage_attn_b(inputs, core):
    hf = core % 2
    m = {}
    for nm, key in (("bwq_img", "L0_attn_b_Wq"), ("bwk_img", "L0_attn_b_Wk"),
                    ("bwv_img", "L0_attn_b_Wv")):
        W = inputs[key]
        m[nm] = np.ascontiguousarray(
            W.reshape(16, 128, 4, 512).transpose(2, 1, 0, 3).reshape(4, 128, 16 * 512))
    m["bwo_img"] = np.ascontiguousarray(
        inputs["L0_attn_b_Wo"].reshape(16, 128, 2048).transpose(1, 0, 2).reshape(128, 16 * 2048))
    m["bwg1_img"] = np.ascontiguousarray(
        inputs["L0_attn_b_Wg1"].reshape(16, 128, 128).transpose(1, 0, 2).reshape(128, 2048))
    m["bwg2_img"] = np.ascontiguousarray(inputs["L0_attn_b_Wg2"])
    m["bnw_img"] = np.ascontiguousarray(
        inputs["L0_attn_b_norm_w"].reshape(16, 128).transpose(1, 0))
    m["bld_img"] = np.ascontiguousarray(inputs["L0_attn_b_log_decay"].reshape(1, 16))
    i_ = np.arange(128)
    dif = i_[None, :] - i_[:, None]
    m["bdiff_img"] = np.where(dif >= 0, dif, 1e30).astype(np.float32)
    m["biota_img"] = np.broadcast_to((i_ + 1).astype(np.float32)[None, :], (128, 128)).copy()
    m["brevj_img"] = (127.0 - i_).astype(np.float32).reshape(128, 1).copy()
    m["bcst_img"] = np.broadcast_to((128.0 * np.arange(8)).astype(np.float32)[None, :],
                                    (128, 8)).copy()
    m["bflag_img"] = np.full((128, 1), float(hf), np.float32)
    return m


def emit_attn_b(pb, cn, w, src_r, dst_r):
    nc = pb.nc
    ar = Arena(nc, "bb", 87040, 9216)
    ident = cn["ident"]
    ones_bf = cn["ones_bf"]
    ones_f = cn["ones_f"]

    x_fm = ar.alloc("x_fm", 16 * 1024)
    wpan = ar.alloc("wpan", 2 * 8192)
    wg1s = ar.alloc("wg1s", 2048)
    xln = ar.alloc("xln", 2 * D)
    rt = ar.alloc("rt", 3 * D, F32)
    tpsg = nc.psum_tensor("bb_tps", [128, 8 * 1024], BF16)
    tps = tpsg.__enter__()
    with nc.sbuf_tensor("bln_st", [128, 16], F32) as stt_:
        sb = dict(rt=rt, st=stt_, xln=xln, tps=tps, ident=ident)
        ph = Phase("c1")
        d2 = pb.bases("c2d0", "c2d1", "c2g")

        @ph.on("gpsimd")
        def _(e):
            pb.inc(e.dma_start(out=wg1s[:, :], in_=w["wg1"]), "c2g", 16)
            pb.inc(e.dma_start(out=wpan[:, 0:8192], in_=w["wq"][0]), "c2d0", 16)

        tiles = [src_r[i * 128:(i + 1) * 128, :] for i in range(NT)]
        emit_ln_fm(pb, ph, "bln", tiles, x_fm, 0, T, sb)
        pb.emit(ph)
        for f in ph.fns.get("_check", []):
            f()
    tpsg.__exit__(None, None, None)
    ar.freev("xln", "rt")

    q_tm = ar.alloc("q_tm", 16384)
    k_tm = ar.alloc("k_tm", 16384)
    v_tm = ar.alloc("v_tm", 16384)
    g1 = ar.alloc("g1", 1024)
    with nc.psum_tensor("c2_ps", [128, 2048], F32) as ps2:
        ph = Phase("c2")
        cT = Clk(pb, "c2T")
        cS = Clk(pb, "c2S")
        tms = [q_tm, k_tm, v_tm]

        def mmd(g): return cT.base + g + 1
        def evd(g): return cS.base + g + 1

        @ph.on("gpsimd")
        def _(e):
            for pi, wkey in enumerate(("wq", "wk", "wv")):
                for fb in range(4):
                    u = pi * 4 + fb
                    if u == 0:
                        continue
                    if u >= 2:
                        cT.wait(e, mmd((u - 1) * 8))
                    ins = e.dma_start(out=wpan[:, (u % 2) * 8192:(u % 2 + 1) * 8192],
                                      in_=w[wkey][fb])
                    pb.inc(ins, f"c2d{u % 2}", 16)

        @ph.on("tensor")
        def _(e):
            for pi in range(3):
                for fb in range(4):
                    u = pi * 4 + fb
                    e.wait_ge(pb.sem(f"c2d{u % 2}"), d2[f"c2d{u % 2}"] + 16 * (u // 2 + 1))
                    for tt in range(NT):
                        g = u * 8 + tt
                        if g >= 4:
                            cS.wait(e, evd(g - 4))
                        for kt in range(16):
                            ins = e.matmul(
                                ps2[:, (g % 4) * 512:(g % 4) * 512 + 512],
                                x_fm[:, kt * T + tt * 128:kt * T + tt * 128 + 128],
                                wpan[:, (u % 2) * 8192 + kt * 512:(u % 2) * 8192 + (kt + 1) * 512],
                                start=(kt == 0), stop=(kt == 15))
                        cT.tick(ins)
            e.wait_ge(pb.sem("c2g"), d2["c2g"] + 16)
            for tb in range(2):
                g = 96 + tb
                cS.wait(e, evd(g - 4))
                for kt in range(16):
                    ins = e.matmul(ps2[:, (g % 4) * 512:(g % 4) * 512 + 512],
                                   wg1s[:, kt * 128:(kt + 1) * 128],
                                   x_fm[:, kt * T + tb * 512:kt * T + tb * 512 + 512],
                                   start=(kt == 0), stop=(kt == 15))
                cT.tick(ins)

        @ph.on("scalar")
        def _(e):
            for g in range(96):
                u, tt = g // 8, g % 8
                pi, fb = u // 4, u % 4
                cT.wait(e, mmd(g))
                cS.guard(e)
                ins = e.activation(tms[pi][:, tt * 2048 + fb * 512:tt * 2048 + fb * 512 + 512],
                                   ps2[:, (g % 4) * 512:(g % 4) * 512 + 512], AF.Silu)
                cS.tick(ins)
            for tb in range(2):
                g = 96 + tb
                cT.wait(e, mmd(g))
                cS.guard(e)
                cS.tick(e.copy(g1[:, tb * 512:tb * 512 + 512],
                               ps2[:, (g % 4) * 512:(g % 4) * 512 + 512]))

        pb.emit(ph)
        cT.check(98)
        cS.check(98)
    ar.freev("x_fm", "wpan", "wg1s")

    q_fm = ar.alloc("q_fm", 16384)
    k_fm = ar.alloc("k_fm", 16384)
    with nc.psum_tensor("c3_tp", [128, 4 * 1024], BF16) as tps3:
        ph = Phase("c3")
        cT = Clk(pb, "c3T")
        cV = Clk(pb, "c3V")

        def tpd(j): return cT.base + j + 1
        def evd(j): return cV.base + j + 1

        @ph.on("tensor")
        def _(e):
            for j in range(256):
                pi = j // 128
                c = (j % 128) // 16
                h = j % 16
                if j >= 4:
                    cV.wait(e, evd(j - 4))
                src = (q_tm if pi == 0 else k_tm)
                ins = e.transpose(tps3[:, (j % 4) * 1024:(j % 4) * 1024 + 128],
                                  src[:, c * 2048 + h * 128:c * 2048 + (h + 1) * 128],
                                  ident[:, :])
                cT.tick(ins)

        @ph.on("vector")
        def _(e):
            for j in range(256):
                pi = j // 128
                c = (j % 128) // 16
                h = j % 16
                cT.wait(e, tpd(j))
                dst = (q_fm if pi == 0 else k_fm)
                cV.guard(e)
                cV.tick(e.tensor_copy(dst[:, h * 1024 + c * 128:h * 1024 + (c + 1) * 128],
                                      tps3[:, (j % 4) * 1024:(j % 4) * 1024 + 128]))

        pb.emit(ph)
        cT.check(256)
        cV.check(256)
    ar.freev("q_tm")

    ldb = ar.alloc("ldb", 16, F32)
    DTt = ar.alloc("DTt", 2048, F32)
    qdecb = ar.alloc("qdecb", 2048, F32)
    kdecs = ar.alloc("kdecs", 16, F32)
    cdpow = ar.alloc("cdpow", 128, F32)
    corrpow = ar.alloc("corrpow", 128, F32)
    diffX = ar.alloc("diffX", 128, F32)
    iota1 = ar.alloc("iota1", 128, F32)
    revj = ar.alloc("revj", 1, F32)
    csteps = ar.alloc("csteps", 8, F32)
    flag = ar.alloc("flag", 1, F32)
    ldrow = ar.alloc("ldrow", 16, F32)
    with nc.psum_tensor("c4_ps", [128, 16], F32) as ps4:
        ph = Phase("c4")
        cT = Clk(pb, "c4T")
        cS = Clk(pb, "c4S")
        cV = Clk(pb, "c4V")
        d4 = pb.bases("c4d")

        @ph.on("sync")
        def _(e):
            for dst, src in ((diffX, w["diffX"]), (iota1, w["iota1"]), (revj, w["revj"]),
                             (csteps, w["csteps"]), (flag, w["flag"])):
                pb.inc(e.dma_start(out=dst[:, :], in_=src), "c4d", 16)
            pb.inc(e.dma_start(out=ldrow[0:1, 0:16], in_=w["ldr"]), "c4d", 16)

        @ph.on("tensor")
        def _(e):
            e.wait_ge(pb.sem("c4d"), d4["c4d"] + 96)
            ins = e.matmul(ps4[:, 0:16], ones_f[:], ldrow[0:1, 0:16], start=True, stop=True)
            cT.tick(ins)

        @ph.on("vector")
        def _(e):
            cT.wait(e, cT.base + 1)
            cV.guard(e)
            cV.tick(e.tensor_copy(ldb[:, :], ps4[:, 0:16]))
            cV.guard(e)
            cV.tick(e.tensor_scalar_mul(kdecs[:, :], ldb[:, :], revj[:, 0:1]))

        @ph.on("scalar")
        def _(e):
            cV.wait(e, cV.base + 2)
            for h in range(16):
                cS.guard(e)
                cS.tick(e.activation(DTt[:, h * 128:(h + 1) * 128], diffX[:, :], AF.Exp,
                                     scale=ldb[:, h:h + 1]))
                cS.guard(e)
                cS.tick(e.activation(qdecb[:, h * 128:(h + 1) * 128], iota1[:, :], AF.Exp,
                                     scale=ldb[:, h:h + 1]))
                cS.guard(e)
                cS.tick(e.activation(cdpow[:, h * 8:(h + 1) * 8], csteps[:, :], AF.Exp,
                                     scale=ldb[:, h:h + 1]))
            cS.guard(e)
            cS.tick(e.activation(kdecs[:, :], kdecs[:, :], AF.Exp))

        @ph.on("vector")
        def _(e):
            cS.wait(e, cS.base + 49)
            cV.guard(e)
            cV.tick(e.tensor_scalar_mul(corrpow[:, :], cdpow[:, :], flag[:, 0:1]))

        pb.emit(ph)
        cT.check(1)
        cS.check(49)
        cV.check(3)

    o_b = ar.alloc("o_b", 16384)
    S_f = ar.alloc("S_f", 2048, F32)
    S_bf = ar.alloc("S_bf", 2048)
    ptm = ar.alloc("ptm", 256)
    vkb = ar.alloc("vkb", 256)
    otmp = ar.alloc("otmp", 256, F32)
    with (
        nc.psum_tensor("c5_pp", [128, 1024], F32) as pp,
        nc.psum_tensor("c5_po", [128, 1024], F32) as po,
        nc.psum_tensor("c5_p2", [128, 1024], F32) as p2,
        nc.psum_tensor("c5_psd", [128, 1024], F32) as psd,
    ):
        ph = Phase("c5")
        cT = Clk(pb, "c5T")
        cV = Clk(pb, "c5V")
        cZ = Clk(pb, "c5Z")

        Tseq, Vseq = [], []
        for n in range(128):
            Tseq.append(("pt", n))
            if n >= 1:
                Tseq += [("in", n - 1), ("ie", n - 1), ("sa", n - 1)]
            Vseq += [("vk", n), ("pm", n)]
            if n >= 1:
                Vseq += [("om", n - 1), ("oa", n - 1), ("su", n - 1), ("sb", n - 1)]
        Tseq += [("in", 127), ("ie", 127), ("sa", 127)]
        Vseq += [("om", 127), ("oa", 127), ("su", 127), ("sb", 127)]
        tmk = {k: i + 1 for i, k in enumerate(Tseq)}
        vmk = {k: i + 1 for i, k in enumerate(Vseq)}

        def tmark(*k): return cT.base + tmk[k]
        def vmark(*k): return cV.base + vmk[k]

        @ph.on("gpsimd")
        def _(e):
            cZ.tick(e.memset(S_f[:, :], 0.0))
            cZ.tick(e.memset(S_bf[:, :], 0.0))

        @ph.on("tensor")
        def _(e):
            for it in Tseq:
                kind, m = it
                c, h = m // 16, m % 16
                sl = (m % 2) * 512
                qs = q_fm[:, h * 1024 + c * 128:h * 1024 + (c + 1) * 128]
                ks_f = k_fm[:, h * 1024 + c * 128:h * 1024 + (c + 1) * 128]
                ks_t = k_tm[:, c * 2048 + h * 128:c * 2048 + (h + 1) * 128]
                vs_t = v_tm[:, c * 2048 + h * 128:c * 2048 + (h + 1) * 128]
                if kind == "pt":
                    if m >= 2:
                        cV.wait(e, vmark("pm", m - 2))
                    cT.tick(e.matmul(pp[:, sl:sl + 128], ks_f, qs, start=True, stop=True))
                elif kind == "in":
                    cV.wait(e, vmark("pm", m))
                    if m >= 2:
                        cV.wait(e, vmark("oa", m - 2))
                    cT.tick(e.matmul(po[:, sl:sl + 128], vs_t, ptm[:, sl // 4:sl // 4 + 128],
                                     start=True, stop=True))
                elif kind == "ie":
                    if m >= 2:
                        cV.wait(e, vmark("om", m - 2))
                    if c == 0:
                        cZ.wait(e, cZ.base + 2)
                    else:
                        cV.wait(e, vmark("sb", (c - 1) * 16 + h))
                    cT.tick(e.matmul(p2[:, sl:sl + 128], S_bf[:, h * 128:(h + 1) * 128], qs,
                                     start=True, stop=True))
                else:
                    cV.wait(e, vmark("vk", m))
                    if m >= 2:
                        cV.wait(e, vmark("su", m - 2))
                    cT.tick(e.matmul(psd[:, sl:sl + 128], ks_t, vkb[:, sl // 4:sl // 4 + 128],
                                     start=True, stop=True))

        @ph.on("vector")
        def _(e):
            for it in Vseq:
                kind, m = it
                c, h = m // 16, m % 16
                sl = (m % 2) * 512
                sv = (m % 2) * 128
                vs_t = v_tm[:, c * 2048 + h * 128:c * 2048 + (h + 1) * 128]
                if kind == "vk":
                    if m >= 2:
                        cT.wait(e, tmark("sa", m - 2))
                    cV.guard(e)
                    cV.tick(e.tensor_scalar_mul(vkb[:, sv:sv + 128], vs_t, kdecs[:, h:h + 1]))
                elif kind == "pm":
                    cT.wait(e, tmark("pt", m))
                    if m >= 2:
                        cT.wait(e, tmark("in", m - 2))
                    cV.guard(e)
                    cV.tick(e.tensor_mul(ptm[:, sv:sv + 128], pp[:, sl:sl + 128],
                                         DTt[:, h * 128:(h + 1) * 128]))
                elif kind == "om":
                    cT.wait(e, tmark("ie", m))
                    cV.guard(e)
                    cV.tick(e.tensor_mul(otmp[:, sv:sv + 128], p2[:, sl:sl + 128],
                                         qdecb[:, h * 128:(h + 1) * 128]))
                elif kind == "oa":
                    cT.wait(e, tmark("in", m))
                    cV.guard(e)
                    cV.tick(e.tensor_add(o_b[:, h * 1024 + c * 128:h * 1024 + (c + 1) * 128],
                                         otmp[:, sv:sv + 128], po[:, sl:sl + 128]))
                elif kind == "su":
                    cT.wait(e, tmark("sa", m))
                    if c == 0:
                        cZ.wait(e, cZ.base + 2)
                    cV.guard(e)
                    cV.tick(e.scalar_tensor_tensor(S_f[:, h * 128:(h + 1) * 128],
                                                   S_f[:, h * 128:(h + 1) * 128],
                                                   cdpow[:, h * 8 + 1:h * 8 + 2],
                                                   psd[:, sl:sl + 128], ALU.mult, ALU.add))
                else:
                    cV.guard(e)
                    cV.tick(e.tensor_copy(S_bf[:, h * 128:(h + 1) * 128],
                                          S_f[:, h * 128:(h + 1) * 128]))

        pb.emit(ph)
        cT.check(len(Tseq))
        cV.check(len(Vseq))
        cZ.check(2)

    ar.freev("k_tm", "k_fm", "ptm", "vkb", "otmp")
    sg = ar.alloc("sg", 16384)
    S0b = ar.alloc("S0b", 2048)
    S0f = ar.alloc("S0f", 2048, F32)
    ctmp = ar.alloc("ctmp", 256, F32)
    wg2s = ar.alloc("wg2s", 2048)
    with (
        nc.psum_tensor("c6_ps", [128, 1024], F32) as ps6,
        nc.psum_tensor("c6_pc", [128, 1024], F32) as pc6,
    ):
        ph = Phase("c6")
        cT = Clk(pb, "c6T")
        cS = Clk(pb, "c6S")
        cV = Clk(pb, "c6V")
        d6 = pb.bases("c6d", "c6cc", "c6g", "c6s0")

        def gmm(g): return cT.base + g + 1
        def cmm(n): return cT.base + 32 + n + 1
        def sgd(g): return cS.base + g + 1
        def s0d(): return cV.base + 1
        def com(n): return cV.base + 1 + 2 * n + 1
        def coa(n): return cV.base + 1 + 2 * n + 2

        @ph.on("sync")
        def _(e):
            pb.inc(e.dma_start(out=w["cc_in"], in_=S_f[:, :]), "c6d", 16)

        @ph.on("gpsimd")
        def _(e):
            pb.inc(e.dma_start(out=wg2s[:, :], in_=w["wg2"]), "c6g", 16)
            e.wait_ge(pb.sem("c6d"), d6["c6d"] + 16)
            ins = e.collective_compute(
                "AllGather", ALU.bypass, replica_groups=w["pairs"],
                ins=[w["cc_in"]], outs=[w["cc_out"]])
            pb.inc(ins, "c6cc", 1)
            e.wait_ge(pb.sem("c6cc"), d6["c6cc"] + 1)
            ins = e.dma_start(out=S0f[:, :], in_=w["cc_out"][0:128, :])
            pb.inc(ins, "c6s0", 16)

        @ph.on("tensor")
        def _(e):
            e.wait_ge(pb.sem("c6g"), d6["c6g"] + 16)
            for ft in range(16):
                for tb in range(2):
                    g = ft * 2 + tb
                    if g >= 2:
                        cS.wait(e, sgd(g - 2))
                    ins = e.matmul(ps6[:, (g % 2) * 512:(g % 2) * 512 + 512],
                                   wg2s[:, ft * 128:(ft + 1) * 128],
                                   g1[:, tb * 512:tb * 512 + 512],
                                   start=True, stop=True)
                    cT.tick(ins)
            cV.wait(e, s0d())
            for c in range(NCH):
                for h in range(H):
                    n = c * 16 + h
                    if n >= 2:
                        cV.wait(e, com(n - 2))
                    ins = e.matmul(pc6[:, (n % 2) * 512:(n % 2) * 512 + 128],
                                   S0b[:, h * 128:(h + 1) * 128],
                                   q_fm[:, h * 1024 + c * 128:h * 1024 + (c + 1) * 128],
                                   start=True, stop=True)
                    cT.tick(ins)

        @ph.on("scalar")
        def _(e):
            for g in range(32):
                ft, tb = g // 2, g % 2
                cT.wait(e, gmm(g))
                cS.guard(e)
                ins = e.activation(sg[:, ft * 1024 + tb * 512:ft * 1024 + tb * 512 + 512],
                                   ps6[:, (g % 2) * 512:(g % 2) * 512 + 512], AF.Sigmoid)
                cS.tick(ins)

        @ph.on("vector")
        def _(e):
            e.wait_ge(pb.sem("c6s0"), d6["c6s0"] + 16)
            cV.guard(e)
            cV.tick(e.tensor_copy(S0b[:, :], S0f[:, :]))
            for c in range(NCH):
                for h in range(H):
                    n = c * 16 + h
                    sl = (n % 2) * 512
                    sv = (n % 2) * 128
                    cT.wait(e, cmm(n))
                    cV.guard(e)
                    ins = e.scalar_tensor_tensor(ctmp[:, sv:sv + 128],
                                                 pc6[:, sl:sl + 128],
                                                 corrpow[:, h * 8 + c:h * 8 + c + 1],
                                                 qdecb[:, h * 128:(h + 1) * 128],
                                                 ALU.mult, ALU.mult)
                    cV.tick(ins)
                    ob = o_b[:, h * 1024 + c * 128:h * 1024 + (c + 1) * 128]
                    cV.guard(e)
                    cV.tick(e.tensor_add(ob, ob, ctmp[:, sv:sv + 128]))

        pb.emit(ph)
        cT.check(32 + 128)
        cS.check(32)
        cV.check(1 + 256)

    ar.freev("q_fm", "v_tm", "S_f", "S_bf", "S0b", "S0f", "ctmp", "wg2s",
             "DTt", "qdecb", "kdecs", "cdpow", "corrpow", "diffX", "iota1", "revj",
             "csteps", "flag", "ldrow", "ldb", "g1")
    sqb = ar.alloc("sqb", 1024)
    wo_sb = ar.alloc("wo_sb", 16 * 2048)
    rstdb = ar.alloc("rstdb", 1024, F32)
    with (
        nc.sbuf_tensor("c7_nw", [128, 16], F32) as nw,
        nc.psum_tensor("c7_ps", [128, 1024], F32) as ps7,
        nc.psum_tensor("c7_pb", [128, 1024], F32) as pb7,
    ):
        ph = Phase("c7")
        cT = Clk(pb, "c7T")
        cS = Clk(pb, "c7S")
        cV = Clk(pb, "c7V")
        d7 = pb.bases("c7d")

        @ph.on("gpsimd")
        def _(e):
            pb.inc(e.dma_start(out=wo_sb[:, :], in_=w["wo"]), "c8w", 16)

        def t_ss(h, tb): return cT.base + h * 4 + tb + 1
        def t_bc(h, tb): return cT.base + h * 4 + 2 + tb + 1
        def s_sq(h): return cS.base + h * 3 + 1
        def s_rs(h): return cS.base + h * 3 + 3
        def v_og(h, tb): return cV.base + h * 5 + tb + 1
        def v_eps(h): return cV.base + h * 5 + 3
        def v_on(h, tb): return cV.base + h * 5 + 3 + tb + 1

        @ph.on("sync")
        def _(e):
            pb.inc(e.dma_start(out=nw[:], in_=w["nw"]), "c7d", 16)

        @ph.on("tensor")
        def _(e):
            for h in range(H):
                for tb in range(2):
                    cS.wait(e, s_sq(h))
                    if h >= 1:
                        cV.wait(e, v_on(h - 1, 1))
                    ins = e.matmul(ps7[0:1, tb * 512:tb * 512 + 512], ones_bf[:],
                                   sqb[:, tb * 512:tb * 512 + 512], start=True, stop=True)
                    cT.tick(ins)
                cS.wait(e, s_rs(h))
                for tb in range(2):
                    ins = e.matmul(pb7[:, tb * 512:tb * 512 + 512], ones_f[:],
                                   rstdb[0:1, tb * 512:tb * 512 + 512], start=True, stop=True)
                    cT.tick(ins)

        @ph.on("scalar")
        def _(e):
            for h in range(H):
                cV.wait(e, v_og(h, 1))
                if h >= 1:
                    cT.wait(e, t_ss(h - 1, 1))
                cS.guard(e)
                cS.tick(e.square(sqb[:, :], o_b[:, h * 1024:(h + 1) * 1024]))
                cV.wait(e, v_eps(h))
                cS.guard(e)
                cS.tick(e.activation(rstdb[0:1, 0:1024], rstdb[0:1, 0:1024], AF.Ln))
                cS.guard(e)
                cS.tick(e.activation(rstdb[0:1, 0:1024], rstdb[0:1, 0:1024], AF.Exp, scale=-0.5))

        @ph.on("vector")
        def _(e):
            e.wait_ge(pb.sem("c7d"), d7["c7d"] + 16)
            for h in range(H):
                for tb in range(2):
                    cV.guard(e)
                    ob = o_b[:, h * 1024 + tb * 512:h * 1024 + tb * 512 + 512]
                    cV.tick(e.tensor_mul(ob, ob, sg[:, h * 1024 + tb * 512:h * 1024 + tb * 512 + 512]))
                cT.wait(e, t_ss(h, 1))
                cV.guard(e)
                cV.tick(e.tensor_scalar(rstdb[0:1, 0:1024], ps7[0:1, 0:1024],
                                        1.0 / 128.0, B_EPS, ALU.mult, ALU.add))
                for tb in range(2):
                    cT.wait(e, t_bc(h, tb))
                    cV.guard(e)
                    ob = o_b[:, h * 1024 + tb * 512:h * 1024 + tb * 512 + 512]
                    ins = e.scalar_tensor_tensor(ob, ob, nw[:, h:h + 1],
                                                 pb7[:, tb * 512:tb * 512 + 512],
                                                 ALU.mult, ALU.mult)
                    cV.tick(ins)

        pb.emit(ph)
        cT.check(H * 4)
        cS.check(H * 3)
        cV.check(H * 5)

    ar.freev("sg", "sqb", "rstdb")
    rtx = ar.alloc("rtx", 2 * D, F32)
    rox = ar.alloc("rox", 2 * D, F32)
    with nc.psum_tensor("c8_ps", [128, 1024], F32) as ps8:
        ph = Phase("c8")
        cT = Clk(pb, "c8T")
        cV = Clk(pb, "c8V")
        d8 = pb.bases("c8w", "c8x0", "c8x1", "c8o0", "c8o1")

        def mmd(g): return cT.base + g + 1
        def add_(tt, db): return cV.base + tt * 4 + db + 1

        @ph.on("sync")
        def _(e):
            for tt in range(NT):
                if tt >= 2:
                    cV.wait(e, add_(tt - 2, 3))
                    ins = e.dma_start(out=dst_r[(tt - 2) * 128:(tt - 1) * 128, :],
                                      in_=rox[:, (tt % 2) * D:(tt % 2 + 1) * D])
                    pb.inc(ins, f"c8o{tt % 2}", 16)
                ins = e.dma_start(out=rtx[:, (tt % 2) * D:(tt % 2 + 1) * D],
                                  in_=src_r[tt * 128:(tt + 1) * 128, :])
                pb.inc(ins, f"c8x{tt % 2}", 16)
            for tt in range(NT - 2, NT):
                cV.wait(e, add_(tt, 3))
                ins = e.dma_start(out=dst_r[tt * 128:(tt + 1) * 128, :],
                                  in_=rox[:, (tt % 2) * D:(tt % 2 + 1) * D])
                pb.inc(ins, f"c8o{tt % 2}", 16)
            e.wait_ge(pb.sem("c8o0"), d8["c8o0"] + 16 * 4)
            e.wait_ge(pb.sem("c8o1"), d8["c8o1"] + 16 * 4)

        @ph.on("tensor")
        def _(e):
            e.wait_ge(pb.sem("c8w"), d8["c8w"])
            for tt in range(NT):
                for db in range(4):
                    g = tt * 4 + db
                    if g >= 2:
                        cV.wait(e, add_((g - 2) // 4, (g - 2) % 4))
                    for kt in range(16):
                        ins = e.matmul(ps8[:, (g % 2) * 512:(g % 2) * 512 + 512],
                                       o_b[:, kt * 1024 + tt * 128:kt * 1024 + tt * 128 + 128],
                                       wo_sb[:, kt * 2048 + db * 512:kt * 2048 + db * 512 + 512],
                                       start=(kt == 0), stop=(kt == 15))
                    cT.tick(ins)

        @ph.on("vector")
        def _(e):
            for tt in range(NT):
                for db in range(4):
                    g = tt * 4 + db
                    cT.wait(e, mmd(g))
                    e.wait_ge(pb.sem(f"c8x{tt % 2}"), d8[f"c8x{tt % 2}"] + 16 * (tt // 2 + 1))
                    if tt >= 2 and db == 0:
                        e.wait_ge(pb.sem(f"c8o{tt % 2}"), d8[f"c8o{tt % 2}"] + 16 * (tt // 2))
                    cV.guard(e)
                    ins = e.tensor_add(rox[:, (tt % 2) * D + db * 512:(tt % 2) * D + db * 512 + 512],
                                       rtx[:, (tt % 2) * D + db * 512:(tt % 2) * D + db * 512 + 512],
                                       ps8[:, (g % 2) * 512:(g % 2) * 512 + 512])
                    cV.tick(ins)

        pb.emit(ph)
        cT.check(32)
        cV.check(32)

    ar.freev("o_b", "wo_sb", "rtx", "rox")
    ar.close()


# version 19 (final)
# speedup vs baseline: 1.0672x; 1.0454x over previous
import numpy as np
import ml_dtypes
import concourse.bass as bass
import concourse.mybir as mybir
from concourse.bass_utils import run_bass_kernel_spmd

F32 = mybir.dt.float32
BF16 = mybir.dt.bfloat16
AF = mybir.ActivationFunctionType
ALU = mybir.AluOpType
AX = mybir.AxisListType

D = 2048
T = 1024
CTX = 2048
NT = T // 128
NCTX = CTX // 128
ND = D // 128
H = 16
LN_EPS = 1e-5
BIG_NEG = -1e30
INTER = 8192
NJC = INTER // 128


class Phase:

    def __init__(self, name):
        self.name = name
        self.fns = {}

    def on(self, engine):
        def deco(fn):
            self.fns.setdefault(engine, []).append(fn)
            return fn
        return deco


_SEM_TABLE = {
    "c_dma": "hx", "a1w": "hx", "a1wg": "gw", "a2w": "gw", "a3k": "hx",
    "a3p": "hx", "a3cc": "ccs",
    "a4w": "gx", "a5w": "gy", "a6w": "gw", "a6m0": "hd0", "a6m1": "hd1",
    "a7w": "gw", "a7x0": "hd0", "a7x1": "hd1", "a7o0": "ho0", "a7o1": "ho1",
    "c2d0": "gs0", "c2d1": "gs1", "c2g": "gw", "c4d": "hx", "c6d": "hx",
    "c6cc": "ccs", "c6g": "gw", "c6s0": "gs0", "c7d": "hx", "c8w": "gw",
    "c8x0": "hd0", "c8x1": "hd1", "c8o0": "ho0", "c8o1": "ho1", "adbg": "hx",
}
_SEM_SUFFIX = [
    ("lnd0", "hd0"), ("lnd1", "hd1"), ("lnd2", "hd2"), ("2d", "gw"), ("3d0", "gs0"), ("3d1", "gs1"),
    ("4u0", "gs0"), ("4u1", "gs1"), ("4d0", "gs2"), ("4d1", "gs3"),
    ("5d0", "hd0"), ("5d1", "hd1"), ("5o0", "ho0"), ("5o1", "ho1"), ("5g", "hx"),
]


def _canon(name):
    if name in _SEM_TABLE:
        return _SEM_TABLE[name]
    for suf, pool in _SEM_SUFFIX:
        if name.endswith(suf):
            return pool
    if name[-1] in "TSVZ":
        return "ck" + name[-1]
    raise KeyError(f"unpooled sem name {name}")


class PB:

    def __init__(self, nc):
        self.nc = nc
        self.sems = {}
        self.cnt = {}

    def sem(self, name):
        k = _canon(name)
        if k not in self.sems:
            self.sems[k] = self.nc.alloc_semaphore(name=k)
            self.cnt[k] = 0
        return self.sems[k]

    def getcnt(self, name):
        self.sem(name)
        return self.cnt[_canon(name)]

    def inc(self, ins, name, n=1):
        ins.then_inc(self.sem(name), n)
        self.cnt[_canon(name)] += n
        return self.cnt[_canon(name)]

    def cw(self, e, key):
        e.wait_ge(self.sem(key), self.getcnt(key))

    def ci(self, ins, key):
        self.inc(ins, key)

    def nopinc(self, e, name, n=1):
        self.inc(e.nop(), name, n)

    def bases(self, *names):
        return {n: self.getcnt(n) for n in names}

    def emit(self, phase):
        with self.nc.Block() as block:
            for eng in ("sync", "gpsimd", "tensor", "scalar", "vector"):
                fns = phase.fns.get(eng)
                if not fns:
                    continue

                def run(e, fns=fns):
                    for f in fns:
                        f(e)
                getattr(block, eng)(run)



class Clk:
    def __init__(self, pb, name):
        self.pb = pb
        self.name = name
        pb.sem(name)
        self.base = pb.getcnt(name)

    def tick(self, ins):
        return self.pb.inc(ins, self.name)

    def guard(self, e):
        e.wait_ge(self.pb.sem(self.name), self.pb.getcnt(self.name))

    def gtick(self, e, ins):
        return self.pb.inc(ins, self.name)

    def wait(self, e, mark):
        e.wait_ge(self.pb.sem(self.name), mark)

    def check(self, expect_total):
        got = self.pb.getcnt(self.name) - self.base
        assert got == expect_total, (self.name, got, expect_total)



def emit_ln_fm(pb, ph, tag, src_tiles, fm_dst, fm_col0, ntok_dst, sb, dsts=None):
    if dsts is None:
        dsts = [(fm_dst, fm_col0 + i * 128, ntok_dst) for i in range(len(src_tiles))]
    n = len(src_tiles)
    rt, st, xln, tps, ident = sb["rt"], sb["st"], sb["xln"], sb["tps"], sb["ident"]
    RSL = rt.n // D if hasattr(rt, "n") else 2
    cS = Clk(pb, f"{tag}S")
    cV = Clk(pb, f"{tag}V")
    cT = Clk(pb, f"{tag}T")
    dsl = [f"{tag}d{j}" for j in range(RSL)]
    dbase = pb.bases(*dsl)

    vseq = []
    for i in range(n):
        if i >= 1:
            vseq += [("ev", i - 1, 0), ("ev", i - 1, 1)]
        for k in range(8):
            vseq.append(("st", i, k))
    vseq += [("ev", n - 1, 0), ("ev", n - 1, 1)]
    vmk = {k: i + 1 for i, k in enumerate(vseq)}

    def sqd(i): return cS.base + 4 * i + 1
    def lnd(i): return cS.base + 4 * i + 2
    def expd(i): return cS.base + 4 * i + 3
    def xld(i): return cS.base + 4 * (i + 1)
    def varpd(i): return cV.base + vmk[("st", i, 5)]
    def negbd(i): return cV.base + vmk[("st", i, 7)]
    def evd(i, half): return cV.base + vmk[("ev", i, half)]
    def tpd(j): return cT.base + j + 1

    @ph.on("sync")
    def _(e):
        for i in range(n):
            if i >= RSL:
                cS.wait(e, xld(i - RSL + 1))
            ins = e.dma_start(out=rt[:, (i % RSL) * D:(i % RSL + 1) * D], in_=src_tiles[i])
            pb.inc(ins, dsl[i % RSL], 16)

    @ph.on("scalar")
    def _(e):
        for i in range(n):
            b = (i % 2) * D
            br = (i % RSL) * D
            c = i % 2
            e.wait_ge(pb.sem(dsl[i % RSL]), dbase[dsl[i % RSL]] + 16 * (i // RSL + 1))
            if i >= 2:
                cT.wait(e, cT.base + (i - 1) * ND)
            cS.guard(e)
            cS.tick(e.activation(xln[:, b:b + D], rt[:, br:br + D], AF.Square,
                                 accum_out=st[:, 1:2]))
            cV.wait(e, varpd(i))
            cS.guard(e)
            cS.tick(e.activation(st[:, 8:9], st[:, 6:7], AF.Ln))
            cS.guard(e)
            cS.tick(e.activation(st[:, 10 + c:11 + c], st[:, 8:9], AF.Exp, scale=-0.5))
            cV.wait(e, negbd(i))
            cS.guard(e)
            cS.tick(e.activation(xln[:, b:b + D], rt[:, br:br + D], AF.Identity,
                                 bias=st[:, 12 + c:13 + c], scale=st[:, 10 + c:11 + c]))

    @ph.on("vector")
    def _(e):
        def emit_ev(i, half):
            cT.wait(e, tpd(i * ND + 8 * half + 7))
            dbuf, c0, ntok = dsts[i]
            if hasattr(dbuf, "tcols"):
                dt_, doff, dcols = dbuf.t, dbuf.off, dbuf.tcols
            else:
                dt_, doff, dcols = dbuf, 0, dbuf.shape[1]
            src = bass.AP(tps, 0, [[8 * 1024, 128], [1024, 8], [1, 128]])
            dst = bass.AP(dt_, doff + (8 * half) * ntok + c0,
                          [[dcols, 128], [ntok, 8], [1, 128]])
            cV.guard(e)
            cV.tick(e.tensor_copy(dst, src))

        for i in range(n):
            if i >= 1:
                emit_ev(i - 1, 0)
                emit_ev(i - 1, 1)
            b = (i % 2) * D
            br = (i % RSL) * D
            c = i % 2
            e.wait_ge(pb.sem(dsl[i % RSL]), dbase[dsl[i % RSL]] + 16 * (i // RSL + 1))
            cV.guard(e)
            cV.tick(e.reduce_sum(st[:, 0:1], rt[:, br:br + D], AX.X))
            if i >= 1:
                cS.wait(e, lnd(i - 1))
            cV.guard(e)
            cV.tick(e.tensor_scalar_mul(st[:, 2:3], st[:, 0:1], 1.0 / D))
            cS.wait(e, sqd(i))
            cV.guard(e)
            cV.tick(e.tensor_scalar_mul(st[:, 3:4], st[:, 1:2], 1.0 / D))
            cV.guard(e)
            cV.tick(e.tensor_mul(st[:, 4:5], st[:, 2:3], st[:, 2:3]))
            cV.guard(e)
            cV.tick(e.tensor_sub(st[:, 5:6], st[:, 3:4], st[:, 4:5]))
            cV.guard(e)
            cV.tick(e.tensor_scalar_add(st[:, 6:7], st[:, 5:6], LN_EPS))
            cS.wait(e, expd(i))
            cV.guard(e)
            cV.tick(e.tensor_mul(st[:, 14:15], st[:, 2:3], st[:, 10 + c:11 + c]))
            cV.guard(e)
            cV.tick(e.tensor_scalar_mul(st[:, 12 + c:13 + c], st[:, 14:15], -1.0))
        emit_ev(n - 1, 0)
        emit_ev(n - 1, 1)

    @ph.on("tensor")
    def _(e):
        for i in range(n):
            b = (i % 2) * D
            cS.wait(e, xld(i))
            for dt in range(ND):
                j = i * ND + dt
                if j >= 8:
                    jj = j - 8
                    cV.wait(e, evd(jj // ND, (jj % ND) // 8))
                ins = e.transpose(tps[:, (j % 8) * 1024:(j % 8) * 1024 + 128],
                                  xln[:, b + dt * 128:b + (dt + 1) * 128], ident[:, :])
                cT.tick(ins)

    def final_check():
        cS.check(4 * n)
        cV.check(len(vseq))
        cT.check(ND * n)
    ph.fns.setdefault("_check", []).append(final_check)
    return dict(cS=cS, cV=cV, cT=cT, xld=xld)



def emit_ffn(pb, cn, w, src_r, dst_r, uid):
    nc = pb.nc
    u = f"f{uid}"
    with (
        nc.sbuf_tensor(f"{u}_xfm", [128, ND * T], BF16) as x_fm,
        nc.sbuf_tensor(f"{u}_yacc", [128, NT * D], F32) as y_acc,
        nc.sbuf_tensor(f"{u}_scale", [128, 8], F32) as scale_sb,
    ):
        with (
            nc.sbuf_tensor(f"{u}_rt", [128, 3 * D], F32) as rt,
            nc.sbuf_tensor(f"{u}_st", [128, 16], F32) as st,
            nc.sbuf_tensor(f"{u}_xln", [128, 2 * D], BF16) as xln,
            nc.psum_tensor(f"{u}_tps", [128, 8 * 1024], BF16) as tps,
        ):
            sb = dict(rt=rt, st=st, xln=xln, tps=tps, ident=cn["ident"])
            ph = Phase(f"{u}_ln")
            tiles = [src_r[i * 128:(i + 1) * 128, :] for i in range(NT)]
            emit_ln_fm(pb, ph, f"{u}ln", tiles, x_fm, 0, T, sb)
            pb.emit(ph)
            for f in ph.fns.get("_check", []):
                f()

        with (
            nc.sbuf_tensor(f"{u}_gsb", [128, ND * 8], BF16) as gsb,
            nc.sbuf_tensor(f"{u}_lg", [128, 8], F32) as lg,
            nc.sbuf_tensor(f"{u}_m", [128, 4], F32) as m,
            nc.sbuf_tensor(f"{u}_msk", [128, 8], F32) as msk,
            nc.psum_tensor(f"{u}_psl", [128, 1024], F32) as ps_l,
        ):
            ph = Phase(f"{u}_top2")
            cT = Clk(pb, f"{u}2T")
            cV = Clk(pb, f"{u}2V")
            dsem = pb.bases(f"{u}2d")

            def mmd(tt): return cT.base + tt + 1
            def scd(tt): return cV.base + 7 * (tt + 1)

            @ph.on("gpsimd")
            def _(e):
                pb.inc(e.dma_start(out=gsb[:], in_=w["gate"]), f"{u}2d", 16)

            @ph.on("tensor")
            def _(e):
                e.wait_ge(pb.sem(f"{u}2d"), dsem[f"{u}2d"] + 16)
                for tt in range(NT):
                    if tt >= 2:
                        cV.wait(e, scd(tt - 2) - 6)
                    for dt in range(ND):
                        ins = e.matmul(ps_l[:, (tt % 2) * 512:(tt % 2) * 512 + 8],
                                       x_fm[:, dt * T + tt * 128:dt * T + tt * 128 + 128],
                                       gsb[:, dt * 8:(dt + 1) * 8],
                                       start=(dt == 0), stop=(dt == ND - 1))
                    cT.tick(ins)

            @ph.on("vector")
            def _(e):
                for tt in range(NT):
                    cT.wait(e, mmd(tt))
                    cV.guard(e)
                    cV.tick(e.tensor_copy(lg[:], ps_l[:, (tt % 2) * 512:(tt % 2) * 512 + 8]))
                    cV.guard(e)
                    cV.tick(e.reduce_max(m[:, 0:1], lg[:], AX.X))
                    cV.guard(e)
                    cV.tick(e.tensor_scalar(msk[:], lg[:], m[:, 0:1], None, ALU.is_ge))
                    cV.guard(e)
                    cV.tick(e.scalar_tensor_tensor(msk[:], msk[:], BIG_NEG, lg[:], ALU.mult, ALU.add))
                    cV.guard(e)
                    cV.tick(e.reduce_max(m[:, 1:2], msk[:], AX.X))
                    cV.guard(e)
                    cV.tick(e.tensor_add(scale_sb[:, tt:tt + 1], m[:, 0:1], m[:, 1:2]))
                    cV.guard(e)
                    cV.tick(e.engine_nop())

            pb.emit(ph)
            cT.check(NT)
            cV.check(7 * NT)

        with (
            nc.sbuf_tensor(f"{u}_ep", [128, 2 * 8192], BF16) as ep,
            nc.psum_tensor(f"{u}_psm", [128, 1024], F32) as ps_m,
        ):
            ph = Phase(f"{u}_moe")
            cT = Clk(pb, f"{u}3T")
            cS = Clk(pb, f"{u}3S")
            dsl = [f"{u}3d0", f"{u}3d1"]
            dbase = pb.bases(*dsl)

            def mmd(g): return cT.base + g + 1
            def evd(g): return cS.base + g + 1

            @ph.on("gpsimd")
            def _(e):
                for db in range(4):
                    if db >= 2:
                        cT.wait(e, mmd((db - 1) * NT - 1))
                    ins = e.dma_start(out=ep[:, (db % 2) * 8192:(db % 2 + 1) * 8192],
                                      in_=w["ex"][db])
                    pb.inc(ins, dsl[db % 2], 16)

            @ph.on("tensor")
            def _(e):
                for db in range(4):
                    e.wait_ge(pb.sem(dsl[db % 2]), dbase[dsl[db % 2]] + 16 * (db // 2 + 1))
                    for tt in range(NT):
                        g = db * NT + tt
                        if g >= 2:
                            cS.wait(e, evd(g - 2))
                        for dt in range(ND):
                            ins = e.matmul(
                                ps_m[:, (tt % 2) * 512:(tt % 2) * 512 + 512],
                                x_fm[:, dt * T + tt * 128:dt * T + tt * 128 + 128],
                                ep[:, (db % 2) * 8192 + dt * 512:(db % 2) * 8192 + (dt + 1) * 512],
                                start=(dt == 0), stop=(dt == ND - 1))
                        cT.tick(ins)

            @ph.on("scalar")
            def _(e):
                for db in range(4):
                    for tt in range(NT):
                        g = db * NT + tt
                        cT.wait(e, mmd(g))
                        cS.guard(e)
                        ins = e.activation(y_acc[:, tt * D + db * 512:tt * D + db * 512 + 512],
                                           ps_m[:, (tt % 2) * 512:(tt % 2) * 512 + 512],
                                           AF.Copy, scale=scale_sb[:, tt:tt + 1])
                        cS.tick(ins)

            pb.emit(ph)
            cT.check(32)
            cS.check(32)

        with (
            nc.sbuf_tensor(f"{u}_hp", [128, 16 * T], BF16) as h_pan,
            nc.sbuf_tensor(f"{u}_upt", [128, 2 * 2048], BF16) as upt,
            nc.sbuf_tensor(f"{u}_dnb", [128, 2 * 8192], BF16) as dnb,
            nc.sbuf_tensor(f"{u}_rlu", [128, 2 * 512], BF16) as rlu,
            nc.psum_tensor(f"{u}_psu", [128, 2048], F32) as ps_u,
            nc.psum_tensor(f"{u}_psd", [128, 1024], F32) as ps_d,
        ):
            ph = Phase(f"{u}_mlp")
            cT = Clk(pb, f"{u}4T")
            cS = Clk(pb, f"{u}4S")
            cV = Clk(pb, f"{u}4V")
            usl = [f"{u}4u0", f"{u}4u1"]
            dsl = [f"{u}4d0", f"{u}4d1"]
            ubase = pb.bases(*usl)
            dbase = pb.bases(*dsl)

            def umm(jp, g): return cT.base + jp * 64 + g + 1
            def dmm(jp, g2): return cT.base + jp * 64 + 32 + g2 + 1
            def reld(jp, g): return cS.base + jp * 32 + g + 1
            def hd(jp, g): return cV.base + jp * 64 + g + 1
            def yad(jp, g2): return cV.base + jp * 64 + 32 + g2 + 1

            @ph.on("gpsimd")
            def _(e):
                for jp in range(4):
                    for jc in range(16):
                        uu = jp * 16 + jc
                        if uu >= 2:
                            jp0, g0 = divmod((uu - 2) * 2 + 1, 32)
                            cT.wait(e, umm(jp0, g0))
                        ins = e.dma_start(out=upt[:, (uu % 2) * 2048:(uu % 2 + 1) * 2048],
                                          in_=w["up"][uu])
                        pb.inc(ins, usl[uu % 2], 16)
                    for db in range(4):
                        d = jp * 4 + db
                        if d >= 2:
                            jp0, g0 = divmod((d - 2) * 8 + 7, 32)
                            cT.wait(e, dmm(jp0, g0))
                        ins = e.dma_start(out=dnb[:, (d % 2) * 8192:(d % 2 + 1) * 8192],
                                          in_=w["dn"][d])
                        pb.inc(ins, dsl[d % 2], 16)

            @ph.on("tensor")
            def _(e):
                for jp in range(4):
                    for jc in range(16):
                        uu = jp * 16 + jc
                        e.wait_ge(pb.sem(usl[uu % 2]), ubase[usl[uu % 2]] + 16 * (uu // 2 + 1))
                        for tb in range(2):
                            g = jc * 2 + tb
                            ag = uu * 2 + tb
                            if ag >= 4:
                                cV.wait(e, hd((ag - 4) // 32, (ag - 4) % 32))
                            sl = ((jc % 2) * 2 + tb) * 512
                            for dt in range(ND):
                                ins = e.matmul(
                                    ps_u[:, sl:sl + 512],
                                    upt[:, (uu % 2) * 2048 + dt * 128:(uu % 2) * 2048 + (dt + 1) * 128],
                                    x_fm[:, dt * T + tb * 512:dt * T + tb * 512 + 512],
                                    start=(dt == 0), stop=(dt == ND - 1))
                            cT.tick(ins)
                    for db in range(4):
                        d = jp * 4 + db
                        e.wait_ge(pb.sem(dsl[d % 2]), dbase[dsl[d % 2]] + 16 * (d // 2 + 1))
                        cV.wait(e, hd(jp, 31))
                        for tt in range(NT):
                            g2 = db * NT + tt
                            if g2 >= 2 or jp > 0:
                                pjp, pg = (jp, g2 - 2) if g2 >= 2 else (jp - 1, 30 + g2)
                                cV.wait(e, yad(pjp, pg))
                            for jc in range(16):
                                ins = e.matmul(
                                    ps_d[:, (tt % 2) * 512:(tt % 2) * 512 + 512],
                                    h_pan[:, jc * T + tt * 128:jc * T + tt * 128 + 128],
                                    dnb[:, (d % 2) * 8192 + jc * 512:(d % 2) * 8192 + (jc + 1) * 512],
                                    start=(jc == 0), stop=(jc == 15))
                            cT.tick(ins)

            @ph.on("scalar")
            def _(e):
                for jp in range(4):
                    for g in range(32):
                        cT.wait(e, umm(jp, g))
                        ag = jp * 32 + g
                        if ag >= 2:
                            cV.wait(e, hd((ag - 2) // 32, (ag - 2) % 32))
                        cS.guard(e)
                        sl = (((g // 2) % 2) * 2 + (g % 2)) * 512
                        cS.tick(e.activation(rlu[:, (ag % 2) * 512:(ag % 2) * 512 + 512],
                                             ps_u[:, sl:sl + 512], AF.Relu))

            @ph.on("vector")
            def _(e):
                for jp in range(4):
                    for g in range(32):
                        jc, tb = g // 2, g % 2
                        ag = jp * 32 + g
                        cS.wait(e, reld(jp, g))
                        cV.guard(e)
                        sl = ((jc % 2) * 2 + tb) * 512
                        ins = e.scalar_tensor_tensor(
                            h_pan[:, jc * T + tb * 512:jc * T + tb * 512 + 512],
                            rlu[:, (ag % 2) * 512:(ag % 2) * 512 + 512],
                            1.0, ps_u[:, sl:sl + 512], ALU.mult, ALU.mult)
                        cV.tick(ins)
                    for g2 in range(32):
                        cT.wait(e, dmm(jp, g2))
                        tt = g2 % NT
                        db = g2 // NT
                        cV.guard(e)
                        ins = e.tensor_add(y_acc[:, tt * D + db * 512:tt * D + db * 512 + 512],
                                           y_acc[:, tt * D + db * 512:tt * D + db * 512 + 512],
                                           ps_d[:, (tt % 2) * 512:(tt % 2) * 512 + 512])
                        cV.tick(ins)

            pb.emit(ph)
            cT.check(4 * 64)
            cS.check(4 * 32)
            cV.check(4 * 64)

        with (
            nc.sbuf_tensor(f"{u}_rt2", [128, 2 * D], F32) as rt2,
            nc.sbuf_tensor(f"{u}_ro", [128, 2 * D], F32) as ro,
            nc.sbuf_tensor(f"{u}_yh", [128, 2 * D], F32) as yh,
            nc.sbuf_tensor(f"{u}_st2", [128, 16], F32) as st2,
            nc.sbuf_tensor(f"{u}_gb", [128, D], F32) as gb,
            nc.sbuf_tensor(f"{u}_bb", [128, D], F32) as bb,
        ):
            ph = Phase(f"{u}_out")
            cS = Clk(pb, f"{u}5S")
            cV = Clk(pb, f"{u}5V")
            dsl = [f"{u}5d0", f"{u}5d1"]
            osl = [f"{u}5o0", f"{u}5o1"]
            gds = f"{u}5g"
            dbase = pb.bases(*dsl, *osl, gds)

            def sqd(i): return cS.base + 4 * i + 1
            def lnd(i): return cS.base + 4 * i + 2
            def expd(i): return cS.base + 4 * i + 3
            def yhd(i): return cS.base + 4 * (i + 1)
            def varpd(i): return cV.base + 11 * i + 6
            def negbd(i): return cV.base + 11 * i + 8
            def rod(i): return cV.base + 11 * (i + 1)

            @ph.on("sync")
            def _(e):
                pb.inc(e.dma_start(out=gb[:], in_=w["gb"]), gds, 16)
                pb.inc(e.dma_start(out=bb[:], in_=w["bb"]), gds, 16)
                for tt in range(NT):
                    if tt >= 2:
                        cV.wait(e, rod(tt - 2))
                        ins = e.dma_start(out=dst_r[(tt - 2) * 128:(tt - 1) * 128, :],
                                          in_=ro[:, (tt % 2) * D:(tt % 2 + 1) * D])
                        pb.inc(ins, osl[tt % 2], 16)
                    ins = e.dma_start(out=rt2[:, (tt % 2) * D:(tt % 2 + 1) * D],
                                      in_=src_r[tt * 128:(tt + 1) * 128, :])
                    pb.inc(ins, dsl[tt % 2], 16)
                for tt in range(NT - 2, NT):
                    cV.wait(e, rod(tt))
                    ins = e.dma_start(out=dst_r[tt * 128:(tt + 1) * 128, :],
                                      in_=ro[:, (tt % 2) * D:(tt % 2 + 1) * D])
                    pb.inc(ins, osl[tt % 2], 16)
                e.wait_ge(pb.sem(osl[0]), dbase[osl[0]] + 16 * 4)
                e.wait_ge(pb.sem(osl[1]), dbase[osl[1]] + 16 * 4)

            @ph.on("scalar")
            def _(e):
                for tt in range(NT):
                    c = tt % 2
                    yt = y_acc[:, tt * D:(tt + 1) * D]
                    if tt >= 2:
                        cV.wait(e, rod(tt - 2))
                    cS.guard(e)
                    cS.tick(e.activation(yh[:, c * D:(c + 1) * D], yt, AF.Square,
                                         accum_out=st2[:, 1:2]))
                    cV.wait(e, varpd(tt))
                    cS.guard(e)
                    cS.tick(e.activation(st2[:, 8:9], st2[:, 6:7], AF.Ln))
                    cS.guard(e)
                    cS.tick(e.activation(st2[:, 10 + c:11 + c], st2[:, 8:9], AF.Exp, scale=-0.5))
                    cV.wait(e, negbd(tt))
                    cS.guard(e)
                    cS.tick(e.activation(yh[:, c * D:(c + 1) * D], yt, AF.Identity,
                                         bias=st2[:, 12 + c:13 + c],
                                         scale=st2[:, 10 + c:11 + c]))

            @ph.on("vector")
            def _(e):
                for tt in range(NT):
                    c = tt % 2
                    yt = y_acc[:, tt * D:(tt + 1) * D]
                    cV.guard(e)
                    cV.tick(e.reduce_sum(st2[:, 0:1], yt, AX.X))
                    cS.wait(e, sqd(tt))
                    if tt >= 1:
                        cS.wait(e, lnd(tt - 1))
                    for args in ((st2[:, 2:3], st2[:, 0:1], 1.0 / D),
                                 (st2[:, 3:4], st2[:, 1:2], 1.0 / D)):
                        cV.guard(e)
                        cV.tick(e.tensor_scalar_mul(*args))
                    cV.guard(e)
                    cV.tick(e.tensor_mul(st2[:, 4:5], st2[:, 2:3], st2[:, 2:3]))
                    cV.guard(e)
                    cV.tick(e.tensor_sub(st2[:, 5:6], st2[:, 3:4], st2[:, 4:5]))
                    cV.guard(e)
                    cV.tick(e.tensor_scalar_add(st2[:, 6:7], st2[:, 5:6], LN_EPS))
                    cS.wait(e, expd(tt))
                    cV.guard(e)
                    cV.tick(e.tensor_mul(st2[:, 14:15], st2[:, 2:3], st2[:, 10 + c:11 + c]))
                    cV.guard(e)
                    cV.tick(e.tensor_scalar_mul(st2[:, 12 + c:13 + c], st2[:, 14:15], -1.0))
                    cS.wait(e, yhd(tt))
                    if tt == 0:
                        e.wait_ge(pb.sem(gds), dbase[gds] + 32)
                    e.wait_ge(pb.sem(dsl[tt % 2]), dbase[dsl[tt % 2]] + 16 * (tt // 2 + 1))
                    if tt >= 2:
                        e.wait_ge(pb.sem(osl[tt % 2]), dbase[osl[tt % 2]] + 16 * (tt // 2))
                    yhc = yh[:, c * D:(c + 1) * D]
                    cV.guard(e)
                    cV.tick(e.tensor_mul(yhc, yhc, gb[:]))
                    cV.guard(e)
                    cV.tick(e.tensor_add(yhc, yhc, bb[:]))
                    cV.guard(e)
                    cV.tick(e.tensor_add(ro[:, c * D:(c + 1) * D],
                                         rt2[:, c * D:(c + 1) * D], yhc))

            pb.emit(ph)
            cS.check(4 * NT)
            cV.check(11 * NT)



def _ffn_dram(nc):
    return dict(
        gate=nc.dram_tensor("gate_st", [128, 128], F32, kind="ExternalInput").ap(),
        ex=nc.dram_tensor("ex_st", [4, 128, 8192], F32, kind="ExternalInput").ap(),
        up=nc.dram_tensor("up_st", [64, 128, 2048], F32, kind="ExternalInput").ap(),
        dn=nc.dram_tensor("dn_st", [16, 128, 8192], F32, kind="ExternalInput").ap(),
        gb=nc.dram_tensor("gb_st", [128, D], F32, kind="ExternalInput").ap(),
        bb=nc.dram_tensor("bb_st", [128, D], F32, kind="ExternalInput").ap(),
    )


def _stage_ffn(inputs):
    up = np.ascontiguousarray(
        inputs["L0_ffn_up_w"].reshape(16, 128, 64, 128).transpose(2, 1, 0, 3).reshape(64, 128, 2048))
    dn = np.ascontiguousarray(
        inputs["L0_ffn_down_w"].reshape(4, 16, 128, 4, 512).transpose(0, 3, 2, 1, 4).reshape(16, 128, 8192))
    ex = np.ascontiguousarray(
        inputs["L0_ffn_expert_weight"].reshape(16, 128, 4, 512).transpose(2, 1, 0, 3).reshape(4, 128, 8192))
    gate = np.ascontiguousarray(
        inputs["L0_ffn_gate_w"].reshape(16, 128, 8).transpose(1, 0, 2).reshape(128, 128))
    gb = np.ascontiguousarray(np.broadcast_to(inputs["L0_ffn_ln_gamma"], (128, D)))
    bb = np.ascontiguousarray(np.broadcast_to(inputs["L0_ffn_ln_beta"], (128, D)))
    return dict(gate_st=gate, ex_st=ex, up_st=up, dn_st=dn, gb_st=gb, bb_st=bb)


def build_nc(dev=None, pairs=4):
    nc = bass.Bass("TRN2", target_bir_lowering=False)
    ident_d = nc.dram_tensor("ident", [128, 128], BF16, kind="ExternalInput").ap()
    onesb_d = nc.dram_tensor("ones_bf", [128, 1], BF16, kind="ExternalInput").ap()
    onesf_d = nc.dram_tensor("ones_f", [1, 128], F32, kind="ExternalInput").ap()
    out = nc.dram_tensor("out", [T, D], F32, kind="ExternalOutput").ap()
    wf = _ffn_dram(nc)
    wa = _a_dram(nc, pairs=pairs)
    wb = _b_dram(nc, pairs=pairs)

    pb = PB(nc)
    with (
        nc.sbuf_tensor("c_ident", [128, 128], BF16) as ident,
        nc.sbuf_tensor("c_onesb", [128, 1], BF16) as onesb,
        nc.sbuf_tensor("c_onesf", [1, 128], F32) as onesf,
    ):
        cn = dict(ident=ident, ones_bf=onesb, ones_f=onesf)
        ph = Phase("init")
        b0 = pb.bases("c_dma")

        @ph.on("sync")
        def _(e):
            pb.inc(e.dma_start(out=ident[:], in_=ident_d), "c_dma", 16)
            pb.inc(e.dma_start(out=onesb[:], in_=onesb_d), "c_dma", 16)
            pb.inc(e.dma_start(out=onesf[:], in_=onesf_d), "c_dma", 16)
            e.wait_ge(pb.sem("c_dma"), b0["c_dma"] + 48)
        pb.emit(ph)

        if dev == "ffn":
            dev_r = nc.dram_tensor("dev_r", [T, D], F32, kind="ExternalInput").ap()
            emit_ffn(pb, cn, wf, dev_r, out, 0)
        elif dev == "attn_a":
            emit_attn_a(pb, cn, wa, out)
        elif dev == "attn_b":
            dev_r = nc.dram_tensor("dev_r", [T, D], F32, kind="ExternalInput").ap()
            emit_attn_b(pb, cn, wb, dev_r, out)
        else:
            r1 = nc.dram_tensor("r1", [T, D], F32).ap()
            r2 = nc.dram_tensor("r2", [T, D], F32).ap()
            r3 = nc.dram_tensor("r3", [T, D], F32).ap()
            emit_attn_a(pb, cn, wa, r1)
            emit_ffn(pb, cn, wf, r1, r2, 0)
            emit_attn_b(pb, cn, wb, r2, r3)
            emit_ffn(pb, cn, wf, r3, out, 1)
    return nc


def _core_inputs(inputs, core):
    m = dict(ident=np.eye(128, dtype=ml_dtypes.bfloat16),
             ones_bf=np.ones((128, 1), ml_dtypes.bfloat16),
             ones_f=np.ones((1, 128), np.float32))
    m.update(_stage_ffn(inputs))
    m.update(_stage_attn_a(inputs, core))
    m.update(_stage_attn_b(inputs, core))
    return m


def kernel(**inputs):
    nc = build_nc()
    in_maps = [_core_inputs(inputs, c) for c in range(8)]
    res = run_bass_kernel_spmd(nc, in_maps, core_ids=list(range(8)))
    shards = [res.results[c]["out"] for c in range(8)]
    out = np.empty((4, CTX, D), np.float32)
    for c in range(8):
        out[c // 2, (c % 2) * T:(c % 2 + 1) * T, :] = shards[c]
    return out


QLR = 1536
KVLR = 512
DR = 64
DQ = 192
A_EPS = 1e-6
A_SCALE = DQ ** -0.5


_DBG = [False]


def _a_dram(nc, pairs=4):
    d = {}
    if _DBG[0]:
        d.update(
            dbg_qn=nc.dram_tensor("dbg_qn", [128, 16 * 1024], BF16, kind="ExternalOutput").ap(),
            dbg_ckvn=nc.dram_tensor("dbg_ckvn", [128, 4 * 2048], BF16, kind="ExternalOutput").ap(),
            dbg_kper=nc.dram_tensor("dbg_kper", [128, 2048], BF16, kind="ExternalOutput").ap(),
            dbg_qper=nc.dram_tensor("dbg_qper", [128, 8 * 1024], BF16, kind="ExternalOutput").ap(),
            dbg_o=nc.dram_tensor("dbg_o", [128, 16 * 1024], BF16, kind="ExternalOutput").ap(),
        )
    d.update(
        x_own=nc.dram_tensor("x_own", [T, D], F32, kind="ExternalInput").ap(),
        cc2_in=nc.dram_tensor("acc2_in", [128, 4608], BF16).ap(),
        cc2_out=nc.dram_tensor("acc2_out", [256, 4608], BF16).ap(),
        wqa=nc.dram_tensor("wqa_img", [128, 16 * 1536], F32, kind="ExternalInput").ap(),
        wqbN=nc.dram_tensor("wqbN_img", [128, 12 * 2048], F32, kind="ExternalInput").ap(),
        wqbP=nc.dram_tensor("wqbP_img", [128, 12 * 1024], F32, kind="ExternalInput").ap(),
        wkva=nc.dram_tensor("wkva_img", [128, 16 * 512], F32, kind="ExternalInput").ap(),
        wkvaP=nc.dram_tensor("wkvaP_img", [128, 16 * 64], F32, kind="ExternalInput").ap(),
        wkvbK=nc.dram_tensor("wkvbK_img", [128, 4 * 2048], F32, kind="ExternalInput").ap(),
        wkvbV=nc.dram_tensor("wkvbV_img", [128, 4 * 2048], F32, kind="ExternalInput").ap(),
        wo=nc.dram_tensor("wo_img", [128, 16 * 2048], F32, kind="ExternalInput").ap(),
        qg=nc.dram_tensor("qg_img", [128, 12], F32, kind="ExternalInput").ap(),
        kvg=nc.dram_tensor("kvg_img", [128, 4], F32, kind="ExternalInput").ap(),
        cosq=nc.dram_tensor("cosq_img", [128, 8 * 64], F32, kind="ExternalInput").ap(),
        sinq=nc.dram_tensor("sinq_img", [128, 8 * 64], F32, kind="ExternalInput").ap(),
        mask=nc.dram_tensor("mask_img", [8, 128, 2048], BF16, kind="ExternalInput").ap(),
    )
    d["pairs"] = [[2 * i, 2 * i + 1] for i in range(pairs)]
    return d


def _rope_tables(positions):
    inv = 1.0 / (10000.0 ** (np.arange(0, DR, 2, dtype=np.float64) / DR))
    fr = positions[:, None].astype(np.float64) * inv[None, :]
    emb = np.concatenate([fr, fr], -1)
    return np.cos(emb).astype(np.float32), np.sin(emb).astype(np.float32)


def _stage_attn_a(inputs, core):
    seq, hf = core // 2, core % 2
    x = inputs["L0_attn_a_hidden_states"][seq]
    w_qa = inputs["L0_attn_a_w_qa"]
    w_qb = inputs["L0_attn_a_w_qb"]
    w_kva = inputs["L0_attn_a_w_kva"]
    w_kvb = inputs["L0_attn_a_w_kvb"]
    w_o = inputs["L0_attn_a_w_o"]
    m = {}
    m["x_own"] = np.ascontiguousarray(x[hf * T:(hf + 1) * T])
    m["wqa_img"] = np.ascontiguousarray(
        w_qa.reshape(1536, 16, 128).transpose(2, 1, 0).reshape(128, 16 * 1536))
    wqb3 = w_qb.reshape(H, 192, 1536)
    wqbN = wqb3[:, :128, :].reshape(2048, 1536)
    wqbP = wqb3[:, 128:, :].reshape(1024, 1536)
    m["wqbN_img"] = np.ascontiguousarray(
        wqbN.transpose(1, 0).reshape(12, 128, 2048).transpose(1, 0, 2).reshape(128, 12 * 2048))
    m["wqbP_img"] = np.ascontiguousarray(
        wqbP.transpose(1, 0).reshape(12, 128, 1024).transpose(1, 0, 2).reshape(128, 12 * 1024))
    m["wkva_img"] = np.ascontiguousarray(
        w_kva[:512].reshape(512, 16, 128).transpose(2, 1, 0).reshape(128, 16 * 512))
    m["wkvaP_img"] = np.ascontiguousarray(
        w_kva[512:].reshape(64, 16, 128).transpose(2, 1, 0).reshape(128, 16 * 64))
    wkvb3 = w_kvb.reshape(H, 256, 512)
    wkvbK = wkvb3[:, :128, :].reshape(2048, 512)
    wkvbV = wkvb3[:, 128:, :].reshape(2048, 512)
    m["wkvbK_img"] = np.ascontiguousarray(
        wkvbK.transpose(1, 0).reshape(4, 128, 2048).transpose(1, 0, 2).reshape(128, 4 * 2048))
    m["wkvbV_img"] = np.ascontiguousarray(
        wkvbV.transpose(1, 0).reshape(4, 128, 2048).transpose(1, 0, 2).reshape(128, 4 * 2048))
    m["wo_img"] = np.ascontiguousarray(
        w_o.reshape(4, 512, 16, 128).transpose(3, 2, 0, 1).reshape(128, 16 * 2048))
    m["qg_img"] = np.ascontiguousarray(
        inputs["L0_attn_a_q_a_ln_w"].reshape(12, 128).transpose(1, 0))
    m["kvg_img"] = np.ascontiguousarray(
        inputs["L0_attn_a_kv_a_ln_w"].reshape(4, 128).transpose(1, 0))
    posq = np.arange(hf * T, (hf + 1) * T)
    cq_, sq_ = _rope_tables(posq)
    m["cosq_img"] = np.ascontiguousarray(cq_.reshape(8, 128, 64).transpose(1, 0, 2).reshape(128, 512))
    m["sinq_img"] = np.ascontiguousarray(sq_.reshape(8, 128, 64).transpose(1, 0, 2).reshape(128, 512))
    qpos = (hf * T + np.arange(T))[:, None]
    kpos = np.arange(CTX)[None, :]
    mask = np.where(kpos <= qpos, 0.0, BIG_NEG).astype(ml_dtypes.bfloat16)
    m["mask_img"] = np.ascontiguousarray(mask.reshape(8, 128, 2048))
    return m


class View:

    def __init__(self, t, tcols, off, n):
        self.t, self.tcols, self.off, self.n = t, tcols, off, n

    def __getitem__(self, idx):
        p, c = idx
        a = self.off + (c.start or 0)
        b = self.off + (c.stop if c.stop is not None else self.n)
        return self.t[p, a:b]

    def sap(self, off, chunks, step, width, p0=0, pn=128):
        return bass.AP(self.t, p0 * self.tcols + self.off + off,
                       [[self.tcols, pn], [step, chunks], [1, width]])


class Arena:
    def __init__(self, nc, name, cols_bf16, cols_f32):
        self._g1 = nc.sbuf_tensor(f"{name}_bf", [128, cols_bf16], BF16)
        self._g2 = nc.sbuf_tensor(f"{name}_f", [128, cols_f32], F32)
        self.tb = self._g1.__enter__()
        self.tf = self._g2.__enter__()
        self.nb, self.nf = cols_bf16, cols_f32
        self.free = {BF16: [(0, cols_bf16)], F32: [(0, cols_f32)]}
        self.live = {}

    def alloc(self, name, n, dt=BF16):
        fl = self.free[dt]
        best = None
        for i, (a, b) in enumerate(fl):
            if b - a >= n and (best is None or b - a < fl[best][1] - fl[best][0]):
                best = i
        if best is None:
            raise MemoryError(f"arena {name} {n} {dt}")
        a, b = fl[best]
        fl[best] = (a + n, b)
        if fl[best][0] == fl[best][1]:
            fl.pop(best)
        self.live[name] = (a, n, dt)
        t = self.tb if dt == BF16 else self.tf
        return View(t, self.nb if dt == BF16 else self.nf, a, n)

    def freev(self, *names):
        for name in names:
            a, n, dt = self.live.pop(name)
            fl = sorted(self.free[dt] + [(a, a + n)])
            out = []
            for seg in fl:
                if out and out[-1][1] == seg[0]:
                    out[-1] = (out[-1][0], seg[1])
                else:
                    out.append(seg)
            self.free[dt] = out

    def close(self):
        self._g2.__exit__(None, None, None)
        self._g1.__exit__(None, None, None)


def emit_attn_a(pb, cn, w, dst_r):
    nc = pb.nc
    ar = Arena(nc, "aa", 87040, 9216)
    ident = cn["ident"]
    ones_bf = cn["ones_bf"]
    ones_f = cn["ones_f"]

    h_own = ar.alloc("h_own", 16 * 1024)
    wqa = ar.alloc("wqa", 16 * 1536)
    xln = ar.alloc("xln", 2 * D)
    rt = ar.alloc("rt", 3 * D, F32)
    cosq = ar.alloc("cosq", 512, F32)
    sinq = ar.alloc("sinq", 512, F32)
    tpsg = nc.psum_tensor("aa_tps", [128, 8 * 1024], BF16)
    tps = tpsg.__enter__()
    with (
        nc.sbuf_tensor("aa_st", [128, 16], F32) as stt_,
        nc.sbuf_tensor("aa_qg", [128, 12], F32) as qg,
        nc.sbuf_tensor("aa_kvg", [128, 4], F32) as kvg,
    ):
        ph = Phase("a1")
        dws = pb.bases("a1w", "a1wg")

        @ph.on("gpsimd")
        def _(e):
            pb.inc(e.dma_start(out=wqa[:, :], in_=w["wqa"]), "a1wg", 16)

        @ph.on("sync")
        def _(e):
            for dst, src in ((cosq, w["cosq"]), (sinq, w["sinq"])):
                pb.inc(e.dma_start(out=dst[:, :], in_=src), "a1w", 16)
            pb.inc(e.dma_start(out=qg[:], in_=w["qg"]), "a1w", 16)
            pb.inc(e.dma_start(out=kvg[:], in_=w["kvg"]), "a1w", 16)

        src_tiles = [w["x_own"][i * 128:(i + 1) * 128, :] for i in range(8)]
        sb = dict(rt=rt, st=stt_, xln=xln, tps=tps, ident=ident)
        emit_ln_fm(pb, ph, "aln", src_tiles, h_own, 0, 1024, sb)
        pb.emit(ph)
        for f in ph.fns.get("_check", []):
            f()
        tpsg.__exit__(None, None, None)

        ar.freev("xln", "rt")
        cq = ar.alloc("cq", 12 * 1024)
        sqc = ar.alloc("sqc", 1024)
        rstdq = ar.alloc("rstdq", 1024, F32)
        with (
            nc.psum_tensor("a2_ps", [128, 4 * 512], F32) as ps2,
            nc.psum_tensor("a2_pss", [128, 1024], F32) as pss,
            nc.psum_tensor("a2_psb", [128, 1024], F32) as psb,
        ):
            ph = Phase("a2")
            cT = Clk(pb, "a2T")
            cS = Clk(pb, "a2S")
            cV = Clk(pb, "a2V")
            dw2 = pb.bases("a2w")


            def cqd(g): return cT.base + g + 1
            def ssd(kt): return cT.base + 24 + kt + 1
            def bcd(nb): return cT.base + 36 + nb + 1

            def sqd(kt): return cS.base + kt + 1
            def rsd(): return cS.base + 12 + 2

            def evq(g): return cV.base + g + 1
            def cqn(j): return cV.base + 24 + j + 1

            @ph.on("tensor")
            def _(e):
                e.wait_ge(pb.sem("a1wg"), dws["a1wg"] + 16)
                e.wait_ge(pb.sem("a1w"), dws["a1w"] + 16 * 4)
                for qt_ in range(12):
                    for tb in range(2):
                        g = qt_ * 2 + tb
                        if g >= 4:
                            cV.wait(e, evq(g - 4))
                        for dt in range(16):
                            ins = e.matmul(ps2[:, (g % 4) * 512:(g % 4) * 512 + 512],
                                           wqa[:, dt * 1536 + qt_ * 128:dt * 1536 + qt_ * 128 + 128],
                                           h_own[:, dt * 1024 + tb * 512:dt * 1024 + tb * 512 + 512],
                                           start=(dt == 0), stop=(dt == 15))
                        cT.tick(ins)
                for kt in range(12):
                    cS.wait(e, sqd(kt))
                    for nb in range(2):
                        ins = e.matmul(pss[0:1, nb * 512:nb * 512 + 512],
                                       ones_bf[:], sqc[:, nb * 512:nb * 512 + 512],
                                       start=(kt == 0), stop=(kt == 11))
                    cT.tick(ins)
                cS.wait(e, rsd())
                for nb in range(2):
                    ins = e.matmul(psb[:, nb * 512:nb * 512 + 512], ones_f[:],
                                   rstdq[0:1, nb * 512:nb * 512 + 512],
                                   start=True, stop=True)
                    cT.tick(ins)

            @ph.on("scalar")
            def _(e):
                for kt in range(12):
                    cV.wait(e, evq(kt * 2 + 1))
                    if kt >= 1:
                        cT.wait(e, ssd(kt - 1))
                    cS.guard(e)
                    cS.tick(e.square(sqc[:, :], cq[:, kt * 1024:(kt + 1) * 1024]))
                cV.wait(e, cV.base + 25)
                cS.guard(e)
                cS.tick(e.activation(rstdq[0:1, 0:1024], rstdq[0:1, 0:1024], AF.Ln))
                cS.guard(e)
                cS.tick(e.activation(rstdq[0:1, 0:1024], rstdq[0:1, 0:1024], AF.Exp, scale=-0.5))

            @ph.on("vector")
            def _(e):
                for g in range(24):
                    cT.wait(e, cqd(g))
                    qt_, tb = g // 2, g % 2
                    cV.guard(e)
                    cV.tick(e.tensor_copy(cq[:, qt_ * 1024 + tb * 512:qt_ * 1024 + tb * 512 + 512],
                                          ps2[:, (g % 4) * 512:(g % 4) * 512 + 512]))
                cT.wait(e, ssd(11))
                cV.guard(e)
                cV.tick(e.tensor_scalar(rstdq[0:1, 0:1024], pss[0:1, 0:1024],
                                        1.0 / QLR, A_EPS, ALU.mult, ALU.add))
                for kt in range(12):
                    for tb in range(2):
                        cT.wait(e, bcd(tb))
                        cV.guard(e)
                        sl = cq[:, kt * 1024 + tb * 512:kt * 1024 + tb * 512 + 512]
                        ins = e.scalar_tensor_tensor(sl, sl, qg[:, kt:kt + 1],
                                                     psb[:, tb * 512:tb * 512 + 512],
                                                     ALU.mult, ALU.mult)
                        cV.tick(ins)

            pb.emit(ph)
            cT.check(24 + 12 + 2)
            cS.check(12 + 2)
            cV.check(24 + 1 + 24)

        ar.freev("wqa", "sqc", "rstdq")
        ckvn = ar.alloc("ckvn", 4 * 2048)
        kper = ar.alloc("kper", 2048)
        kpet2f = ar.alloc("kpet2f", 1024)
        wkva = ar.alloc("wkva", 16 * 512)
        wkvaP = ar.alloc("wkvaP", 16 * 64)
        wqbN = ar.alloc("wqbN", 6 * 2048)
        ckvno = ar.alloc("ckvno", 4 * 1024)
        kpet = ar.alloc("kpet", 512)
        kpet2o = ar.alloc("kpet2o", 512)
        sqk = ar.alloc("sqk", 1024)
        rstdk = ar.alloc("rstdk", 1024, F32)
        ropet = ar.alloc("ropet", 128, F32)
        with (
            nc.psum_tensor("a3_ps", [128, 1024], F32) as ps3,
            nc.psum_tensor("a3_pss", [128, 1024], F32) as pss3,
        ):
            ph = Phase("a3")
            cT = Clk(pb, "a3T")
            cS = Clk(pb, "a3S")
            cV = Clk(pb, "a3V")
            dk = pb.bases("a3p", "a3cc")

            def ckvd(g): return cT.base + g + 1
            def kpd(ct): return cT.base + 8 + ct + 1
            def ssd(kt): return cT.base + 16 + kt + 1
            def bcd(nb): return cT.base + 20 + nb + 1
            def cevd(g): return cS.base + g + 1
            def kevd(ct): return cS.base + 8 + ct + 1
            def sqd(kt): return cS.base + 16 + kt + 1
            def rsd(): return cS.base + 22
            def rod(ct): return cV.base + 6 * (ct + 1)
            def nsd(j): return cV.base + 49 + j + 1

            @ph.on("gpsimd")
            def _(e):
                pb.inc(e.dma_start(out=wkva[:, :], in_=w["wkva"]), "a2w", 16)
                pb.inc(e.dma_start(out=wkvaP[:, :], in_=w["wkvaP"]), "a2w", 16)
                pb.inc(e.dma_start(out=wqbN[:, 0:12288], in_=w["wqbN"][:, 0:12288]), "a4w", 16)
                e.wait_ge(pb.sem("a3p"), dk["a3p"] + 32)
                ins = e.collective_compute(
                    "AllGather", ALU.bypass, replica_groups=w["pairs"],
                    ins=[w["cc2_in"]], outs=[w["cc2_out"]])
                pb.inc(ins, "a3cc", 1)

            @ph.on("tensor")
            def _(e):
                e.wait_ge(pb.sem("a2w"), dw2["a2w"] + 32)
                for ft in range(4):
                    for nb in range(2):
                        g = ft * 2 + nb
                        if g >= 2:
                            cS.wait(e, cevd(g - 2))
                        for dt in range(16):
                            ins = e.matmul(ps3[:, (g % 2) * 512:(g % 2) * 512 + 512],
                                           wkva[:, dt * 512 + ft * 128:dt * 512 + ft * 128 + 128],
                                           h_own[:, dt * 1024 + nb * 512:dt * 1024 + nb * 512 + 512],
                                           start=(dt == 0), stop=(dt == 15))
                        cT.tick(ins)
                for ct in range(8):
                    if ct >= 2:
                        cS.wait(e, kevd(ct - 2))
                    else:
                        cS.wait(e, cevd(6 + ct))
                    for dt in range(16):
                        ins = e.matmul(ps3[:, (ct % 2) * 512:(ct % 2) * 512 + 64],
                                       h_own[:, dt * 1024 + ct * 128:dt * 1024 + ct * 128 + 128],
                                       wkvaP[:, dt * 64:(dt + 1) * 64],
                                       start=(dt == 0), stop=(dt == 15))
                    cT.tick(ins)
                for kt in range(4):
                    cS.wait(e, sqd(kt))
                    for nb in range(2):
                        ins = e.matmul(pss3[0:1, nb * 512:nb * 512 + 512],
                                       ones_bf[:], sqk[:, nb * 512:nb * 512 + 512],
                                       start=(kt == 0), stop=(kt == 3))
                    cT.tick(ins)
                cS.wait(e, rsd())
                for nb in range(2):
                    ins = e.matmul(pss3[:, nb * 512:nb * 512 + 512], ones_f[:],
                                   rstdk[0:1, nb * 512:nb * 512 + 512],
                                   start=True, stop=True)
                    cT.tick(ins)

            @ph.on("scalar")
            def _(e):
                for g in range(8):
                    ft, nb = g // 2, g % 2
                    cT.wait(e, ckvd(g))
                    cS.guard(e)
                    cS.tick(e.copy(ckvno[:, ft * 1024 + nb * 512:ft * 1024 + nb * 512 + 512],
                                   ps3[:, (g % 2) * 512:(g % 2) * 512 + 512]))
                for ct in range(8):
                    cT.wait(e, kpd(ct))
                    cS.guard(e)
                    cS.tick(e.copy(kpet[:, ct * 64:(ct + 1) * 64],
                                   ps3[:, (ct % 2) * 512:(ct % 2) * 512 + 64]))
                for kt in range(4):
                    if kt >= 1:
                        cT.wait(e, ssd(kt - 1))
                    cS.guard(e)
                    cS.tick(e.square(sqk[:, :], ckvno[:, kt * 1024:(kt + 1) * 1024]))
                cV.wait(e, cV.base + 49)
                cS.guard(e)
                cS.tick(e.activation(rstdk[0:1, 0:1024], rstdk[0:1, 0:1024], AF.Ln))
                cS.guard(e)
                cS.tick(e.activation(rstdk[0:1, 0:1024], rstdk[0:1, 0:1024], AF.Exp, scale=-0.5))

            @ph.on("vector")
            def _(e):
                for ct in range(8):
                    cS.wait(e, kevd(ct))
                    b = ct * 64
                    ck0 = cosq[:, ct * 64:ct * 64 + 32]
                    ck1 = cosq[:, ct * 64 + 32:ct * 64 + 64]
                    sk0 = sinq[:, ct * 64:ct * 64 + 32]
                    sk1 = sinq[:, ct * 64 + 32:ct * 64 + 64]
                    lo = kpet[:, b:b + 32]
                    hi = kpet[:, b + 32:b + 64]
                    cV.guard(e)
                    cV.tick(e.tensor_mul(ropet[:, 0:32], lo, ck0))
                    cV.guard(e)
                    cV.tick(e.tensor_mul(ropet[:, 32:64], hi, sk0))
                    cV.guard(e)
                    cV.tick(e.tensor_sub(kpet2o[:, b:b + 32], ropet[:, 0:32], ropet[:, 32:64]))
                    cV.guard(e)
                    cV.tick(e.tensor_mul(ropet[:, 64:96], hi, ck1))
                    cV.guard(e)
                    cV.tick(e.tensor_mul(ropet[:, 96:128], lo, sk1))
                    cV.guard(e)
                    cV.tick(e.tensor_add(kpet2o[:, b + 32:b + 64], ropet[:, 64:96], ropet[:, 96:128]))
                cT.wait(e, ssd(3))
                cV.guard(e)
                cV.tick(e.tensor_scalar(rstdk[0:1, 0:1024], pss3[0:1, 0:1024],
                                        1.0 / KVLR, A_EPS, ALU.mult, ALU.add))
                for kt in range(4):
                    for nb in range(2):
                        j = kt * 2 + nb
                        cT.wait(e, bcd(nb))
                        cV.guard(e)
                        sl = ckvno[:, kt * 1024 + nb * 512:kt * 1024 + nb * 512 + 512]
                        ins = e.scalar_tensor_tensor(sl, sl, kvg[:, kt:kt + 1],
                                                     pss3[:, nb * 512:nb * 512 + 512],
                                                     ALU.mult, ALU.mult)
                        cV.tick(ins)

            @ph.on("sync")
            def _(e):
                cV.wait(e, nsd(7))
                pb.inc(e.dma_start(out=w["cc2_in"][:, 0:4096], in_=ckvno[:, 0:4096]), "a3p", 16)
                pb.inc(e.dma_start(out=w["cc2_in"][:, 4096:4608], in_=kpet2o[:, 0:512]), "a3p", 16)

            pb.emit(ph)
            cT.check(8 + 8 + 4 + 2)
            cS.check(8 + 8 + 4 + 2)
            cV.check(48 + 1 + 8)

        ar.freev("h_own", "wkva", "wkvaP", "kpet", "kpet2o", "ckvno", "sqk", "rstdk", "ropet")
        qn = ar.alloc("qn", 16 * 1024)
        wqbN2 = ar.alloc("wqbN2", 6 * 2048)
        wqbP = ar.alloc("wqbP", 12 * 1024)
        with nc.psum_tensor("a4_ps", [128, 2048], F32) as ps4:
            ph = Phase("a4")
            cT = Clk(pb, "a4T")
            cS = Clk(pb, "a4S")
            d4 = pb.bases("a4w")

            def mmd(g): return cT.base + g + 1
            def evd(g): return cS.base + g + 1

            @ph.on("gpsimd")
            def _(e):
                pb.inc(e.dma_start(out=wqbN2[:, :], in_=w["wqbN"][:, 12288:24576]), "a4w", 16)
                pb.inc(e.dma_start(out=wqbP[:, :], in_=w["wqbP"]), "a5w", 16)

            @ph.on("tensor")
            def _(e):
                e.wait_ge(pb.sem("a4w"), d4["a4w"] + 16)
                for ft in range(16):
                    for tb in range(2):
                        g = ft * 2 + tb
                        if g >= 4:
                            cS.wait(e, evd(g - 4))
                        for kt in range(12):
                            wsrc = wqbN if kt < 6 else wqbN2
                            kk = kt if kt < 6 else kt - 6
                            ins = e.matmul(ps4[:, (g % 4) * 512:(g % 4) * 512 + 512],
                                           wsrc[:, kk * 2048 + ft * 128:kk * 2048 + ft * 128 + 128],
                                           cq[:, kt * 1024 + tb * 512:kt * 1024 + tb * 512 + 512],
                                           start=(kt == 0), stop=(kt == 11))
                        cT.tick(ins)

            @ph.on("scalar")
            def _(e):
                for g in range(32):
                    ft, tb = g // 2, g % 2
                    cT.wait(e, mmd(g))
                    cS.guard(e)
                    ins = e.activation(qn[:, ft * 1024 + tb * 512:ft * 1024 + tb * 512 + 512],
                                       ps4[:, (g % 4) * 512:(g % 4) * 512 + 512],
                                       AF.Copy, scale=A_SCALE)
                    cS.tick(ins)

            pb.emit(ph)
            cT.check(32)
            cS.check(32)

        ar.freev("wqbN", "wqbN2")
        qpet = ar.alloc("qpet", 8 * 1024)
        qpet2 = ar.alloc("qpet2", 8 * 1024)
        qper = ar.alloc("qper", 8 * 1024)
        ropeq = ar.alloc("ropeq", 4 * 256, F32)
        with (
            nc.psum_tensor("a5_ps", [128, 1024], F32) as ps5,
            nc.psum_tensor("a5_tp", [128, 2 * 1024], BF16) as tpsq,
        ):
            ph = Phase("a5")
            cT = Clk(pb, "a5T")
            cS = Clk(pb, "a5S")
            cV = Clk(pb, "a5V")
            d5 = pb.bases("a5w", "a3k")

            def mmd(g): return cT.base + g + 1
            def tpd(j): return cT.base + 16 + j + 1
            def ktd(j): return cT.base + 80 + j + 1
            def evd(g): return cS.base + g + 1
            def rod(h): return cV.base + 6 * (h + 1)
            def ted(j): return cV.base + 96 + j + 1
            def kted(j): return cV.base + 160 + j + 1

            @ph.on("gpsimd")
            def _(e):
                pb.cw(e, "a3cc")
                for s in range(2):
                    for kt in range(4):
                        ins = e.dma_start(
                            out=ckvn[:, kt * 2048 + s * 1024:kt * 2048 + s * 1024 + 1024],
                            in_=w["cc2_out"][s * 128:(s + 1) * 128, kt * 1024:(kt + 1) * 1024])
                        pb.inc(ins, "a6w", 16)
                    ins = e.dma_start(out=kpet2f[:, s * 512:(s + 1) * 512],
                                      in_=w["cc2_out"][s * 128:(s + 1) * 128, 4096:4608])
                    pb.inc(ins, "a6w", 16)

            @ph.on("sync")
            def _(e):
                cV.wait(e, kted(15))
                pb.inc(e.dma_start(out=kper[64:128, 0:2048], in_=kper[0:64, 0:2048]), "a3k", 16)
                e.wait_ge(pb.sem("a3k"), d5["a3k"] + 16)

            @ph.on("tensor")
            def _(e):
                e.wait_ge(pb.sem("a5w"), d5["a5w"])
                for tt in range(8):
                    for pbk in range(2):
                        g = tt * 2 + pbk
                        if g >= 2:
                            cS.wait(e, evd(g - 2))
                        for kt in range(12):
                            ins = e.matmul(ps5[:, (g % 2) * 512:(g % 2) * 512 + 512],
                                           cq[:, kt * 1024 + tt * 128:kt * 1024 + tt * 128 + 128],
                                           wqbP[:, kt * 1024 + pbk * 512:kt * 1024 + pbk * 512 + 512],
                                           start=(kt == 0), stop=(kt == 11))
                        cT.tick(ins)
                for tt in range(8):
                    for hp in range(8):
                        j = tt * 8 + hp
                        cV.wait(e, rod(2 * hp + 1))
                        if j >= 2:
                            cV.wait(e, ted(j - 2))
                        ins = e.transpose(tpsq[:, (j % 2) * 1024:(j % 2) * 1024 + 128],
                                          qpet2[:, tt * 1024 + hp * 128:tt * 1024 + (hp + 1) * 128],
                                          ident[:, :])
                        cT.tick(ins)
                pb.cw(e, "a6w")
                for ct in range(16):
                    if ct >= 2:
                        cV.wait(e, kted(ct - 2))
                    else:
                        cV.wait(e, ted(62 + ct))
                    ins = e.transpose(tpsq[0:64, (ct % 2) * 1024:(ct % 2) * 1024 + 128],
                                      kpet2f[:, ct * 64:(ct + 1) * 64], ident[:, :])
                    cT.tick(ins)

            @ph.on("scalar")
            def _(e):
                for g in range(16):
                    tt, pbk = g // 2, g % 2
                    cT.wait(e, mmd(g))
                    cS.guard(e)
                    ins = e.activation(qpet[:, tt * 1024 + pbk * 512:tt * 1024 + pbk * 512 + 512],
                                       ps5[:, (g % 2) * 512:(g % 2) * 512 + 512],
                                       AF.Copy, scale=A_SCALE)
                    cS.tick(ins)

            @ph.on("vector")
            def _(e):
                for h in range(16):
                    cS.wait(e, evd(15))
                    lo = qpet.sap(h * 64, 8, 1024, 32)
                    hi = qpet.sap(h * 64 + 32, 8, 1024, 32)
                    lo2 = qpet2.sap(h * 64, 8, 1024, 32)
                    hi2 = qpet2.sap(h * 64 + 32, 8, 1024, 32)
                    c0 = cosq.sap(0, 8, 64, 32)
                    c1 = cosq.sap(32, 8, 64, 32)
                    s0 = sinq.sap(0, 8, 64, 32)
                    s1 = sinq.sap(32, 8, 64, 32)
                    tA = ropeq.sap(0, 8, 32, 32)
                    tB = ropeq.sap(256, 8, 32, 32)
                    cV.guard(e)
                    cV.tick(e.tensor_mul(tA, lo, c0))
                    cV.guard(e)
                    cV.tick(e.tensor_mul(tB, hi, s0))
                    cV.guard(e)
                    cV.tick(e.tensor_sub(lo2, tA, tB))
                    cV.guard(e)
                    cV.tick(e.tensor_mul(tA, hi, c1))
                    cV.guard(e)
                    cV.tick(e.tensor_mul(tB, lo, s1))
                    cV.guard(e)
                    cV.tick(e.tensor_add(hi2, tA, tB))
                for tt in range(8):
                    for hp in range(8):
                        j = tt * 8 + hp
                        cT.wait(e, tpd(j))
                        cV.guard(e)
                        cV.tick(e.tensor_copy(qper[:, hp * 1024 + tt * 128:hp * 1024 + (tt + 1) * 128],
                                              tpsq[:, (j % 2) * 1024:(j % 2) * 1024 + 128]))
                for ct in range(16):
                    cT.wait(e, ktd(ct))
                    cV.guard(e)
                    cV.tick(e.tensor_copy(kper[0:64, ct * 128:(ct + 1) * 128],
                                          tpsq[0:64, (ct % 2) * 1024:(ct % 2) * 1024 + 128]))

            pb.emit(ph)
            cT.check(16 + 64 + 16)
            cS.check(16)
            cV.check(96 + 64 + 16)

        ar.freev("cq", "wqbP", "qpet", "qpet2", "ropeq", "kpet2f", "cosq", "sinq")
        o_fm = ar.alloc("o_fm", 16 * 1024)
        wkbK = ar.alloc("wkbK", 4 * 2048)
        wkbV = ar.alloc("wkbV", 4 * 2048)
        knop = ar.alloc("knop", 2048)
        vh = ar.alloc("vh", 2 * 2048)
        probs = ar.alloc("probs", 2 * 2048)
        probsT = ar.alloc("probsT", 2 * 2048)
        mskb0 = ar.alloc("mskb0", 2048)
        mskb1 = ar.alloc("mskb1", 2048)
        mskb_sl = [mskb0, mskb1]
        sc_m = ar.alloc("sc_m", 2 * 2048, F32)
        nmx = ar.alloc("nmx", 16, F32)
        with (
            nc.psum_tensor("a6_psA", [128, 1024], F32) as psA,
            nc.psum_tensor("a6_tp", [128, 4 * 1024], BF16) as tps6,
            nc.psum_tensor("a6_av", [128, 1024], F32) as psav,
        ):
            ph = Phase("a6")
            cT = Clk(pb, "a6T")
            cS = Clk(pb, "a6S")
            cV = Clk(pb, "a6V")
            d6 = pb.bases("a6w", "a6m0", "a6m1")

            def hq(n):
                return n // 8, n % 8

            Titems, Sitems, Vitems = [], [], []
            for h in range(H):
                Titems += [("kn", h, nb) for nb in range(4)]
                Titems += [("vg", h, ct) for ct in range(16)]
                Sitems += [("knev", h, nb) for nb in range(4)]
                Sitems += [("vev", h, b) for b in range(4)]
                for qt in range(NT):
                    n = h * 8 + qt
                    if n >= 2:
                        Titems += [("sc", n, 0), ("sc", n, 1)]
                        if n >= 4:
                            Titems.append(("av", n - 4))
                        Titems += [("tp", n - 2, k) for k in range(4)]
                        Titems += [("sc", n, 2), ("sc", n, 3)]
                        Titems += [("tp", n - 2, k) for k in range(4, 16)]
                    else:
                        Titems += [("sc", n, kb) for kb in range(4)]
                    if n >= 2:
                        Sitems += [("pte", n - 2, j) for j in range(4)]
                    if n >= 4:
                        Sitems.append(("oev", n - 4))
                    Sitems.append(("exp", n))
                    if n >= 2:
                        Vitems.append(("rcp", n - 2))
                        Vitems.append(("pml", n - 2))
                    Vitems += [("stt", n, kb) for kb in range(4)]
                    Vitems.append(("nmx", n))
            Titems += [("av", 124)] + [("tp", 126, k) for k in range(16)] + [("av", 125)] + [("tp", 127, k) for k in range(16)] + [("av", 126), ("av", 127)]
            Sitems += [("pte", 126, j) for j in range(4)] + [("pte", 127, j) for j in range(4)] + [("oev", n2) for n2 in (124, 125, 126, 127)]
            Vitems += [("rcp", 126), ("pml", 126), ("rcp", 127), ("pml", 127)]
            tmark = {it: i + 1 for i, it in enumerate(Titems)}
            smark = {it: i + 1 for i, it in enumerate(Sitems)}
            vmark = {it: i + 1 for i, it in enumerate(Vitems)}

            def tm(*it): return cT.base + tmark[it]
            def sm(*it): return cS.base + smark[it]
            def vm(*it): return cV.base + vmark[it]

            @ph.on("gpsimd")
            def _(e):
                pb.inc(e.dma_start(out=wkbK[:, :], in_=w["wkvbK"]), "a6w", 16)
                pb.inc(e.dma_start(out=wkbV[:, :], in_=w["wkvbV"]), "a6w", 16)

            @ph.on("sync")
            def _(e):
                for n in range(128):
                    h, qt = hq(n)
                    if n >= 2:
                        cV.wait(e, vm("stt", n - 2, 3))
                    ins = e.dma_start(out=mskb_sl[qt % 2][:, 0:2048], in_=w["mask"][qt])
                    pb.inc(ins, f"a6m{qt % 2}", 16)

            @ph.on("tensor")
            def _(e):
                e.wait_ge(pb.sem("a6w"), d6["a6w"] + 32)
                for it in Titems:
                    if it[0] == "kn":
                        _, h, nb = it
                        if h >= 1:
                            cV.wait(e, vm("stt", h * 8 - 1, 3))
                        if nb >= 2:
                            cS.wait(e, sm("knev", h, nb - 2))
                        for kt in range(4):
                            ins = e.matmul(psA[:, (nb % 2) * 512:(nb % 2) * 512 + 512],
                                           wkbK[:, kt * 2048 + h * 128:kt * 2048 + h * 128 + 128],
                                           ckvn[:, kt * 2048 + nb * 512:kt * 2048 + nb * 512 + 512],
                                           start=(kt == 0), stop=(kt == 3))
                        cT.tick(ins)
                    elif it[0] == "vg":
                        _, h, ct = it
                        if ct < 8:
                            cS.wait(e, sm("knev", h, 2 + ct // 4))
                        else:
                            cS.wait(e, sm("vev", h, (ct - 8) // 4))
                        sl6 = ((ct // 4) % 2) * 512 + (ct % 4) * 128
                        for kt in range(4):
                            ins = e.matmul(psA[:, sl6:sl6 + 128],
                                           ckvn[:, kt * 2048 + ct * 128:kt * 2048 + ct * 128 + 128],
                                           wkbV[:, kt * 2048 + h * 128:kt * 2048 + h * 128 + 128],
                                           start=(kt == 0), stop=(kt == 3))
                        cT.tick(ins)
                    elif it[0] == "sc":
                        _, n, kb = it
                        h, qt = hq(n)
                        hv = (h % 2) * 64
                        hqc = (h // 2) * 1024
                        if qt == 0:
                            cS.wait(e, sm("vev", h, 3))
                        elif kb < 2:
                            cV.wait(e, vm("stt", n - 1, kb + 2))
                        if kb >= 2:
                            cV.wait(e, vm("stt", n, kb - 2))
                        ins = e.matmul(psA[:, (kb % 2) * 512:(kb % 2) * 512 + 512],
                                       qn[:, h * 1024 + qt * 128:h * 1024 + qt * 128 + 128],
                                       knop[:, kb * 512:(kb + 1) * 512],
                                       start=True, stop=False)
                        ins = e.matmul(psA[:, (kb % 2) * 512:(kb % 2) * 512 + 512],
                                       qper[hv:hv + 64, hqc + qt * 128:hqc + qt * 128 + 128],
                                       kper[hv:hv + 64, kb * 512:(kb + 1) * 512],
                                       start=False, stop=True)
                        cT.tick(ins)
                    elif it[0] == "tp":
                        _, n, k = it
                        h, qt = hq(n)
                        cp = (qt % 2) * 2048
                        cV.wait(e, vm("pml", n))
                        if k >= 4:
                            cS.wait(e, sm("pte", n, (k - 4) // 4))
                        elif n >= 1:
                            cS.wait(e, sm("pte", n - 1, 3))
                        ins = e.transpose(tps6[:, (k % 4) * 1024:(k % 4) * 1024 + 128],
                                          probs[:, cp + k * 128:cp + (k + 1) * 128], ident[:, :])
                        cT.tick(ins)
                    else:
                        n = it[1]
                        h, qt = hq(n)
                        cp = (qt % 2) * 2048
                        cS.wait(e, sm("pte", n, 3))
                        if n >= 2:
                            cS.wait(e, sm("oev", n - 2))
                        for kt in range(16):
                            ins = e.matmul(psav[:, (qt % 2) * 512:(qt % 2) * 512 + 128],
                                           vh[:, (h % 2) * 2048 + kt * 128:(h % 2) * 2048 + (kt + 1) * 128],
                                           probsT[:, cp + kt * 128:cp + (kt + 1) * 128],
                                           start=(kt == 0), stop=(kt == 15))
                        cT.tick(ins)

            @ph.on("scalar")
            def _(e):
                for it in Sitems:
                    if it[0] == "knev":
                        _, h, nb = it
                        cT.wait(e, tm("kn", h, nb))
                        cS.guard(e)
                        cS.tick(e.copy(knop[:, nb * 512:(nb + 1) * 512],
                                       psA[:, (nb % 2) * 512:(nb % 2) * 512 + 512]))
                    elif it[0] == "vev":
                        _, h, b = it
                        cT.wait(e, tm("vg", h, 4 * b + 3))
                        cS.guard(e)
                        cS.tick(e.copy(vh[:, (h % 2) * 2048 + b * 512:(h % 2) * 2048 + (b + 1) * 512],
                                       psA[:, (b % 2) * 512:(b % 2) * 512 + 512]))
                    elif it[0] == "exp":
                        n = it[1]
                        h, qt = hq(n)
                        c_ = qt % 2
                        cp = c_ * 2048
                        cV.wait(e, vm("nmx", n))
                        if n >= 2:
                            cT.wait(e, tm("tp", n - 2, 15))
                        cS.guard(e)
                        cS.tick(e.activation(probs[:, cp:cp + 2048], sc_m[:, cp:cp + 2048], AF.Exp,
                                             bias=nmx[:, c_:c_ + 1], accum_out=nmx[:, 2 + c_:3 + c_]))
                    elif it[0] == "pte":
                        _, n, j = it
                        h, qt = hq(n)
                        cp = (qt % 2) * 2048
                        cT.wait(e, tm("tp", n, 4 * j + 3))
                        cS.guard(e)
                        src = bass.AP(tps6, 0, [[4096, 128], [1024, 4], [1, 128]])
                        dst = bass.AP(probsT.t, probsT.off + cp + j * 512,
                                      [[probsT.tcols, 128], [128, 4], [1, 128]])
                        cS.tick(e.copy(dst, src))
                    else:
                        n = it[1]
                        h, qt = hq(n)
                        cT.wait(e, tm("av", n))
                        cS.guard(e)
                        cS.tick(e.copy(o_fm[:, h * 1024 + qt * 128:h * 1024 + qt * 128 + 128],
                                       psav[:, (qt % 2) * 512:(qt % 2) * 512 + 128]))

            @ph.on("vector")
            def _(e):
                for it in Vitems:
                    if it[0] == "stt":
                        _, n, kb = it
                        h, qt = hq(n)
                        c_ = qt % 2
                        cp = c_ * 2048
                        if kb == 0:
                            e.wait_ge(pb.sem(f"a6m{qt % 2}"), d6[f"a6m{qt % 2}"] + 16 * (n // 2 + 1))
                            if n >= 2:
                                cS.wait(e, sm("exp", n - 2))
                        cT.wait(e, tm("sc", n, kb))
                        cV.guard(e)
                        ins = e.scalar_tensor_tensor(
                            sc_m[:, cp + kb * 512:cp + (kb + 1) * 512],
                            psA[:, (kb % 2) * 512:(kb % 2) * 512 + 512], 1.0,
                            mskb_sl[c_][:, kb * 512:kb * 512 + 512],
                            ALU.mult, ALU.add)
                        cV.tick(ins)
                    elif it[0] == "nmx":
                        n = it[1]
                        h, qt = hq(n)
                        c_ = qt % 2
                        cp = c_ * 2048
                        cV.guard(e)
                        cV.tick(e.reduce_max(nmx[:, c_:c_ + 1], sc_m[:, cp:cp + 2048],
                                             AX.X, negate=True))
                    elif it[0] == "rcp":
                        n = it[1]
                        c_ = (n % 8) % 2
                        cS.wait(e, sm("exp", n))
                        cV.guard(e)
                        cV.tick(e.reciprocal(nmx[:, 4 + c_:5 + c_], nmx[:, 2 + c_:3 + c_]))
                    else:
                        n = it[1]
                        c_ = (n % 8) % 2
                        cp = c_ * 2048
                        cV.guard(e)
                        cV.tick(e.tensor_scalar_mul(probs[:, cp:cp + 2048], probs[:, cp:cp + 2048],
                                                    nmx[:, 4 + c_:5 + c_]))

            pb.emit(ph)
            cT.check(len(Titems))
            cS.check(len(Sitems))
            cV.check(len(Vitems))

        ar.freev("wkbK", "wkbV", "knop", "vh", "probs", "probsT", "mskb0", "mskb1", "sc_m", "nmx",
                 "ckvn", "kper", "qn", "qper")
        wo_sb = ar.alloc("wo_sb", 16 * 2048)
        rtx = ar.alloc("rtx", 2 * D, F32)
        rox = ar.alloc("rox", 2 * D, F32)
        with nc.psum_tensor("a7_ps", [128, 2048], F32) as ps7:
            ph = Phase("a7")
            cT = Clk(pb, "a7T")
            cV = Clk(pb, "a7V")
            d7 = pb.bases("a7w", "a7x0", "a7x1", "a7o0", "a7o1")

            def mmd(g): return cT.base + g + 1
            def add_(tt, db): return cV.base + tt * 4 + db + 1

            @ph.on("gpsimd")
            def _(e):
                pb.inc(e.dma_start(out=wo_sb[:, :], in_=w["wo"]), "a7w", 16)

            @ph.on("sync")
            def _(e):
                for tt in range(NT):
                    if tt >= 2:
                        cV.wait(e, add_(tt - 2, 3))
                        ins = e.dma_start(out=dst_r[(tt - 2) * 128:(tt - 1) * 128, :],
                                          in_=rox[:, (tt % 2) * D:(tt % 2 + 1) * D])
                        pb.inc(ins, f"a7o{tt % 2}", 16)
                    ins = e.dma_start(out=rtx[:, (tt % 2) * D:(tt % 2 + 1) * D],
                                      in_=w["x_own"][tt * 128:(tt + 1) * 128, :])
                    pb.inc(ins, f"a7x{tt % 2}", 16)
                for tt in range(NT - 2, NT):
                    cV.wait(e, add_(tt, 3))
                    ins = e.dma_start(out=dst_r[tt * 128:(tt + 1) * 128, :],
                                      in_=rox[:, (tt % 2) * D:(tt % 2 + 1) * D])
                    pb.inc(ins, f"a7o{tt % 2}", 16)
                e.wait_ge(pb.sem("a7o0"), d7["a7o0"] + 16 * 4)
                e.wait_ge(pb.sem("a7o1"), d7["a7o1"] + 16 * 4)

            @ph.on("tensor")
            def _(e):
                e.wait_ge(pb.sem("a7w"), d7["a7w"] + 16)
                for tt in range(NT):
                    for db in range(4):
                        g = tt * 4 + db
                        if g >= 4:
                            cV.wait(e, add_((g - 4) // 4, (g - 4) % 4))
                        for kt in range(16):
                            ins = e.matmul(ps7[:, (g % 4) * 512:(g % 4) * 512 + 512],
                                           o_fm[:, kt * 1024 + tt * 128:kt * 1024 + tt * 128 + 128],
                                           wo_sb[:, kt * 2048 + db * 512:kt * 2048 + db * 512 + 512],
                                           start=(kt == 0), stop=(kt == 15))
                        cT.tick(ins)

            @ph.on("vector")
            def _(e):
                for tt in range(NT):
                    for db in range(4):
                        g = tt * 4 + db
                        cT.wait(e, mmd(g))
                        e.wait_ge(pb.sem(f"a7x{tt % 2}"), d7[f"a7x{tt % 2}"] + 16 * (tt // 2 + 1))
                        if tt >= 2 and db == 0:
                            e.wait_ge(pb.sem(f"a7o{tt % 2}"), d7[f"a7o{tt % 2}"] + 16 * (tt // 2))
                        cV.guard(e)
                        ins = e.tensor_add(rox[:, (tt % 2) * D + db * 512:(tt % 2) * D + db * 512 + 512],
                                           rtx[:, (tt % 2) * D + db * 512:(tt % 2) * D + db * 512 + 512],
                                           ps7[:, (g % 4) * 512:(g % 4) * 512 + 512])
                        cV.tick(ins)

            pb.emit(ph)
            cT.check(32)
            cV.check(32)

    ar.freev("wo_sb", "rtx", "rox", "o_fm")
    ar.close()


B_EPS = 1e-5
CH = 128
NCH = 8


def _b_dram(nc, pairs=4):
    d = dict(
        wq=nc.dram_tensor("bwq_img", [4, 128, 16 * 512], F32, kind="ExternalInput").ap(),
        wk=nc.dram_tensor("bwk_img", [4, 128, 16 * 512], F32, kind="ExternalInput").ap(),
        wv=nc.dram_tensor("bwv_img", [4, 128, 16 * 512], F32, kind="ExternalInput").ap(),
        wo=nc.dram_tensor("bwo_img", [128, 16 * 2048], F32, kind="ExternalInput").ap(),
        wg1=nc.dram_tensor("bwg1_img", [128, 16 * 128], F32, kind="ExternalInput").ap(),
        wg2=nc.dram_tensor("bwg2_img", [128, 2048], F32, kind="ExternalInput").ap(),
        nw=nc.dram_tensor("bnw_img", [128, 16], F32, kind="ExternalInput").ap(),
        ldr=nc.dram_tensor("bld_img", [1, 16], F32, kind="ExternalInput").ap(),
        diffX=nc.dram_tensor("bdiff_img", [128, 128], F32, kind="ExternalInput").ap(),
        iota1=nc.dram_tensor("biota_img", [128, 128], F32, kind="ExternalInput").ap(),
        revj=nc.dram_tensor("brevj_img", [128, 1], F32, kind="ExternalInput").ap(),
        csteps=nc.dram_tensor("bcst_img", [128, 8], F32, kind="ExternalInput").ap(),
        flag=nc.dram_tensor("bflag_img", [128, 1], F32, kind="ExternalInput").ap(),
        cc_in=nc.dram_tensor("bcc_in", [128, 2048], F32).ap(),
        cc_out=nc.dram_tensor("bcc_out", [256, 2048], F32).ap(),
    )
    d["pairs"] = [[2 * i, 2 * i + 1] for i in range(pairs)]
    return d


def _stage_attn_b(inputs, core):
    hf = core % 2
    m = {}
    for nm, key in (("bwq_img", "L0_attn_b_Wq"), ("bwk_img", "L0_attn_b_Wk"),
                    ("bwv_img", "L0_attn_b_Wv")):
        W = inputs[key]
        m[nm] = np.ascontiguousarray(
            W.reshape(16, 128, 4, 512).transpose(2, 1, 0, 3).reshape(4, 128, 16 * 512))
    m["bwo_img"] = np.ascontiguousarray(
        inputs["L0_attn_b_Wo"].reshape(16, 128, 2048).transpose(1, 0, 2).reshape(128, 16 * 2048))
    m["bwg1_img"] = np.ascontiguousarray(
        inputs["L0_attn_b_Wg1"].reshape(16, 128, 128).transpose(1, 0, 2).reshape(128, 2048))
    m["bwg2_img"] = np.ascontiguousarray(inputs["L0_attn_b_Wg2"])
    m["bnw_img"] = np.ascontiguousarray(
        inputs["L0_attn_b_norm_w"].reshape(16, 128).transpose(1, 0))
    m["bld_img"] = np.ascontiguousarray(inputs["L0_attn_b_log_decay"].reshape(1, 16))
    i_ = np.arange(128)
    dif = i_[None, :] - i_[:, None]
    m["bdiff_img"] = np.where(dif >= 0, dif, 1e30).astype(np.float32)
    m["biota_img"] = np.broadcast_to((i_ + 1).astype(np.float32)[None, :], (128, 128)).copy()
    m["brevj_img"] = (127.0 - i_).astype(np.float32).reshape(128, 1).copy()
    m["bcst_img"] = np.broadcast_to((128.0 * np.arange(8)).astype(np.float32)[None, :],
                                    (128, 8)).copy()
    m["bflag_img"] = np.full((128, 1), float(hf), np.float32)
    return m


def emit_attn_b(pb, cn, w, src_r, dst_r):
    nc = pb.nc
    ar = Arena(nc, "bb", 87040, 9216)
    ident = cn["ident"]
    ones_bf = cn["ones_bf"]
    ones_f = cn["ones_f"]

    x_fm = ar.alloc("x_fm", 16 * 1024)
    wpan = ar.alloc("wpan", 2 * 8192)
    wg1s = ar.alloc("wg1s", 2048)
    xln = ar.alloc("xln", 2 * D)
    rt = ar.alloc("rt", 3 * D, F32)
    tpsg = nc.psum_tensor("bb_tps", [128, 8 * 1024], BF16)
    tps = tpsg.__enter__()
    with nc.sbuf_tensor("bln_st", [128, 16], F32) as stt_:
        sb = dict(rt=rt, st=stt_, xln=xln, tps=tps, ident=ident)
        ph = Phase("c1")
        d2 = pb.bases("c2d0", "c2d1", "c2g")

        @ph.on("gpsimd")
        def _(e):
            pb.inc(e.dma_start(out=wg1s[:, :], in_=w["wg1"]), "c2g", 16)
            pb.inc(e.dma_start(out=wpan[:, 0:8192], in_=w["wq"][0]), "c2d0", 16)

        tiles = [src_r[i * 128:(i + 1) * 128, :] for i in range(NT)]
        emit_ln_fm(pb, ph, "bln", tiles, x_fm, 0, T, sb)
        pb.emit(ph)
        for f in ph.fns.get("_check", []):
            f()
    tpsg.__exit__(None, None, None)
    ar.freev("xln", "rt")

    q_tm = ar.alloc("q_tm", 16384)
    k_tm = ar.alloc("k_tm", 16384)
    v_tm = ar.alloc("v_tm", 16384)
    g1 = ar.alloc("g1", 1024)
    with nc.psum_tensor("c2_ps", [128, 2048], F32) as ps2:
        ph = Phase("c2")
        cT = Clk(pb, "c2T")
        cS = Clk(pb, "c2S")
        tms = [q_tm, k_tm, v_tm]

        def mmd(g): return cT.base + g + 1
        def evd(g): return cS.base + g + 1

        @ph.on("gpsimd")
        def _(e):
            for pi, wkey in enumerate(("wq", "wk", "wv")):
                for fb in range(4):
                    u = pi * 4 + fb
                    if u == 0:
                        continue
                    if u >= 2:
                        cT.wait(e, mmd((u - 1) * 8))
                    ins = e.dma_start(out=wpan[:, (u % 2) * 8192:(u % 2 + 1) * 8192],
                                      in_=w[wkey][fb])
                    pb.inc(ins, f"c2d{u % 2}", 16)

        @ph.on("tensor")
        def _(e):
            for pi in range(3):
                for fb in range(4):
                    u = pi * 4 + fb
                    e.wait_ge(pb.sem(f"c2d{u % 2}"), d2[f"c2d{u % 2}"] + 16 * (u // 2 + 1))
                    for tt in range(NT):
                        g = u * 8 + tt
                        if g >= 4:
                            cS.wait(e, evd(g - 4))
                        for kt in range(16):
                            ins = e.matmul(
                                ps2[:, (g % 4) * 512:(g % 4) * 512 + 512],
                                x_fm[:, kt * T + tt * 128:kt * T + tt * 128 + 128],
                                wpan[:, (u % 2) * 8192 + kt * 512:(u % 2) * 8192 + (kt + 1) * 512],
                                start=(kt == 0), stop=(kt == 15))
                        cT.tick(ins)
            e.wait_ge(pb.sem("c2g"), d2["c2g"] + 16)
            for tb in range(2):
                g = 96 + tb
                cS.wait(e, evd(g - 4))
                for kt in range(16):
                    ins = e.matmul(ps2[:, (g % 4) * 512:(g % 4) * 512 + 512],
                                   wg1s[:, kt * 128:(kt + 1) * 128],
                                   x_fm[:, kt * T + tb * 512:kt * T + tb * 512 + 512],
                                   start=(kt == 0), stop=(kt == 15))
                cT.tick(ins)

        @ph.on("scalar")
        def _(e):
            for g in range(96):
                u, tt = g // 8, g % 8
                pi, fb = u // 4, u % 4
                cT.wait(e, mmd(g))
                cS.guard(e)
                ins = e.activation(tms[pi][:, tt * 2048 + fb * 512:tt * 2048 + fb * 512 + 512],
                                   ps2[:, (g % 4) * 512:(g % 4) * 512 + 512], AF.Silu)
                cS.tick(ins)
            for tb in range(2):
                g = 96 + tb
                cT.wait(e, mmd(g))
                cS.guard(e)
                cS.tick(e.copy(g1[:, tb * 512:tb * 512 + 512],
                               ps2[:, (g % 4) * 512:(g % 4) * 512 + 512]))

        pb.emit(ph)
        cT.check(98)
        cS.check(98)
    ar.freev("x_fm", "wpan", "wg1s")

    q_fm = ar.alloc("q_fm", 16384)
    k_fm = ar.alloc("k_fm", 16384)
    with nc.psum_tensor("c3_tp", [128, 8 * 1024], BF16) as tps3:
        ph = Phase("c3")
        cT = Clk(pb, "c3T")
        cV = Clk(pb, "c3V")

        def tpd(j): return cT.base + j + 1
        def evd(b): return cV.base + b + 1

        @ph.on("tensor")
        def _(e):
            for j in range(256):
                pi = j // 128
                c = (j % 128) // 16
                h = j % 16
                if j >= 8:
                    cV.wait(e, evd((j - 8) // 8))
                src = (q_tm if pi == 0 else k_tm)
                ins = e.transpose(tps3[:, (j % 8) * 1024:(j % 8) * 1024 + 128],
                                  src[:, c * 2048 + h * 128:c * 2048 + (h + 1) * 128],
                                  ident[:, :])
                cT.tick(ins)

        @ph.on("vector")
        def _(e):
            for b in range(32):
                j0 = b * 8
                pi = j0 // 128
                c = (j0 % 128) // 16
                h0 = j0 % 16
                cT.wait(e, tpd(j0 + 7))
                dst = (q_fm if pi == 0 else k_fm)
                src = bass.AP(tps3, 0, [[8 * 1024, 128], [1024, 8], [1, 128]])
                dstap = bass.AP(dst.t, dst.off + h0 * 1024 + c * 128,
                                [[dst.tcols, 128], [1024, 8], [1, 128]])
                cV.guard(e)
                cV.tick(e.tensor_copy(dstap, src))

        pb.emit(ph)
        cT.check(256)
        cV.check(32)
    ar.freev("q_tm")

    ldb = ar.alloc("ldb", 16, F32)
    DTt = ar.alloc("DTt", 2048, F32)
    qdecb = ar.alloc("qdecb", 2048, F32)
    kdecs = ar.alloc("kdecs", 16, F32)
    cdpow = ar.alloc("cdpow", 128, F32)
    corrpow = ar.alloc("corrpow", 128, F32)
    diffX = ar.alloc("diffX", 128, F32)
    iota1 = ar.alloc("iota1", 128, F32)
    revj = ar.alloc("revj", 1, F32)
    csteps = ar.alloc("csteps", 8, F32)
    flag = ar.alloc("flag", 1, F32)
    ldrow = ar.alloc("ldrow", 16, F32)
    with nc.psum_tensor("c4_ps", [128, 16], F32) as ps4:
        ph = Phase("c4")
        cT = Clk(pb, "c4T")
        cS = Clk(pb, "c4S")
        cV = Clk(pb, "c4V")
        d4 = pb.bases("c4d")

        @ph.on("sync")
        def _(e):
            for dst, src in ((diffX, w["diffX"]), (iota1, w["iota1"]), (revj, w["revj"]),
                             (csteps, w["csteps"]), (flag, w["flag"])):
                pb.inc(e.dma_start(out=dst[:, :], in_=src), "c4d", 16)
            pb.inc(e.dma_start(out=ldrow[0:1, 0:16], in_=w["ldr"]), "c4d", 16)

        @ph.on("tensor")
        def _(e):
            e.wait_ge(pb.sem("c4d"), d4["c4d"] + 96)
            ins = e.matmul(ps4[:, 0:16], ones_f[:], ldrow[0:1, 0:16], start=True, stop=True)
            cT.tick(ins)

        @ph.on("vector")
        def _(e):
            cT.wait(e, cT.base + 1)
            cV.guard(e)
            cV.tick(e.tensor_copy(ldb[:, :], ps4[:, 0:16]))
            cV.guard(e)
            cV.tick(e.tensor_scalar_mul(kdecs[:, :], ldb[:, :], revj[:, 0:1]))

        @ph.on("scalar")
        def _(e):
            cV.wait(e, cV.base + 2)
            for h in range(16):
                cS.guard(e)
                cS.tick(e.activation(DTt[:, h * 128:(h + 1) * 128], diffX[:, :], AF.Exp,
                                     scale=ldb[:, h:h + 1]))
                cS.guard(e)
                cS.tick(e.activation(qdecb[:, h * 128:(h + 1) * 128], iota1[:, :], AF.Exp,
                                     scale=ldb[:, h:h + 1]))
                cS.guard(e)
                cS.tick(e.activation(cdpow[:, h * 8:(h + 1) * 8], csteps[:, :], AF.Exp,
                                     scale=ldb[:, h:h + 1]))
            cS.guard(e)
            cS.tick(e.activation(kdecs[:, :], kdecs[:, :], AF.Exp))

        @ph.on("vector")
        def _(e):
            cS.wait(e, cS.base + 49)
            cV.guard(e)
            cV.tick(e.tensor_scalar_mul(corrpow[:, :], cdpow[:, :], flag[:, 0:1]))

        pb.emit(ph)
        cT.check(1)
        cS.check(49)
        cV.check(3)

    o_b = ar.alloc("o_b", 16384)
    S_f = ar.alloc("S_f", 2048, F32)
    S_bf = ar.alloc("S_bf", 2048)
    ptm = ar.alloc("ptm", 256)
    vkb = ar.alloc("vkb", 256)
    otmp = ar.alloc("otmp", 256, F32)
    with (
        nc.psum_tensor("c5_pp", [128, 1024], F32) as pp,
        nc.psum_tensor("c5_po", [128, 1024], F32) as po,
        nc.psum_tensor("c5_p2", [128, 1024], F32) as p2,
        nc.psum_tensor("c5_psd", [128, 1024], F32) as psd,
    ):
        ph = Phase("c5")
        cT = Clk(pb, "c5T")
        cV = Clk(pb, "c5V")
        cZ = Clk(pb, "c5Z")

        Tseq, Vseq = [], []
        for n in range(128):
            Tseq.append(("pt", n))
            if n >= 1:
                Tseq += [("in", n - 1), ("ie", n - 1), ("sa", n - 1)]
            Vseq += [("vk", n), ("pm", n)]
            if n >= 1:
                Vseq += [("om", n - 1), ("oa", n - 1), ("su", n - 1), ("sb", n - 1)]
        Tseq += [("in", 127), ("ie", 127), ("sa", 127)]
        Vseq += [("om", 127), ("oa", 127), ("su", 127), ("sb", 127)]
        tmk = {k: i + 1 for i, k in enumerate(Tseq)}
        vmk = {k: i + 1 for i, k in enumerate(Vseq)}

        def tmark(*k): return cT.base + tmk[k]
        def vmark(*k): return cV.base + vmk[k]

        @ph.on("gpsimd")
        def _(e):
            cZ.tick(e.memset(S_f[:, :], 0.0))
            cZ.tick(e.memset(S_bf[:, :], 0.0))

        @ph.on("tensor")
        def _(e):
            for it in Tseq:
                kind, m = it
                c, h = m // 16, m % 16
                sl = (m % 2) * 512
                qs = q_fm[:, h * 1024 + c * 128:h * 1024 + (c + 1) * 128]
                ks_f = k_fm[:, h * 1024 + c * 128:h * 1024 + (c + 1) * 128]
                ks_t = k_tm[:, c * 2048 + h * 128:c * 2048 + (h + 1) * 128]
                vs_t = v_tm[:, c * 2048 + h * 128:c * 2048 + (h + 1) * 128]
                if kind == "pt":
                    if m >= 2:
                        cV.wait(e, vmark("pm", m - 2))
                    cT.tick(e.matmul(pp[:, sl:sl + 128], ks_f, qs, start=True, stop=True))
                elif kind == "in":
                    cV.wait(e, vmark("pm", m))
                    if m >= 2:
                        cV.wait(e, vmark("oa", m - 2))
                    cT.tick(e.matmul(po[:, sl:sl + 128], vs_t, ptm[:, sl // 4:sl // 4 + 128],
                                     start=True, stop=True))
                elif kind == "ie":
                    if m >= 2:
                        cV.wait(e, vmark("om", m - 2))
                    if c == 0:
                        cZ.wait(e, cZ.base + 2)
                    else:
                        cV.wait(e, vmark("sb", (c - 1) * 16 + h))
                    cT.tick(e.matmul(p2[:, sl:sl + 128], S_bf[:, h * 128:(h + 1) * 128], qs,
                                     start=True, stop=True))
                else:
                    cV.wait(e, vmark("vk", m))
                    if m >= 2:
                        cV.wait(e, vmark("su", m - 2))
                    cT.tick(e.matmul(psd[:, sl:sl + 128], ks_t, vkb[:, sl // 4:sl // 4 + 128],
                                     start=True, stop=True))

        @ph.on("vector")
        def _(e):
            for it in Vseq:
                kind, m = it
                c, h = m // 16, m % 16
                sl = (m % 2) * 512
                sv = (m % 2) * 128
                vs_t = v_tm[:, c * 2048 + h * 128:c * 2048 + (h + 1) * 128]
                if kind == "vk":
                    if m >= 2:
                        cT.wait(e, tmark("sa", m - 2))
                    cV.guard(e)
                    cV.tick(e.tensor_scalar_mul(vkb[:, sv:sv + 128], vs_t, kdecs[:, h:h + 1]))
                elif kind == "pm":
                    cT.wait(e, tmark("pt", m))
                    if m >= 2:
                        cT.wait(e, tmark("in", m - 2))
                    cV.guard(e)
                    cV.tick(e.tensor_mul(ptm[:, sv:sv + 128], pp[:, sl:sl + 128],
                                         DTt[:, h * 128:(h + 1) * 128]))
                elif kind == "om":
                    cT.wait(e, tmark("ie", m))
                    cV.guard(e)
                    cV.tick(e.tensor_mul(otmp[:, sv:sv + 128], p2[:, sl:sl + 128],
                                         qdecb[:, h * 128:(h + 1) * 128]))
                elif kind == "oa":
                    cT.wait(e, tmark("in", m))
                    cV.guard(e)
                    cV.tick(e.tensor_add(o_b[:, h * 1024 + c * 128:h * 1024 + (c + 1) * 128],
                                         otmp[:, sv:sv + 128], po[:, sl:sl + 128]))
                elif kind == "su":
                    cT.wait(e, tmark("sa", m))
                    if c == 0:
                        cZ.wait(e, cZ.base + 2)
                    cV.guard(e)
                    cV.tick(e.scalar_tensor_tensor(S_f[:, h * 128:(h + 1) * 128],
                                                   S_f[:, h * 128:(h + 1) * 128],
                                                   cdpow[:, h * 8 + 1:h * 8 + 2],
                                                   psd[:, sl:sl + 128], ALU.mult, ALU.add))
                else:
                    cV.guard(e)
                    cV.tick(e.tensor_copy(S_bf[:, h * 128:(h + 1) * 128],
                                          S_f[:, h * 128:(h + 1) * 128]))

        pb.emit(ph)
        cT.check(len(Tseq))
        cV.check(len(Vseq))
        cZ.check(2)

    ar.freev("k_tm", "k_fm", "ptm", "vkb", "otmp")
    sg = ar.alloc("sg", 16384)
    S0b = ar.alloc("S0b", 2048)
    S0f = ar.alloc("S0f", 2048, F32)
    ctmp = ar.alloc("ctmp", 256, F32)
    wg2s = ar.alloc("wg2s", 2048)
    with (
        nc.psum_tensor("c6_ps", [128, 1024], F32) as ps6,
        nc.psum_tensor("c6_pc", [128, 1024], F32) as pc6,
    ):
        ph = Phase("c6")
        cT = Clk(pb, "c6T")
        cS = Clk(pb, "c6S")
        cV = Clk(pb, "c6V")
        d6 = pb.bases("c6d", "c6cc", "c6g", "c6s0")

        def gmm(g): return cT.base + g + 1
        def cmm(n): return cT.base + 32 + n + 1
        def sgd(g): return cS.base + g + 1
        def s0d(): return cV.base + 1
        def com(n): return cV.base + 1 + 2 * n + 1
        def coa(n): return cV.base + 1 + 2 * n + 2

        @ph.on("sync")
        def _(e):
            pb.inc(e.dma_start(out=w["cc_in"], in_=S_f[:, :]), "c6d", 16)

        @ph.on("gpsimd")
        def _(e):
            pb.inc(e.dma_start(out=wg2s[:, :], in_=w["wg2"]), "c6g", 16)
            e.wait_ge(pb.sem("c6d"), d6["c6d"] + 16)
            ins = e.collective_compute(
                "AllGather", ALU.bypass, replica_groups=w["pairs"],
                ins=[w["cc_in"]], outs=[w["cc_out"]])
            pb.inc(ins, "c6cc", 1)
            e.wait_ge(pb.sem("c6cc"), d6["c6cc"] + 1)
            ins = e.dma_start(out=S0f[:, :], in_=w["cc_out"][0:128, :])
            pb.inc(ins, "c6s0", 16)

        @ph.on("tensor")
        def _(e):
            e.wait_ge(pb.sem("c6g"), d6["c6g"] + 16)
            for ft in range(16):
                for tb in range(2):
                    g = ft * 2 + tb
                    if g >= 2:
                        cS.wait(e, sgd(g - 2))
                    ins = e.matmul(ps6[:, (g % 2) * 512:(g % 2) * 512 + 512],
                                   wg2s[:, ft * 128:(ft + 1) * 128],
                                   g1[:, tb * 512:tb * 512 + 512],
                                   start=True, stop=True)
                    cT.tick(ins)
            cV.wait(e, s0d())
            for c in range(NCH):
                for h in range(H):
                    n = c * 16 + h
                    if n >= 2:
                        cV.wait(e, com(n - 2))
                    ins = e.matmul(pc6[:, (n % 2) * 512:(n % 2) * 512 + 128],
                                   S0b[:, h * 128:(h + 1) * 128],
                                   q_fm[:, h * 1024 + c * 128:h * 1024 + (c + 1) * 128],
                                   start=True, stop=True)
                    cT.tick(ins)

        @ph.on("scalar")
        def _(e):
            for g in range(32):
                ft, tb = g // 2, g % 2
                cT.wait(e, gmm(g))
                cS.guard(e)
                ins = e.activation(sg[:, ft * 1024 + tb * 512:ft * 1024 + tb * 512 + 512],
                                   ps6[:, (g % 2) * 512:(g % 2) * 512 + 512], AF.Sigmoid)
                cS.tick(ins)

        @ph.on("vector")
        def _(e):
            e.wait_ge(pb.sem("c6s0"), d6["c6s0"] + 16)
            cV.guard(e)
            cV.tick(e.tensor_copy(S0b[:, :], S0f[:, :]))
            for c in range(NCH):
                for h in range(H):
                    n = c * 16 + h
                    sl = (n % 2) * 512
                    sv = (n % 2) * 128
                    cT.wait(e, cmm(n))
                    cV.guard(e)
                    ins = e.scalar_tensor_tensor(ctmp[:, sv:sv + 128],
                                                 pc6[:, sl:sl + 128],
                                                 corrpow[:, h * 8 + c:h * 8 + c + 1],
                                                 qdecb[:, h * 128:(h + 1) * 128],
                                                 ALU.mult, ALU.mult)
                    cV.tick(ins)
                    ob = o_b[:, h * 1024 + c * 128:h * 1024 + (c + 1) * 128]
                    cV.guard(e)
                    cV.tick(e.tensor_add(ob, ob, ctmp[:, sv:sv + 128]))

        pb.emit(ph)
        cT.check(32 + 128)
        cS.check(32)
        cV.check(1 + 256)

    ar.freev("q_fm", "v_tm", "S_f", "S_bf", "S0b", "S0f", "ctmp", "wg2s",
             "DTt", "qdecb", "kdecs", "cdpow", "corrpow", "diffX", "iota1", "revj",
             "csteps", "flag", "ldrow", "ldb", "g1")
    sqb = ar.alloc("sqb", 1024)
    wo_sb = ar.alloc("wo_sb", 16 * 2048)
    rstdb = ar.alloc("rstdb", 1024, F32)
    with (
        nc.sbuf_tensor("c7_nw", [128, 16], F32) as nw,
        nc.psum_tensor("c7_ps", [128, 1024], F32) as ps7,
        nc.psum_tensor("c7_pb", [128, 1024], F32) as pb7,
    ):
        ph = Phase("c7")
        cT = Clk(pb, "c7T")
        cS = Clk(pb, "c7S")
        cV = Clk(pb, "c7V")
        d7 = pb.bases("c7d")

        @ph.on("gpsimd")
        def _(e):
            pb.inc(e.dma_start(out=wo_sb[:, :], in_=w["wo"]), "c8w", 16)

        def t_ss(h, tb): return cT.base + h * 4 + tb + 1
        def t_bc(h, tb): return cT.base + h * 4 + 2 + tb + 1
        def s_sq(h): return cS.base + h * 3 + 1
        def s_rs(h): return cS.base + h * 3 + 3
        def v_og(h, tb): return cV.base + h * 5 + tb + 1
        def v_eps(h): return cV.base + h * 5 + 3
        def v_on(h, tb): return cV.base + h * 5 + 3 + tb + 1

        @ph.on("sync")
        def _(e):
            pb.inc(e.dma_start(out=nw[:], in_=w["nw"]), "c7d", 16)

        @ph.on("tensor")
        def _(e):
            for h in range(H):
                for tb in range(2):
                    cS.wait(e, s_sq(h))
                    if h >= 1:
                        cV.wait(e, v_on(h - 1, 1))
                    ins = e.matmul(ps7[0:1, tb * 512:tb * 512 + 512], ones_bf[:],
                                   sqb[:, tb * 512:tb * 512 + 512], start=True, stop=True)
                    cT.tick(ins)
                cS.wait(e, s_rs(h))
                for tb in range(2):
                    ins = e.matmul(pb7[:, tb * 512:tb * 512 + 512], ones_f[:],
                                   rstdb[0:1, tb * 512:tb * 512 + 512], start=True, stop=True)
                    cT.tick(ins)

        @ph.on("scalar")
        def _(e):
            for h in range(H):
                cV.wait(e, v_og(h, 1))
                if h >= 1:
                    cT.wait(e, t_ss(h - 1, 1))
                cS.guard(e)
                cS.tick(e.square(sqb[:, :], o_b[:, h * 1024:(h + 1) * 1024]))
                cV.wait(e, v_eps(h))
                cS.guard(e)
                cS.tick(e.activation(rstdb[0:1, 0:1024], rstdb[0:1, 0:1024], AF.Ln))
                cS.guard(e)
                cS.tick(e.activation(rstdb[0:1, 0:1024], rstdb[0:1, 0:1024], AF.Exp, scale=-0.5))

        @ph.on("vector")
        def _(e):
            e.wait_ge(pb.sem("c7d"), d7["c7d"] + 16)
            for h in range(H):
                for tb in range(2):
                    cV.guard(e)
                    ob = o_b[:, h * 1024 + tb * 512:h * 1024 + tb * 512 + 512]
                    cV.tick(e.tensor_mul(ob, ob, sg[:, h * 1024 + tb * 512:h * 1024 + tb * 512 + 512]))
                cT.wait(e, t_ss(h, 1))
                cV.guard(e)
                cV.tick(e.tensor_scalar(rstdb[0:1, 0:1024], ps7[0:1, 0:1024],
                                        1.0 / 128.0, B_EPS, ALU.mult, ALU.add))
                for tb in range(2):
                    cT.wait(e, t_bc(h, tb))
                    cV.guard(e)
                    ob = o_b[:, h * 1024 + tb * 512:h * 1024 + tb * 512 + 512]
                    ins = e.scalar_tensor_tensor(ob, ob, nw[:, h:h + 1],
                                                 pb7[:, tb * 512:tb * 512 + 512],
                                                 ALU.mult, ALU.mult)
                    cV.tick(ins)

        pb.emit(ph)
        cT.check(H * 4)
        cS.check(H * 3)
        cV.check(H * 5)

    ar.freev("sg", "sqb", "rstdb")
    rtx = ar.alloc("rtx", 2 * D, F32)
    rox = ar.alloc("rox", 2 * D, F32)
    with nc.psum_tensor("c8_ps", [128, 2048], F32) as ps8:
        ph = Phase("c8")
        cT = Clk(pb, "c8T")
        cV = Clk(pb, "c8V")
        d8 = pb.bases("c8w", "c8x0", "c8x1", "c8o0", "c8o1")

        def mmd(g): return cT.base + g + 1
        def add_(tt, db): return cV.base + tt * 4 + db + 1

        @ph.on("sync")
        def _(e):
            for tt in range(NT):
                if tt >= 2:
                    cV.wait(e, add_(tt - 2, 3))
                    ins = e.dma_start(out=dst_r[(tt - 2) * 128:(tt - 1) * 128, :],
                                      in_=rox[:, (tt % 2) * D:(tt % 2 + 1) * D])
                    pb.inc(ins, f"c8o{tt % 2}", 16)
                ins = e.dma_start(out=rtx[:, (tt % 2) * D:(tt % 2 + 1) * D],
                                  in_=src_r[tt * 128:(tt + 1) * 128, :])
                pb.inc(ins, f"c8x{tt % 2}", 16)
            for tt in range(NT - 2, NT):
                cV.wait(e, add_(tt, 3))
                ins = e.dma_start(out=dst_r[tt * 128:(tt + 1) * 128, :],
                                  in_=rox[:, (tt % 2) * D:(tt % 2 + 1) * D])
                pb.inc(ins, f"c8o{tt % 2}", 16)
            e.wait_ge(pb.sem("c8o0"), d8["c8o0"] + 16 * 4)
            e.wait_ge(pb.sem("c8o1"), d8["c8o1"] + 16 * 4)

        @ph.on("tensor")
        def _(e):
            e.wait_ge(pb.sem("c8w"), d8["c8w"])
            for tt in range(NT):
                for db in range(4):
                    g = tt * 4 + db
                    if g >= 4:
                        cV.wait(e, add_((g - 4) // 4, (g - 4) % 4))
                    for kt in range(16):
                        ins = e.matmul(ps8[:, (g % 4) * 512:(g % 4) * 512 + 512],
                                       o_b[:, kt * 1024 + tt * 128:kt * 1024 + tt * 128 + 128],
                                       wo_sb[:, kt * 2048 + db * 512:kt * 2048 + db * 512 + 512],
                                       start=(kt == 0), stop=(kt == 15))
                    cT.tick(ins)

        @ph.on("vector")
        def _(e):
            for tt in range(NT):
                for db in range(4):
                    g = tt * 4 + db
                    cT.wait(e, mmd(g))
                    e.wait_ge(pb.sem(f"c8x{tt % 2}"), d8[f"c8x{tt % 2}"] + 16 * (tt // 2 + 1))
                    if tt >= 2 and db == 0:
                        e.wait_ge(pb.sem(f"c8o{tt % 2}"), d8[f"c8o{tt % 2}"] + 16 * (tt // 2))
                    cV.guard(e)
                    ins = e.tensor_add(rox[:, (tt % 2) * D + db * 512:(tt % 2) * D + db * 512 + 512],
                                       rtx[:, (tt % 2) * D + db * 512:(tt % 2) * D + db * 512 + 512],
                                       ps8[:, (g % 4) * 512:(g % 4) * 512 + 512])
                    cV.tick(ins)

        pb.emit(ph)
        cT.check(32)
        cV.check(32)

    ar.freev("o_b", "wo_sb", "rtx", "rox")
    ar.close()
